# version 10
# baseline (speedup 1.0000x reference)
import numpy as np
import ml_dtypes
import concourse.bass as bass
import concourse.mybir as mybir

F32 = mybir.dt.float32
BF16 = mybir.dt.bfloat16
I32 = mybir.dt.int32
AF = mybir.ActivationFunctionType
ALU = mybir.AluOpType
PE = mybir.EngineType.PE
AXX = mybir.AxisListType.X

ENGS = ("sync", "gpsimd", "vector", "scalar", "tensor")


class Sched:
    def __init__(self, nc):
        self.nc = nc
        self.thunks = {e: [] for e in ENGS}
        self.sems = {}
        self.cnt = {}
        self.last_wait = {}

    def sem(self, name):
        self.sems[name] = self.nc.alloc_semaphore(name)
        self.cnt[name] = 0

    def op(self, eng, build, waits=(), inc=None, amt=1):
        eff = []
        for wt in waits:
            if wt is None:
                continue
            sn, v = wt
            if sn is None or v is None or v <= 0:
                continue
            key = (eng, sn)
            if self.last_wait.get(key, 0) >= v:
                continue
            self.last_wait[key] = v
            eff.append((self.sems[sn], v))
        newc = None
        semh = None
        if inc is not None:
            self.cnt[inc] += amt
            newc = self.cnt[inc]
            semh = self.sems[inc]

        def thunk(e, eff=eff, build=build, semh=semh, amt=amt):
            for (s, v) in eff:
                e.wait_ge(s, v)
            ins2 = build(e)
            if semh is not None:
                ins2.then_inc(semh, amt)
        self.thunks[eng].append(thunk)
        return (inc, newc)

    def replay(self, block):
        for e in ENGS:
            fns = self.thunks[e]

            def run(engine, fns=fns):
                for f in fns:
                    f(engine)
            getattr(block, e)(run)


class Uses:
    def __init__(self):
        self.m = {}

    def use(self, key, tok):
        if tok is None or tok[0] is None:
            return
        d = self.m.setdefault(key, {})
        sn, v = tok
        if d.get(sn, 0) < v:
            d[sn] = v

    def waits(self, key, clear=True):
        d = self.m.get(key, {})
        w = tuple(d.items())
        if clear and key in self.m:
            del self.m[key]
        return w


def build_program(NTOK=4096, NOWN=2048, HPW=4, stop_after=None, dump=None):
    D, H, DH = 1024, 16, 64
    NB = NTOK // 128
    NTILE = NB
    NOTILE = NOWN // 128
    QH = min(1024, NOWN)
    NQH = NOWN // QH
    NW = H // HPW
    C = D // 128
    NS = 512
    SS = min(NS, QH); NSS = QH // SS
    QSEC = min(NS, NOWN); NQSEC = NOWN // QSEC
    TOPK = max(1, NB // 2)
    NBR = 32
    KEXT = 64 + NBR

    nc = bass.Bass("TRN2", target_bir_lowering=False, debug=False,
                   dynamic_dma_scratch_size=16384)

    def din(name, shape, dt=F32):
        return nc.dram_tensor(name, shape, dt, kind="ExternalInput").ap()

    xb = din("xb", [NTOK, D])
    xo = din("xo", [NOWN, D])
    wqkvT = din("wqkvT", [D, 3 * D])
    wprojT = din("wprojT", [D, D])
    pbd = din("pb", [1, D])
    wrTd = din("wrT", [D, D])
    wiTd = din("wiT", [D, D])
    fbd = din("fb", [1, D])
    c1d = din("c1", [D, D]); c2d = din("c2", [D, D])
    s1d = din("s1", [D, D]); s2d = din("s2", [D, D])
    ucst = din("ucst", [NB, NOWN], BF16)
    e32v = din("e32v", [128, NB * NTILE])
    ekb = din("ekb", [NB, NTOK])
    identf = din("identf", [128, 128])
    goffd = nc.dram_tensor("goff", [1, 1], I32, kind="ExternalInput").ap()
    out = nc.dram_tensor("out", [NOWN, D], F32, kind="ExternalOutput").ap()
    recd = nc.dram_tensor("recd", [H * NQH, QH], BF16, kind="Internal").ap()
    md = nc.dram_tensor("md", [2 * D, D], BF16, kind="Internal").ap()

    S = Sched(nc)
    for sname in ("tP", "tV", "tA", "dO"):
        S.sem(sname)
    POOLS = {"sync": ["dS%d" % i for i in range(6)],
             "gpsimd": ["dG%d" % i for i in range(4)],
             "dB": ["dB%d" % i for i in range(2)]}
    for names in POOLS.values():
        for n in names:
            S.sem(n)
    pool_rr = {k: 0 for k in POOLS}
    U = Uses()

    cur = [nc.sbuf_top]

    def sb(name, shape, dt):
        nbytes = (shape[1] * mybir.dt.size(dt) + 31) // 32 * 32
        cur[0] -= nbytes
        return nc.alloc_sbuf_tensor_at(name, shape, dt, offset=cur[0])

    def sb_at(name, shape, dt, offset):
        return nc.alloc_sbuf_tensor_at(name, shape, dt, offset=offset)

    RA = cur[0] - max(C * NTOK * 2, 4 * C * 2048)
    h1T = [sb_at(f"h1T{c}", [128, NTOK], BF16, RA + c * NTOK * 2) for c in range(C)]
    Chi = [sb_at(f"Chi{c}", [128, D], BF16, RA + c * 2048) for c in range(C)]
    Shi = [sb_at(f"Shi{c}", [128, D], BF16, RA + (C + c) * 2048) for c in range(C)]
    Clo = [sb_at(f"Clo{c}", [128, D], BF16, RA + (2 * C + c) * 2048) for c in range(C)]
    Slo = [sb_at(f"Slo{c}", [128, D], BF16, RA + (3 * C + c) * 2048) for c in range(C)]
    cur[0] = RA

    kx_b = NTOK * 2
    qx_b = NOWN * 2
    vx_b = ((NB * 65 * 2 + 31) // 32) * 32
    RC = cur[0] - max(HPW * (kx_b + qx_b + vx_b), 4 * C * 2048, NOTILE * 2048 + 2 * C * 2048)
    k_ext = [sb_at(f"kext{h}", [128, NTOK], BF16, RC + h * kx_b) for h in range(HPW)]
    q_ext = [sb_at(f"qext{h}", [128, NOWN], BF16, RC + HPW * kx_b + h * qx_b)
             for h in range(HPW)]
    v_ext = [sb_at(f"vext{h}", [128, NB * 65], BF16,
                   RC + HPW * (kx_b + qx_b) + h * vx_b) for h in range(HPW)]
    R1h = [sb_at(f"R1h{c}", [128, D], BF16, RC + c * 2048) for c in range(C)]
    R1l = [sb_at(f"R1l{c}", [128, D], BF16, RC + (C + c) * 2048) for c in range(C)]
    R2h = [sb_at(f"R2h{c}", [128, D], BF16, RC + (2 * C + c) * 2048) for c in range(C)]
    R2l = [sb_at(f"R2l{c}", [128, D], BF16, RC + (3 * C + c) * 2048) for c in range(C)]
    atn = [sb_at(f"atn{t}", [128, D], BF16, RC + t * 2048) for t in range(NOTILE)]
    MhS = [sb_at(f"MhS{c}", [128, D], BF16, RC + NOTILE * 2048 + c * 2048) for c in range(C)]
    MlS = [sb_at(f"MlS{c}", [128, D], BF16, RC + NOTILE * 2048 + (C + c) * 2048) for c in range(C)]
    cur[0] = RC

    RD = cur[0] - max(C * NOWN * 2, 32768)
    ofullT = [sb_at(f"ofT{c}", [128, NOWN], BF16, RD + c * NOWN * 2) for c in range(C)]
    h2T = [sb_at(f"h2T{c}", [128, NOWN], BF16, RD + c * NOWN * 2) for c in range(C)]
    wfsr = [sb_at("wfsr%d" % i, [128, D], F32, RD + i * 4096) for i in range(2)]
    wfsi = [sb_at("wfsi%d" % i, [128, D], F32, RD + 8192 + i * 4096) for i in range(2)]
    whr = [sb_at("whr%d" % i, [128, D], BF16, RD + 16384 + i * 2048) for i in range(2)]
    wlr = [sb_at("wlr%d" % i, [128, D], BF16, RD + 20480 + i * 2048) for i in range(2)]
    whi = [sb_at("whi%d" % i, [128, D], BF16, RD + 24576 + i * 2048) for i in range(2)]
    wli = [sb_at("wli%d" % i, [128, D], BF16, RD + 28672 + i * 2048) for i in range(2)]
    xstn = 2
    xst = [sb_at("xst%d" % i, [128, 2 * D], F32, RD + i * 8192) for i in range(2)]
    wst = [sb_at("wst%d" % i, [128, 2 * D], F32, RD + 16384 + i * 8192) for i in range(2)]
    cur[0] = RD

    RB = cur[0] - 16384
    cstg = [sb_at("cstg%d" % i, [128, D], F32, RB + i * 4096) for i in range(2)]
    mhv = [sb_at("mhv0", [128, D], BF16, RB + 8192)]
    mlv = [sb_at("mlv0", [128, D], BF16, RB + 10240)]
    wvcols = 3 * DH * HPW
    wvW = [[sb_at(f"wvW{b}_{c}", [128, wvcols], BF16, RB + (b * C + c) * wvcols * 2)
            for c in range(C)] for b in range(1)]
    wpjb = [sb_at(f"wpjb{c}", [128, D], BF16, RB + c * 2048) for c in range(C)]
    cur[0] = RB

    R5 = cur[0] - 8192
    sc32 = [sb_at("sc32%d" % i, [128, D], F32, R5 + i * 4096) for i in range(2)]
    xot = [sb_at("xot%d" % i, [128, D], F32, R5 + i * 4096) for i in range(2)]
    cur[0] = R5

    R6 = cur[0] - 17504
    e32s = sb_at("e32s", [128, NB * NTILE], F32, R6)
    bmsb = sb_at("bmsb", [NB, D], F32, R6)
    bmT = sb_at("bmT", [128, C * NB], F32, R6 + 4096)
    qms = sb_at("qms", [NB, D], F32, R6 + 4096 + ((C * NB * 4 + 31) // 32) * 32)
    kms = sb_at("kms", [NB, D], F32, R6 + 8192 + ((C * NB * 4 + 31) // 32) * 32)
    _o = R6 + 12288 + ((C * NB * 4 + 31) // 32) * 32
    qmT = sb_at("qmT", [64, 2 * C * NB], F32, _o)
    kmT = sb_at("kmT", [64, 2 * C * NB], F32, _o + ((2 * C * NB * 4 + 31) // 32) * 32)
    _o2 = _o + 2 * (((C * NB * 4 + 31) // 32) * 32)
    sims = sb_at("sims", [NB, H * NB], F32, _o2)
    _oq = R6 + 4096 + ((C * NB * 4 + 31) // 32) * 32
    rankt = sb_at("rankt", [NB, H * NB], F32, _oq)
    gtt = sb_at("gtt", [NB, H * NB], F32, _oq + ((H * NB * 4 + 31) // 32) * 32)
    mk32 = sb_at("mk32", [NB, H * NB], F32, _oq + ((H * NB * 4 + 31) // 32) * 32)
    attnT = [sb_at("attnT%d" % i, [128, QH], BF16, R6 + i * (QH * 2)) for i in range(2)]
    recb = sb_at("recb", [64, QH], BF16, R6 + 2 * QH * 2)
    otmp = sb_at("otmp", [64, QH], BF16, R6 + 3 * QH * 2)
    outb = [sb_at("outb%d" % i, [128, D], F32, R6 + i * 4096) for i in range(3)]
    h1s6 = sb_at("h1s6", [128, D], F32, R6 + 12288)
    cur[0] = R6

    h1s = [sb_at("h1s0", [128, D], F32, RC + HPW * kx_b)]
    ekbb = sb_at("ekbb", [NB, NTOK], BF16, R6 + 8256)
    idf = sb("idf", [128, 128], F32)
    mTbb = sb("mTbb", [NB, H * NB], BF16)
    stat = [sb("stat%d" % i, [128, 20], F32) for i in range(2)]
    onesb = sb("onesb", [1, 128], BF16)
    pbb = sb_at("pbb", [1, D], BF16, R5)
    fbb = sb("fbb", [1, D], BF16)
    recr = sb("recr", [65, QH], BF16)
    wxs = [sb_at("wxs0", [NB, NS], BF16, R5 + 2048),
           sb_at("wxs1", [NB, NS], BF16, R6 + 16448)]
    goffs = sb("goffs", [1, 1], I32)
    epsb = sb("epsb", [128, 1], F32)
    assert cur[0] >= nc.sbuf_base + 160, f"SBUF overflow: cur={cur[0]} base={nc.sbuf_base}"

    ps = [nc.alloc_psum_tensor(f"ps{i}", [128, 1024], F32) for i in range(4)]

    def ps_waits(i, halves):
        w = []
        for hf in halves:
            w += list(U.waits(("psb", 2 * i + hf)))
        return w

    def ps_use(i, halves, tok):
        for hf in halves:
            U.use(("psb", 2 * i + hf), tok)

    def dma(eng, outap, inap, sem=None, waits=()):
        if sem == "dO":
            return S.op(eng, lambda e, o=outap, i=inap: e.dma_start(out=o, in_=i),
                        waits=waits, inc="dO", amt=16)
        pool = sem if sem in POOLS else eng
        names = POOLS[pool]
        sn = names[pool_rr[pool] % len(names)]
        pool_rr[pool] += 1
        w = list(waits) + [(sn, S.cnt[sn])]
        return S.op(eng, lambda e, o=outap, i=inap: e.dma_start(out=o, in_=i),
                    waits=w, inc=sn, amt=16)

    def ve(build, waits=()):
        return S.op("vector", build, waits=waits, inc="tV")

    def ae(build, waits=()):
        return S.op("scalar", build, waits=waits, inc="tA")

    def pe(build, waits=()):
        return S.op("tensor", build, waits=waits, inc="tP")

    d_id = dma("sync", idf[:], identf[:, :])
    d_e32 = dma("sync", e32s[:], e32v[:, :])
    d_go = dma("sync", goffs[:], goffd[:, :])
    d_fb = dma("gpsimd", fbb[:], fbd[:, :], sem="gpsimd")
    t_ones = ve(lambda e: e.memset(onesb[:], 1.0))
    t_eps = ve(lambda e: e.memset(epsb[:], 1e-5))

    goff_box = {}

    def load_goff(e):
        (ld,), (val,) = nc.values_load_multi_w_load_instructions(
            goffs[0:1, 0:1], engines=[PE],
            min_val=0, max_val=NOWN // QSEC, skip_runtime_bounds_check=True)
        goff_box["v"] = val
        return ld
    S.op("tensor", load_goff, waits=(d_go,))

    def wave_w_dma(w, buf=0):
        toks = []
        for c in range(C):
            src = bass.AP(wqkvT.tensor, (c * 128) * (3 * D) + w * (DH * HPW),
                          [[3 * D, 128], [D, 3], [1, DH * HPW]])
            wts = U.waits(("wvW", 0, c))
            toks.append(dma("gpsimd", wvW[0][c][:].rearrange("p (a b) -> p a b", a=3),
                        src, sem="gpsimd", waits=wts))
        return toks

    wv_tok = [None] * NW

    csC_tok = [None] * C
    csS_tok = [None] * C
    for c in range(C):
        dc1 = dma("sync", cstg[0][:], c1d[c * 128:(c + 1) * 128, :],
                  waits=U.waits(("cstg", 0)))
        dc2 = dma("sync", cstg[1][:], c2d[c * 128:(c + 1) * 128, :],
                  waits=U.waits(("cstg", 1)))
        wsc = U.waits(("sc32", 0))
        t1 = ve(lambda e: e.tensor_tensor(sc32[0][:], cstg[0][:], cstg[1][:], ALU.add),
                waits=(dc1, dc2) + wsc)
        t2 = ve(lambda e, c=c: e.tensor_copy(Chi[c][:], sc32[0][:]), waits=(t1,))
        t3 = ve(lambda e, c=c: e.tensor_tensor(Clo[c][:], sc32[0][:], Chi[c][:], ALU.subtract),
                waits=(t2,))
        U.use(("cstg", 0), t1)
        U.use(("cstg", 1), t1)
        U.use(("sc32", 0), t3)
        csC_tok[c] = t3
    for c in range(C):
        dc1 = dma("sync", cstg[0][:], s1d[c * 128:(c + 1) * 128, :],
                  waits=U.waits(("cstg", 0)))
        dc2 = dma("sync", cstg[1][:], s2d[c * 128:(c + 1) * 128, :],
                  waits=U.waits(("cstg", 1)))
        wsc = U.waits(("sc32", 0))
        t1 = ve(lambda e: e.tensor_tensor(sc32[0][:], cstg[0][:], cstg[1][:], ALU.add),
                waits=(dc1, dc2) + wsc)
        t2 = ve(lambda e, c=c: e.tensor_copy(Shi[c][:], sc32[0][:]), waits=(t1,))
        t3 = ve(lambda e, c=c: e.tensor_tensor(Slo[c][:], sc32[0][:], Shi[c][:], ALU.subtract),
                waits=(t2,))
        U.use(("cstg", 0), t1)
        U.use(("cstg", 1), t1)
        U.use(("sc32", 0), t3)
        csS_tok[c] = t3

    r1_tok = [None] * C
    r2_tok = [None] * C
    for dc in range(C):
        bi = dc % 2
        wts = U.waits(("wfs", bi))
        srcr = bass.AP(wrTd.tensor, dc * 128, [[D, 128], [128 * D, C], [1, 128]])
        srci = bass.AP(wiTd.tensor, dc * 128, [[D, 128], [128 * D, C], [1, 128]])
        dwr = dma("sync", wfsr[bi][:].rearrange("p (o j) -> p o j", o=C), srcr, waits=wts)
        dwi = dma("sync", wfsi[bi][:].rearrange("p (o j) -> p o j", o=C), srci, waits=wts)
        wsp = U.waits(("wsp", bi))
        s1a = ve(lambda e, bi=bi: e.tensor_copy(whr[bi][:], wfsr[bi][:]), waits=(dwr,) + wsp)
        s1b = ve(lambda e, bi=bi: e.tensor_tensor(wlr[bi][:], wfsr[bi][:], whr[bi][:], ALU.subtract),
                 waits=(s1a,))
        s1c = ve(lambda e, bi=bi: e.tensor_copy(whi[bi][:], wfsi[bi][:]), waits=(dwi,) + wsp)
        s1d_ = ve(lambda e, bi=bi: e.tensor_tensor(wli[bi][:], wfsi[bi][:], whi[bi][:], ALU.subtract),
                  waits=(s1c,))
        U.use(("wfs", bi), s1b)
        U.use(("wfs", bi), s1d_)
        wsplit = [s1b, s1d_]
        wtr0 = ps_waits(0, [0, 1]) + wsplit + [csC_tok[C - 1], csS_tok[C - 1]]
        wtr1 = ps_waits(1, [0, 1])
        wtr2 = ps_waits(2, [0, 1])
        last0 = last1 = last2 = None
        for oc in range(C):
            for nh in range(2):
                for ti, (wt, ct) in enumerate(((whr, Chi), (wlr, Chi), (whi, Shi), (wli, Shi))):
                    last0 = pe(lambda e, bi=bi, oc=oc, nh=nh, wt=wt, ct=ct,
                               st_c=(oc == 0 and ti == 0),
                               sp_c=(oc == C - 1 and ti == 3): e.matmul(
                        ps[0][:, nh * NS:(nh + 1) * NS],
                        wt[bi][oc * 128:(oc + 1) * 128, 0:128] if False else
                        wt[bi][:, oc * 128:(oc + 1) * 128],
                        ct[oc][:, nh * NS:(nh + 1) * NS],
                        start=st_c, stop=sp_c),
                        waits=tuple(wtr0) if (oc == 0 and nh == 0 and ti == 0) else ())
                for ti, (wt, ct) in enumerate(((whr, Shi), (wlr, Shi))):
                    last1 = pe(lambda e, bi=bi, oc=oc, nh=nh, wt=wt, ct=ct,
                               st_c=(oc == 0 and ti == 0),
                               sp_c=(oc == C - 1 and ti == 1): e.matmul(
                        ps[1][:, nh * NS:(nh + 1) * NS],
                        wt[bi][:, oc * 128:(oc + 1) * 128],
                        ct[oc][:, nh * NS:(nh + 1) * NS],
                        start=st_c, stop=sp_c),
                        waits=tuple(wtr1 + wsplit) if (oc == 0 and nh == 0 and ti == 0) else ())
                for ti, (wt, ct) in enumerate(((whi, Chi), (wli, Chi))):
                    last2 = pe(lambda e, bi=bi, oc=oc, nh=nh, wt=wt, ct=ct,
                               st_c=(oc == 0 and ti == 0),
                               sp_c=(oc == C - 1 and ti == 1): e.matmul(
                        ps[2][:, nh * NS:(nh + 1) * NS],
                        wt[bi][:, oc * 128:(oc + 1) * 128],
                        ct[oc][:, nh * NS:(nh + 1) * NS],
                        start=st_c, stop=sp_c),
                        waits=tuple(wtr2 + wsplit) if (oc == 0 and nh == 0 and ti == 0) else ())
        U.use(("wsp", bi), last0)
        U.use(("wsp", bi), last1)
        U.use(("wsp", bi), last2)
        e1 = ve(lambda e, dc=dc: e.tensor_copy(R1h[dc][:], ps[0][:, 0:D]), waits=(last0,))
        e2 = ve(lambda e, dc=dc: e.tensor_tensor(R1l[dc][:], ps[0][:, 0:D], R1h[dc][:], ALU.subtract),
                waits=(e1,))
        ps_use(0, [0, 1], e2)
        wsc = U.waits(("sc32", 1))
        e3 = ve(lambda e: e.tensor_copy(sc32[1][:], ps[2][:, 0:D]), waits=(last2,) + wsc)
        ps_use(2, [0, 1], e3)
        e4 = ve(lambda e: e.tensor_tensor(sc32[0][:], ps[1][:, 0:D], sc32[1][:], ALU.subtract),
                waits=(last1, e3) + U.waits(("sc32", 0)))
        ps_use(1, [0, 1], e4)
        e5 = ve(lambda e, dc=dc: e.tensor_copy(R2h[dc][:], sc32[0][:]), waits=(e4,))
        e6 = ve(lambda e, dc=dc: e.tensor_tensor(R2l[dc][:], sc32[0][:], R2h[dc][:], ALU.subtract),
                waits=(e5,))
        U.use(("sc32", 0), e6)
        U.use(("sc32", 1), e4)
        r1_tok[dc] = e2
        r2_tok[dc] = e6

    s1_done = [("tP", S.cnt["tP"]), ("tV", S.cnt["tV"])]
    md_tok = []
    for tc in range(C):
        pt = tc % 2
        wtr = ps_waits(pt, [0, 1])
        seq = []
        for dc2 in range(C):
            seq += [(Chi[dc2], R1h[dc2], csC_tok[dc2], r1_tok[dc2]),
                    (Chi[dc2], R1l[dc2], csC_tok[dc2], r1_tok[dc2]),
                    (Clo[dc2], R1h[dc2], csC_tok[dc2], r1_tok[dc2]),
                    (Shi[dc2], R2h[dc2], csS_tok[dc2], r2_tok[dc2]),
                    (Shi[dc2], R2l[dc2], csS_tok[dc2], r2_tok[dc2]),
                    (Slo[dc2], R2h[dc2], csS_tok[dc2], r2_tok[dc2])]
        last = None
        for nh in range(2):
            for i, (lh, rh, t1, t2) in enumerate(seq):
                last = pe(lambda e, lh=lh, rh=rh, tc=tc, pt=pt, nh=nh,
                          st_c=(i == 0), sp_c=(i == len(seq) - 1): e.matmul(
                    ps[pt][:, nh * NS:(nh + 1) * NS],
                    lh[:, tc * 128:(tc + 1) * 128],
                    rh[:, nh * NS:(nh + 1) * NS],
                    start=st_c, stop=sp_c),
                    waits=tuple(wtr + [t1, t2]) if (nh == 0 and i == 0) else (t1, t2))
        bi = 0
        wmv = U.waits(("mhv", bi))
        m1 = ve(lambda e, bi=bi, pt=pt: e.tensor_scalar(
            mhv[bi][:], ps[pt][:, 0:D], 1.0 / D, None, ALU.mult), waits=(last,) + wmv)
        m2 = ve(lambda e, bi=bi, pt=pt: e.scalar_tensor_tensor(
            mlv[bi][:], ps[pt][:, 0:D], 1.0 / D, mhv[bi][:], ALU.mult, ALU.subtract),
            waits=(m1,))
        ps_use(pt, [0, 1], m2)
        dmh = dma("sync", md[tc * 128:(tc + 1) * 128, :], mhv[bi][:], waits=(m1,))
        dml = dma("sync", md[D + tc * 128:D + (tc + 1) * 128, :], mlv[bi][:], waits=(m2,))
        U.use(("mhv", bi), dmh)
        U.use(("mhv", bi), dml)
        md_tok.append(dmh)
        md_tok.append(dml)
    p7_tp = ("tP", S.cnt["tP"])
    d_pb = dma("gpsimd", pbb[:], pbd[:, :], sem="gpsimd",
               waits=(("tV", S.cnt["tV"]),))
    p7_done = list(md_tok)
    for c in range(C):
        U.use(("wvW", 0, c), p7_tp)
        for _t in p7_done:
            U.use(("wvW", 0, c), _t)
    wv_tok[0] = wave_w_dma(0)

    def ln_tile(src_fn, sc, st_, wload, extra_sc_waits=()):
        scw = tuple(U.waits(("h1s", id(sc) % 97))) + tuple(extra_sc_waits)
        stw = U.waits(("stat", id(st_) % 97))
        b0 = ve(lambda e: e.bn_stats(st_[:, 0:6], src_fn(0, 512)), waits=wload + stw)
        b1 = ve(lambda e: e.bn_stats(st_[:, 6:12], src_fn(512, 1024)), waits=wload + stw)
        ag = ve(lambda e: e.bn_aggr(st_[:, 12:14], st_[:, 0:12]), waits=(b0, b1))
        a2 = ae(lambda e: e.activation(st_[:, 15:16], st_[:, 13:14], AF.Sqrt,
                                       bias=epsb[:, 0:1], scale=1.0),
                waits=(ag, t_eps))
        v8 = ve(lambda e: e.reciprocal(st_[:, 16:17], st_[:, 15:16]), waits=(a2,))
        v13 = ve(lambda e: e.scalar_tensor_tensor(st_[:, 19:20], st_[:, 12:13], -1.0, st_[:, 16:17], ALU.mult, ALU.mult),
                 waits=(v8,))
        v14 = ve(lambda e: e.tensor_scalar(sc[:], src_fn(0, D), st_[:, 16:17], st_[:, 19:20], ALU.mult, ALU.add),
                 waits=(v13,) + scw)
        U.use(("stat", id(st_) % 97), v14)
        return v14, v14

    def _finish_early():
        z = ve(lambda e: e.memset(outb[0][:], 0.0), waits=(("tP", S.cnt["tP"]), ("tV", S.cnt["tV"]), ("tA", S.cnt["tA"])))
        for tb in range(NOTILE):
            dma("sync", out[tb * 128:(tb + 1) * 128, :], outb[0][:], waits=(z,))
        if dump:
            names = sorted(dump)
            row = 0
            prevtok = z
            for nm in names:
                ap = dump[nm]
                p_, f_ = ap.shape[0], ap.shape[1]
                cv = ve(lambda e, ap=ap, p_=p_, f_=f_: e.tensor_copy(outb[1][0:p_, 0:f_], ap), waits=(z, prevtok))
                prevtok = dma("sync", out[row:row + p_, 0:f_], outb[1][0:p_, 0:f_], waits=(cv,))
                row += p_
        S.op("sync", lambda e: None,
             waits=tuple((sn, S.cnt[sn]) for sn in POOLS["sync"])
             + tuple((sn, S.cnt[sn]) for sn in POOLS["gpsimd"])
             + tuple((sn, S.cnt[sn]) for sn in POOLS["dB"]))
        block = nc.Block()
        with block:
            S.replay(block)
        stats = {e: len(S.thunks[e]) for e in ENGS}
        return nc, stats, {}

    if stop_after == "p7":
        if dump is not None:
            dump.clear()
            dump["a_mhv"] = mhv[0][:, 0:D]
            dump["b_mlv"] = mlv[0][:, 0:D]
        return _finish_early()

    NGRP = NTILE // 2

    def x_group_dma(gI):
        srcap = xb[:, :].rearrange("(n p) m -> p n m", p=128)[:, 2 * gI:2 * gI + 2, :]
        wts = U.waits(("xst", gI % xstn))
        if gI < xstn:
            wts = wts + tuple(s1_done)
        return dma("sync", xst[gI % xstn][:].rearrange("p (n m) -> p n m", n=2),
                   srcap, waits=wts)

    xg_tok = [None] * NGRP
    for gI in range(xstn):
        xg_tok[gI] = x_group_dma(gI)

    tr_i = 0
    bm_pe = None
    p1_marks = (("tP", S.cnt["tP"]), ("tV", S.cnt["tV"]))
    for t in range(NTILE):
        gI, half = t // 2, t % 2
        xt = (lambda a, b, gI=gI, half=half: xst[gI % xstn][:, half * D + a:half * D + b])
        sc = h1s[0]
        st_ = stat[t % 2]
        v14, a1 = ln_tile(xt, sc, st_, (xg_tok[gI],),
                          extra_sc_waits=p1_marks if t == 0 else ())
        U.use(("xst", gI % xstn), v14)
        U.use(("xst", gI % xstn), a1)
        for nh in range(2):
            wts = [v14, d_e32] + (ps_waits(3, [nh]) if t == 0 else [])
            bm_pe = pe(lambda e, t=t, nh=nh, sc=sc: e.matmul(
                ps[3][0:NB, nh * NS:(nh + 1) * NS],
                e32s[:, NB * t:NB * t + NB],
                sc[:, nh * NS:(nh + 1) * NS],
                start=(t == 0), stop=(t == NTILE - 1)),
                waits=tuple(wts))
        for c in range(C):
            bank = tr_i % 2
            wtr = [v14, d_id] + ps_waits(2, [bank])
            if t == 0 and c == 0:
                wtr += [p7_tp] + p7_done
            p = pe(lambda e, sc=sc, c=c, bank=bank: e.transpose(
                ps[2][:, bank * NS:bank * NS + 128], sc[:, c * 128:(c + 1) * 128], idf[:]),
                waits=tuple(wtr))
            ev = ae(lambda e, c=c, t=t, bank=bank: e.activation(
                h1T[c][:, t * 128:(t + 1) * 128], ps[2][:, bank * NS:bank * NS + 128],
                AF.Copy),
                waits=(p,))
            ps_use(2, [bank], ev)
            U.use(("h1s", id(sc) % 97), p)
            tr_i += 1
        U.use(("h1s", id(sc) % 97), bm_pe)
        if half == 1 and gI + xstn < NGRP:
            xg_tok[gI + xstn] = x_group_dma(gI + xstn)
    h1T_done = [("tV", S.cnt["tV"]), ("tA", S.cnt["tA"])]
    if stop_after == "p1":
        return _finish_early()

    def emit_qkv(w, marks):
        buf = 0
        wtokw = wv_tok[w]
        qk_ev = {}
        slot_i = 0
        for fp in range(HPW // 2):
            hA, hB = 2 * fp, 2 * fp + 1
            for sect, ntn, colbase in (("q", NQSEC, 0), ("k", NTOK // NS, HPW * DH)):
                for nt in range(ntn):
                    pi, hf = (slot_i % 4) // 2, (slot_i % 4) % 2
                    slot_i += 1
                    off = hf * NS
                    wtr = ps_waits(pi, [hf]) + list(wtokw)
                    if w == 0:
                        wtr += h1T_done
                    last = None
                    for c in range(C):
                        if sect == "q":
                            def bq(e, c=c, fp=fp, nt=nt, pi=pi, off=off, buf=buf,
                                   st_c=(c == 0), sp_c=(c == C - 1),
                                   colbase=colbase):
                                return e.matmul(
                                    ps[pi][:, off:off + QSEC],
                                    wvW[0][c][:, colbase + fp * 128:colbase + (fp + 1) * 128],
                                    h1T[c][:, bass.ts(goff_box["v"] + nt, QSEC)],
                                    start=st_c, stop=sp_c)
                        else:
                            def bq(e, c=c, fp=fp, nt=nt, pi=pi, off=off, buf=buf,
                                   st_c=(c == 0), sp_c=(c == C - 1), colbase=colbase):
                                return e.matmul(
                                    ps[pi][:, off:off + NS],
                                    wvW[0][c][:, colbase + fp * 128:colbase + (fp + 1) * 128],
                                    h1T[c][:, nt * NS:(nt + 1) * NS],
                                    start=st_c, stop=sp_c)
                        last = pe(bq, waits=tuple(wtr) if c == 0 else ())
                    dstA = q_ext[hA] if sect == "q" else k_ext[hA]
                    dstB = q_ext[hB] if sect == "q" else k_ext[hB]
                    wdt = QSEC if sect == "q" else NS
                    w0e = marks if w == 0 else ()
                    evA = ve(lambda e, dstA=dstA, nt=nt, pi=pi, off=off, wdt=wdt: e.tensor_copy(
                        dstA[0:64, nt * wdt:(nt + 1) * wdt], ps[pi][0:64, off:off + wdt]),
                        waits=(last,) + w0e)
                    evB = ve(lambda e, dstB=dstB, nt=nt, pi=pi, off=off, wdt=wdt: e.tensor_copy(
                        dstB[64:128, nt * wdt:(nt + 1) * wdt], ps[pi][64:128, off:off + wdt]),
                        waits=(last,) + w0e)
                    ps_use(pi, [hf], evA)
                    ps_use(pi, [hf], evB)
                    qk_ev[(sect, fp, nt)] = (evA, evB)
        v_ev = {}
        qkv_last_mm = None
        for tb in range(NTILE):
            pi, hf = (slot_i % 4) // 2, (slot_i % 4) % 2
            slot_i += 1
            off = hf * NS
            wtr = ps_waits(pi, [hf]) + list(wtokw)
            if w == 0:
                wtr += h1T_done
            last = None
            for c in range(C):
                last = pe(lambda e, c=c, tb=tb, pi=pi, off=off, buf=buf,
                          st_c=(c == 0), sp_c=(c == C - 1): e.matmul(
                    ps[pi][:, off:off + HPW * DH],
                    h1T[c][:, tb * 128:(tb + 1) * 128],
                    wvW[0][c][:, 2 * HPW * DH:3 * HPW * DH],
                    start=st_c, stop=sp_c),
                    waits=tuple(wtr) if c == 0 else ())
            qkv_last_mm = last
            for hl in range(HPW):
                evv = ve(lambda e, hl=hl, tb=tb, pi=pi, off=off: e.tensor_copy(
                    v_ext[hl][:, tb * 65:tb * 65 + 64],
                    ps[pi][:, off + hl * DH:off + (hl + 1) * DH]),
                    waits=(last,) + (marks if w == 0 else ()))
                ps_use(pi, [hf], evv)
                v_ev[(tb, hl)] = evv

        for c in range(C):
            U.use(("wvW", 0, c), qkv_last_mm)
        return qk_ev, v_ev

    e_bm = ve(lambda e: e.tensor_copy(bmsb[:], ps[3][0:NB, :]), waits=(bm_pe,))
    ps_use(3, [0, 1], e_bm)

    def small_transpose(src_ap, dst_ap, rows, tok_extra, bank):
        wtr = list(tok_extra) + ps_waits(2, [bank])
        p = pe(lambda e: e.transpose(
            ps[2][0:rows if False else 128, bank * NS:bank * NS + rows][0:rows, :]
            if False else ps[2][0:rows, bank * NS:bank * NS + rows],
            src_ap, idf[0:rows, 0:rows]), waits=tuple(wtr))
        ev = ve(lambda e: e.tensor_copy(dst_ap, ps[2][0:rows, bank * NS:bank * NS + rows]),
                waits=(p,))
        ps_use(2, [bank], ev)
        return ev

    bmT_ev = [None] * C
    for c in range(C):
        bank = c % 2
        wtr = [e_bm, d_id] + ps_waits(2, [bank])
        p = pe(lambda e, c=c, bank=bank: e.transpose(
            ps[2][:, bank * NS:bank * NS + NB], bmsb[0:NB, c * 128:(c + 1) * 128],
            idf[0:NB, 0:NB]), waits=tuple(wtr))
        ev = ve(lambda e, c=c, bank=bank: e.tensor_copy(
            bmT[:, c * NB:(c + 1) * NB], ps[2][:, bank * NS:bank * NS + NB]),
            waits=(p,))
        ps_use(2, [bank], ev)
        bmT_ev[c] = ev
    if stop_after == "p2a":
        return _finish_early()
    qk_last = None
    for c in range(C):
        wts = U.waits(("wst", c % 2))
        dtok = dma("sync", wst[c % 2][:].rearrange("p (a b) -> p a b", a=2),
                   bass.AP(wqkvT.tensor, (c * 128) * (3 * D), [[3 * D, 128], [D, 2], [1, D]]),
                   waits=wts)
        for part, pi in ((0, 0), (1, 1)):
            for nh in range(2):
                wts2 = [dtok, bmT_ev[c]] + (ps_waits(pi, [nh]) if c == 0 else [])
                qk_last = pe(lambda e, c=c, part=part, nh=nh, pi=pi: e.matmul(
                    ps[pi][0:NB, nh * NS:(nh + 1) * NS],
                    bmT[:, c * NB:(c + 1) * NB],
                    wst[c % 2][:, part * D + nh * NS: part * D + (nh + 1) * NS],
                    start=(c == 0), stop=(c == C - 1)),
                    waits=tuple(wts2))
                U.use(("wst", c % 2), qk_last)
    e_qm = ve(lambda e: e.tensor_copy(qms[:], ps[0][0:NB, :]), waits=(qk_last,))
    e_km = ve(lambda e: e.tensor_copy(kms[:], ps[1][0:NB, :]), waits=(qk_last,))
    ps_use(0, [0, 1], e_qm)
    ps_use(1, [0, 1], e_km)
    if stop_after == "p2b":
        return _finish_early()
    qmT_ev = [None] * (2 * C)
    kmT_ev = [None] * (2 * C)
    for hc in range(2 * C):
        for (srcT, dstT, evs, etok) in ((qms, qmT, qmT_ev, e_qm), (kms, kmT, kmT_ev, e_km)):
            bank = hc % 2
            wtr = [etok] + ps_waits(2, [bank])
            p = pe(lambda e, srcT=srcT, hc=hc, bank=bank: e.transpose(
                ps[2][0:64, bank * NS:bank * NS + NB], srcT[0:NB, hc * 64:(hc + 1) * 64],
                idf[0:NB, 0:NB]), waits=tuple(wtr))
            ev = ve(lambda e, dstT=dstT, hc=hc, bank=bank: e.tensor_copy(
                dstT[:, hc * NB:(hc + 1) * NB], ps[2][0:64, bank * NS:bank * NS + NB]),
                waits=(p,))
            ps_use(2, [bank], ev)
            evs[hc] = ev
    if stop_after == "p2c":
        return _finish_early()
    sim_last = None
    for h in range(H):
        wts = [qmT_ev[h], kmT_ev[h]] + (ps_waits(3, [0]) if h == 0 else [])
        sim_last = pe(lambda e, h=h: e.matmul(
            ps[3][0:NB, h * NB:(h + 1) * NB],
            qmT[:, h * NB:(h + 1) * NB],
            kmT[:, h * NB:(h + 1) * NB],
            start=True, stop=True),
            waits=tuple(wts))
    e_sim = ve(lambda e: e.tensor_copy(sims[:], ps[3][0:NB, 0:H * NB]), waits=(sim_last,))
    ps_use(3, [0, 1], e_sim)
    if stop_after == "p2d":
        return _finish_early()
    qkv0_marks = (("tP", S.cnt["tP"]), ("tV", S.cnt["tV"]), ("tA", S.cnt["tA"]))
    qkv_cache = {0: emit_qkv(0, qkv0_marks)}
    prev = ve(lambda e: e.memset(rankt[:], 0.0),
              waits=tuple(qmT_ev + kmT_ev))
    sims3 = sims[:].rearrange("p (h k) -> p h k", k=NB)
    rank3 = rankt[:].rearrange("p (h k) -> p h k", k=NB)
    gtt3 = gtt[:].rearrange("p (h k) -> p h k", k=NB)
    for s in range(1, NB):
        wdt = NB - s
        g1 = ve(lambda e, s=s, w=wdt: e.tensor_tensor(
            gtt3[:, :, 0:w], sims3[:, :, s:NB], sims3[:, :, 0:w], ALU.is_gt),
            waits=(e_sim, prev))
        prev = ve(lambda e, w=wdt: e.tensor_tensor(
            rank3[:, :, 0:w], rank3[:, :, 0:w], gtt3[:, :, 0:w], ALU.add),
            waits=(g1, prev))
        g2 = ve(lambda e, s=s, w=wdt: e.tensor_tensor(
            gtt3[:, :, w:NB], sims3[:, :, 0:s], sims3[:, :, w:NB], ALU.is_gt),
            waits=(prev,))
        prev = ve(lambda e, w=wdt: e.tensor_tensor(
            rank3[:, :, w:NB], rank3[:, :, w:NB], gtt3[:, :, w:NB], ALU.add),
            waits=(g2, prev))
    if stop_after == "p2e":
        return _finish_early()
    e_mk = ve(lambda e: e.tensor_scalar(mk32[:], rankt[:], TOPK - 0.5, -1.0e9,
                                        ALU.is_gt, ALU.mult), waits=(prev,))
    mT_ev = [None] * H
    for h in range(H):
        bank = h % 2
        wtr = [e_mk] + ps_waits(2, [bank])
        p = pe(lambda e, h=h, bank=bank: e.transpose(
            ps[2][0:NB, bank * NS:bank * NS + NB], mk32[0:NB, h * NB:(h + 1) * NB],
            idf[0:NB, 0:NB]), waits=tuple(wtr))
        ev = ve(lambda e, h=h, bank=bank: e.tensor_copy(
            mTbb[:, h * NB:(h + 1) * NB], ps[2][0:NB, bank * NS:bank * NS + NB]),
            waits=(p,))
        ps_use(2, [bank], ev)
        mT_ev[h] = ev
    if stop_after == "p2":
        if dump is not None:
            dump.clear()
            dump["a_sims"] = sims[0:NB, :]
            dump["b_rank"] = rankt[0:NB, :]
            dump["c_mk32"] = mk32[0:NB, :]
            dump["d_mTbb"] = mTbb[0:NB, :]
        return _finish_early()
    d_ekb = dma("gpsimd", ekbb[:], ekb[:, :], sem="gpsimd",
                waits=(qk_last, ("tV", S.cnt["tV"]), ("tA", S.cnt["tA"])))

    band_tok = []
    w0_marks = (("tP", S.cnt["tP"]), ("tV", S.cnt["tV"]), ("tA", S.cnt["tA"]))
    for hl in range(HPW):
        b0 = 64 if hl % 2 == 0 else 0
        t1 = ve(lambda e, hl=hl, b0=b0: e.memset(k_ext[hl][b0:b0 + 64, :], 0.0),
                waits=(p7_tp,) + tuple(p7_done))
        t2 = ve(lambda e, hl=hl, b0=b0: e.memset(q_ext[hl][b0:b0 + 64, :], 0.0),
                waits=w0_marks)
        band_tok += [t1, t2]
    ofullT_done = []
    exp_scale = 0.125
    for w in range(NW):
        buf = 0
        wtokw = wv_tok[w]
        urow_tok = [None] * HPW
        wrow_tok = [None] * HPW
        ones_tok = [None] * HPW
        if w in qkv_cache:
            qk_ev, v_ev = qkv_cache.pop(w)
        else:
            qk_ev, v_ev = emit_qkv(w, w0_marks)
        if stop_after == f"w{w}q":
            if dump is not None:
                dump.clear()
                dump["a_mT"] = mTbb[0:NB, :]
                dump["b_kx"] = k_ext[0][:, 0:1024]
                dump["c_qx"] = q_ext[0][:, 0:1024]
            return _finish_early()
        for hl in range(HPW):
            h = w * HPW + hl
            odd = h % 2
            ur0 = 64 if not odd else 0
            wts = U.waits(("qext_u", hl)) + (tuple(band_tok) if w == 0 else ())
            urow_tok[hl] = dma("sync", q_ext[hl][ur0:ur0 + NB, :], ucst[:, :],
                               waits=wts)
            colp = ur0
            wrow_ev = None
            wrow_all = []
            for nt in range(NTOK // NS):
                pi = nt % 2
                wtr = ([mT_ev[h], d_ekb] + ps_waits(pi, [0])
                       + list(U.waits(("kext_w", hl)))
                       + (band_tok if w == 0 and hl == 0 and nt == 0 else []))
                p = pe(lambda e, h=h, nt=nt, pi=pi: e.matmul(
                    ps[pi][0:NB, 0:NS],
                    mTbb[:, h * NB:(h + 1) * NB],
                    ekbb[:, nt * NS:(nt + 1) * NS],
                    start=True, stop=True),
                    waits=tuple(wtr))
                if colp == 0:
                    wrow_ev = ve(lambda e, hl=hl, nt=nt, pi=pi: e.tensor_copy(
                        k_ext[hl][0:NB, nt * NS:(nt + 1) * NS],
                        ps[pi][0:NB, 0:NS]),
                        waits=(p,))
                    ps_use(pi, [0], wrow_ev)
                else:
                    wb = nt % 2
                    ww = U.waits(("wxs", wb))
                    cv = ve(lambda e, wb=wb, pi=pi: e.tensor_copy(
                        wxs[wb][:], ps[pi][0:NB, 0:NS]),
                        waits=(p,) + ww + (w0_marks if w == 0 else ()))
                    ps_use(pi, [0], cv)
                    wrow_ev = dma("sync", k_ext[hl][colp:colp + NB, nt * NS:(nt + 1) * NS],
                                  wxs[wb][:], waits=(cv,))
                    U.use(("wxs", wb), wrow_ev)
                wrow_all.append(wrow_ev)
            wrow_tok[hl] = wrow_all
            onw = U.waits(("vext_o", hl))
            if w == 0:
                onw = onw + (p7_tp,) + tuple(p7_done) + w0_marks
            ones_tok[hl] = ve(lambda e, hl=hl: e.memset(
                v_ext[hl][:].rearrange("p (a b) -> p a b", b=65)[:, :, 64:65], 1.0),
                waits=onw)

        if stop_after == f"w{w}u":
            if dump is not None:
                dump.clear()
                dump["a_mT"] = mTbb[0:NB, :]
                dump["b_idf"] = idf[:, :]
                dump["c_pbb"] = pbb[0:1, :]
                dump["d_fbb"] = fbb[0:1, :]
                dump["e_recr"] = recr[0:65, 0:QH]
                dump["f_ones"] = onesb[0:1, 0:128]
                dump["g_stat"] = stat[0][:, 0:12]
            return _finish_early()
        for hl in range(HPW):
            h = w * HPW + hl
            odd = h % 2
            base = 0
            head_ready = list(wrow_tok[hl]) + [urow_tok[hl], ones_tok[hl]]
            if w == 0:
                head_ready += band_tok
            head_ready += [qk_ev[("q", hl // 2, nt)][hl % 2] for nt in range(NQSEC)]
            head_ready += [qk_ev[("k", hl // 2, nt)][hl % 2] for nt in range(NTOK // NS)]
            head_ready += [v_ev[(tb, hl)] for tb in range(NTILE)]
            last_attn = None
            for qh in range(NQH):
                oslot = (h * NQH + qh) % 2
                po = ps[2 + oslot]
                lastv = None
                ex_half = {}
                for step in range(NB + 1):
                    if step < NB:
                        kb = step
                        pi = kb % 2
                        abuf = kb % 2
                        for nt in range(NSS):
                            wsc = ps_waits(pi, [nt])
                            if kb < 2 and nt == 0:
                                wsc += head_ready
                            last = pe(lambda e, hl=hl, kb=kb, qh=qh, nt=nt, pi=pi,
                                      base=base: e.matmul(
                                ps[pi][:, nt * SS:(nt + 1) * SS],
                                k_ext[hl][:, kb * 128:(kb + 1) * 128],
                                q_ext[hl][:, qh * QH + nt * SS: qh * QH + (nt + 1) * SS],
                                start=True, stop=True),
                                waits=tuple(wsc))
                            wex = [last] + list(U.waits(("attnT", abuf, nt)))
                            exn = ae(lambda e, abuf=abuf, pi=pi, nt=nt: e.activation(
                                attnT[abuf][:, nt * SS:(nt + 1) * SS],
                                ps[pi][:, nt * SS:(nt + 1) * SS],
                                AF.Exp, bias=0.0, scale=exp_scale),
                                waits=tuple(wex))
                            ps_use(pi, [nt], exn)
                            ex_half[(kb, nt)] = exn
                    if step >= 1:
                        kb = step - 1
                        abuf = kb % 2
                        for nt in range(NSS):
                            wv_ = [ex_half[(kb, nt)], t_ones]
                            if kb == 0:
                                wv_ += ps_waits(2 + oslot, [nt])
                            lastv = pe(lambda e, hl=hl, kb=kb, nt=nt, po=po: e.matmul(
                                po[0:65, nt * SS:(nt + 1) * SS],
                                v_ext[hl][:, kb * 65:kb * 65 + 65],
                                attnT[kb % 2][:, nt * SS:(nt + 1) * SS],
                                start=(kb == 0), stop=(kb == NB - 1)),
                                waits=tuple(wv_))
                            U.use(("attnT", abuf, nt), lastv)
                chunk = h // 2
                wrr = U.waits(("recr",))
                er = ve(lambda e, po=po: e.tensor_copy(recr[64:65, :], po[64:65, 0:QH]),
                        waits=(lastv,) + wrr)
                def _recip(e):
                    with nc.allow_low_precision(reason="softmax renorm factor; attention-path error is diluted ~50x in the output"):
                        return e.reciprocal(recr[64:65, :], recr[64:65, :])
                rr = ve(_recip, waits=(er,))
                dwr = dma("sync", recd[h * NQH + qh:h * NQH + qh + 1, :], recr[64:65, :],
                          sem="dB", waits=(rr,))
                U.use(("recr",), dwr)
                wrb = U.waits(("recb",))
                dbr = dma("sync", recb[:],
                          bass.AP(recd.tensor, (h * NQH + qh) * QH, [[0, 64], [1, QH]]),
                          sem="dB", waits=(dwr,) + wrb)
                if not odd:
                    evo = ve(lambda e, po=po, chunk=chunk, qh=qh: e.tensor_tensor(
                        ofullT[chunk][0:64, qh * QH:(qh + 1) * QH], po[0:64, 0:QH],
                        recb[:], ALU.mult), waits=(dbr, lastv))
                    fin = evo
                else:
                    wot = U.waits(("otmp",))
                    evo = ve(lambda e, po=po: e.tensor_tensor(
                        otmp[:], po[0:64, 0:QH], recb[:], ALU.mult),
                        waits=(dbr, lastv) + wot)
                    fin = dma("sync", ofullT[chunk][64:128, qh * QH:(qh + 1) * QH],
                              otmp[:], sem="dB", waits=(evo,))
                    U.use(("otmp",), fin)
                U.use(("recb",), evo)
                ps_use(2 + oslot, [0, 1], evo)
                ofullT_done.append(fin)
                last_attn = lastv
            if stop_after == f"w{w}h{hl}":
                if dump is not None:
                    dump.clear()
                    dump["a_mT"] = mTbb[0:NB, :]
                    dump["b_kx"] = k_ext[0][:, 0:1024]
                    dump["c_at"] = attnT[0][:, 0:1024]
                    dump["d_rb"] = recb[0:64, 0:QH]
                return _finish_early()
            U.use(("kext_w", hl), last_attn)
            U.use(("qext_u", hl), last_attn)
            U.use(("vext_o", hl), last_attn)
        if w + 1 < NW:
            wv_tok[w + 1] = wave_w_dma(w + 1)
    waves_marks = (("tP", S.cnt["tP"]), ("tV", S.cnt["tV"]), ("tA", S.cnt["tA"]))

    if stop_after == "waves":
        if dump is not None:
            dump.clear()
            dump["a_kx"] = k_ext[0][:, 0:1024]
            dump["b_qx"] = q_ext[0][:, 0:1024]
            dump["c_vx"] = v_ext[0][:, 0:1024]
            dump["d_ek"] = ekbb[0:NB, 0:1024]
            dump["e_mT"] = mTbb[0:NB, :]
        return _finish_early()
    wpj_tok = [None] * C
    for c in range(C):
        wpj_tok[c] = dma("gpsimd", wpjb[c][:], wprojT[c * 128:(c + 1) * 128, :],
                         sem="gpsimd", waits=U.waits(("wvW", 0, c)))
    of_done = tuple(ofullT_done)
    atn_tok = [None] * NOTILE
    proj_last_pe = None
    for tb in range(NOTILE):
        pt = tb % 4
        wtr = ps_waits(pt, [0, 1])
        if tb == 0:
            wtr += list(of_done)
        last = None
        for nh in range(2):
            for i, c in enumerate(range(C)):
                wmm = tuple(wtr + [wpj_tok[c]]) if (nh == 0 and i == 0) else (wpj_tok[c],)
                last = pe(lambda e, tb=tb, c=c, nh=nh, pt=pt,
                          st_c=(i == 0): e.matmul(
                    ps[pt][:, nh * NS:(nh + 1) * NS],
                    ofullT[c][:, tb * 128:(tb + 1) * 128],
                    wpjb[c][:, nh * NS:(nh + 1) * NS],
                    start=st_c, stop=False),
                    waits=wmm)
            last = pe(lambda e, nh=nh, pt=pt: e.matmul(
                ps[pt][:, nh * NS:(nh + 1) * NS],
                onesb[0:1, 0:128],
                pbb[0:1, nh * NS:(nh + 1) * NS],
                start=False, stop=True),
                waits=(d_pb, t_ones))
        proj_last_pe = last
        atw = U.waits(("atn", tb))
        atn_tok[tb] = ve(lambda e, tb=tb, pt=pt: e.tensor_copy(
            atn[tb][:], ps[pt][:, 0:D]), waits=(last,) + atw)
        ps_use(pt, [0, 1], atn_tok[tb])

    if stop_after == "p5":
        if dump is not None:
            dump.clear()
            dump["a_of0"] = ofullT[0][:, 0:min(NOWN, 1024)]
            dump["b_of7"] = ofullT[C - 1][:, 0:min(NOWN, 1024)]
            dump["c_atn0"] = atn[0][:, 0:D]
            dump["d_atnL"] = atn[NOTILE - 1][:, 0:D]
        return _finish_early()
    p6_marks = (("tP", S.cnt["tP"]), ("tV", S.cnt["tV"]))
    mh_tok = [None] * C
    ml_tok = [None] * C
    for c in range(C):
        mh_tok[c] = dma("sync", MhS[c][:], md[c * 128:(c + 1) * 128, :],
                        waits=waves_marks)
        ml_tok[c] = dma("sync", MlS[c][:], md[D + c * 128:D + (c + 1) * 128, :],
                        waits=waves_marks)
    xo_tok = [None] * NOTILE

    def xo_dma(tb):
        wts = U.waits(("xot", tb % 2))
        if tb < 2:
            wts = wts + tuple(s1_done) + p6_marks
        return dma("sync", xot[tb % 2][:], xo[tb * 128:(tb + 1) * 128, :],
                   waits=wts)

    for tb in range(min(2, NOTILE)):
        xo_tok[tb] = xo_dma(tb)
    tr_i = 0
    n_out_dma = 0
    for tb in range(NOTILE):
        sc = h1s6
        st_ = stat[tb % 2]
        ob = outb[tb % 3]
        wob = U.waits(("outb", tb % 3))
        if tb < 3:
            wob = wob + p6_marks
        x1t = ve(lambda e, tb=tb, ob=ob: e.tensor_tensor(
            ob[:], xot[tb % 2][:], atn[tb][:], ALU.add),
            waits=(xo_tok[tb], atn_tok[tb]) + wob)
        U.use(("xot", tb % 2), x1t)
        if tb + 2 < NOTILE:
            xo_tok[tb + 2] = xo_dma(tb + 2)
        v14, _ = ln_tile(lambda a, b, ob=ob: ob[:, a:b], sc, st_, (x1t,),
                         extra_sc_waits=p6_marks if tb == 0 else ())
        U.use(("outb", tb % 3), v14)
        h2ev_t = [None] * C
        for c in range(C):
            bank = tr_i % 2
            wtr = [v14] + ps_waits(2, [bank])
            if tb == 0 and c == 0:
                wtr.append(proj_last_pe)
            p = pe(lambda e, sc=sc, c=c, bank=bank: e.transpose(
                ps[2][:, bank * NS:bank * NS + 128], sc[:, c * 128:(c + 1) * 128], idf[:]),
                waits=tuple(wtr))
            ev = ae(lambda e, c=c, tb=tb, bank=bank: e.activation(
                h2T[c][:, tb * 128:(tb + 1) * 128], ps[2][:, bank * NS:bank * NS + 128],
                AF.Copy),
                waits=(p,))
            ps_use(2, [bank], ev)
            U.use(("h1s", id(sc) % 97), p)
            h2ev_t[c] = ev
            tr_i += 1
        pt = tb % 2
        wtr = ps_waits(pt, [0, 1])
        last = None
        for nh in range(2):
            seq = [(MhS[c], mh_tok[c], c) for c in range(C)] + \
                  [(MlS[c], ml_tok[c], c) for c in range(C)]
            for i, (mt, mtok, c) in enumerate(seq):
                wmm = [mtok, h2ev_t[c]]
                if nh == 0 and i == 0:
                    wmm = wtr + wmm
                last = pe(lambda e, tb=tb, c=c, nh=nh, pt=pt, mt=mt,
                          st_c=(i == 0): e.matmul(
                    ps[pt][:, nh * NS:(nh + 1) * NS],
                    h2T[c][:, tb * 128:(tb + 1) * 128],
                    mt[:, nh * NS:(nh + 1) * NS],
                    start=st_c, stop=False),
                    waits=tuple(wmm))
            last = pe(lambda e, nh=nh, pt=pt: e.matmul(
                ps[pt][:, nh * NS:(nh + 1) * NS],
                onesb[0:1, 0:128],
                fbb[0:1, nh * NS:(nh + 1) * NS],
                start=False, stop=True),
                waits=(d_fb, t_ones))
        evo = ve(lambda e, ob=ob, pt=pt: e.tensor_tensor(
            ob[:], ob[:], ps[pt][:, 0:D], ALU.add),
            waits=(last, v14))
        ps_use(pt, [0, 1], evo)
        dot = dma("sync", out[tb * 128:(tb + 1) * 128, :], ob[:],
                  waits=(evo,))
        U.use(("outb", tb % 3), dot)
        n_out_dma += 16

    if stop_after == "p6":
        if dump is not None:
            dump.clear()
            dump["a_h2T0"] = h2T[0][:, 0:min(NOWN, 1024)]
            dump["b_h2TL"] = h2T[C - 1][:, 0:min(NOWN, 1024)]
        return _finish_early()
    S.op("sync", lambda e: None,
         waits=tuple((sn, S.cnt[sn]) for sn in POOLS["sync"]))

    block = nc.Block()
    with block:
        S.replay(block)
    stats = {e: len(S.thunks[e]) for e in ENGS}
    probes = {"atn": [t.name for t in atn], "mk32": mk32.name,
              "ofullT": [t.name for t in ofullT], "h2T": [t.name for t in h2T],
              "sims": sims.name}
    return nc, stats, probes


def make_consts(NTOK=4096, NOWN=2048):
    D = 1024
    NB = NTOK // 128
    NTILE = NB
    i = np.arange(D, dtype=np.int64)
    Mm = (i[:, None] * i[None, :]) % D
    ang = 2.0 * np.pi * Mm.astype(np.float64) / D
    Cm = np.cos(ang)
    Sm = -np.sin(ang)
    c1 = Cm.astype(np.float32)
    c2 = (Cm - c1.astype(np.float64)).astype(np.float32)
    s1 = Sm.astype(np.float32)
    s2 = (Sm - s1.astype(np.float64)).astype(np.float32)
    e32v = np.zeros((128, NB * NTILE), dtype=np.float32)
    for t in range(NTILE):
        e32v[:, NB * t + t] = 1.0 / 128.0
    ekb = np.zeros((NB, NTOK), dtype=np.float32)
    for kbi in range(NB):
        ekb[kbi, kbi * 128:(kbi + 1) * 128] = 1.0
    identf = np.eye(128, dtype=np.float32)
    return dict(c1=c1, c2=c2, s1=s1, s2=s2, e32v=e32v, ekb=ekb, identf=identf)


def make_inputs_for_core(core, full, NTOK=4096, NOWN=2048, consts=None):
    g = core % 2
    b = core // 2
    x = np.asarray(full["L0_attn_x"], dtype=np.float32)
    NBl = NTOK // 128
    d = dict(consts)
    d["xb"] = np.ascontiguousarray(x[b])
    d["xo"] = np.ascontiguousarray(x[b, g * NOWN:(g + 1) * NOWN])
    d["wqkvT"] = np.ascontiguousarray(np.asarray(full["L0_attn_qkv_w"], np.float32).T)
    d["wprojT"] = np.ascontiguousarray(np.asarray(full["L0_attn_proj_w"], np.float32).T)
    d["pb"] = np.ascontiguousarray(np.asarray(full["L0_attn_proj_b"], np.float32).reshape(1, -1))
    d["wrT"] = np.ascontiguousarray(np.asarray(full["L0_ffn_w_freq_real"], np.float32).T)
    d["wiT"] = np.ascontiguousarray(np.asarray(full["L0_ffn_w_freq_imag"], np.float32).T)
    d["fb"] = np.ascontiguousarray(np.asarray(full["L0_ffn_bias"], np.float32).reshape(1, -1))
    u = np.zeros((NBl, NOWN), dtype=np.float32)
    for ql in range(NOWN):
        u[(g * NOWN + ql) // 128, ql] = 1.0
    d["ucst"] = u.astype(ml_dtypes.bfloat16)
    d["goff"] = np.array([[g * (NOWN // min(512, NOWN))]], dtype=np.int32)
    return d


from concourse.bass_utils import run_bass_kernel_spmd

_BUILD_CACHE = {}


def _get_program():
    if "nc" not in _BUILD_CACHE:
        nc, stats, probes = build_program(4096, 2048, 4)
        _BUILD_CACHE["nc"] = nc
    return _BUILD_CACHE["nc"]


def kernel(L0_attn_x, L0_attn_qkv_w, L0_attn_proj_w, L0_attn_proj_b,
           L0_ffn_w_freq_real, L0_ffn_w_freq_imag, L0_ffn_bias):
    full = {"L0_attn_x": L0_attn_x, "L0_attn_qkv_w": L0_attn_qkv_w,
            "L0_attn_proj_w": L0_attn_proj_w, "L0_attn_proj_b": L0_attn_proj_b,
            "L0_ffn_w_freq_real": L0_ffn_w_freq_real,
            "L0_ffn_w_freq_imag": L0_ffn_w_freq_imag, "L0_ffn_bias": L0_ffn_bias}
    consts = make_consts(4096, 2048)
    in_maps = [make_inputs_for_core(c, full, 4096, 2048, consts) for c in range(8)]
    nc = _get_program()
    res = run_bass_kernel_spmd(nc, in_maps, core_ids=list(range(8)))
    out = np.empty((4, 4096, 1024), np.float32)
    for c in range(8):
        out[c // 2, (c % 2) * 2048:(c % 2 + 1) * 2048] = res.results[c]["out"]
    return out


# version 11
# speedup vs baseline: 1.0035x; 1.0035x over previous
import numpy as np
import ml_dtypes
import concourse.bass as bass
import concourse.mybir as mybir

F32 = mybir.dt.float32
BF16 = mybir.dt.bfloat16
I32 = mybir.dt.int32
AF = mybir.ActivationFunctionType
ALU = mybir.AluOpType
PE = mybir.EngineType.PE
AXX = mybir.AxisListType.X

ENGS = ("sync", "gpsimd", "vector", "scalar", "tensor")


class Sched:
    def __init__(self, nc):
        self.nc = nc
        self.thunks = {e: [] for e in ENGS}
        self.sems = {}
        self.cnt = {}
        self.last_wait = {}

    def sem(self, name):
        self.sems[name] = self.nc.alloc_semaphore(name)
        self.cnt[name] = 0

    def op(self, eng, build, waits=(), inc=None, amt=1):
        eff = []
        for wt in waits:
            if wt is None:
                continue
            sn, v = wt
            if sn is None or v is None or v <= 0:
                continue
            key = (eng, sn)
            if self.last_wait.get(key, 0) >= v:
                continue
            self.last_wait[key] = v
            eff.append((self.sems[sn], v))
        newc = None
        semh = None
        if inc is not None:
            self.cnt[inc] += amt
            newc = self.cnt[inc]
            semh = self.sems[inc]

        def thunk(e, eff=eff, build=build, semh=semh, amt=amt):
            for (s, v) in eff:
                e.wait_ge(s, v)
            ins2 = build(e)
            if semh is not None:
                ins2.then_inc(semh, amt)
        self.thunks[eng].append(thunk)
        return (inc, newc)

    def replay(self, block):
        for e in ENGS:
            fns = self.thunks[e]

            def run(engine, fns=fns):
                for f in fns:
                    f(engine)
            getattr(block, e)(run)


class Uses:
    def __init__(self):
        self.m = {}

    def use(self, key, tok):
        if tok is None or tok[0] is None:
            return
        d = self.m.setdefault(key, {})
        sn, v = tok
        if d.get(sn, 0) < v:
            d[sn] = v

    def waits(self, key, clear=True):
        d = self.m.get(key, {})
        w = tuple(d.items())
        if clear and key in self.m:
            del self.m[key]
        return w


def build_program(NTOK=4096, NOWN=2048, HPW=4, stop_after=None, dump=None):
    D, H, DH = 1024, 16, 64
    NB = NTOK // 128
    NTILE = NB
    NOTILE = NOWN // 128
    QH = min(1024, NOWN)
    NQH = NOWN // QH
    NW = H // HPW
    C = D // 128
    NS = 512
    SS = min(NS, QH); NSS = QH // SS
    QSEC = min(NS, NOWN); NQSEC = NOWN // QSEC
    TOPK = max(1, NB // 2)
    NBR = 32
    KEXT = 64 + NBR

    nc = bass.Bass("TRN2", target_bir_lowering=False, debug=False,
                   dynamic_dma_scratch_size=16384)

    def din(name, shape, dt=F32):
        return nc.dram_tensor(name, shape, dt, kind="ExternalInput").ap()

    xb = din("xb", [NTOK, D])
    xo = din("xo", [NOWN, D])
    wqkvT = din("wqkvT", [D, 3 * D])
    wprojT = din("wprojT", [D, D])
    pbd = din("pb", [1, D])
    wrTd = din("wrT", [D, D])
    wiTd = din("wiT", [D, D])
    fbd = din("fb", [1, D])
    c1d = din("c1", [D, D]); c2d = din("c2", [D, D])
    s1d = din("s1", [D, D]); s2d = din("s2", [D, D])
    ucst = din("ucst", [NB, NOWN], BF16)
    e32v = din("e32v", [128, NB * NTILE])
    ekb = din("ekb", [NB, NTOK])
    identf = din("identf", [128, 128])
    goffd = nc.dram_tensor("goff", [1, 1], I32, kind="ExternalInput").ap()
    out = nc.dram_tensor("out", [NOWN, D], F32, kind="ExternalOutput").ap()
    recd = nc.dram_tensor("recd", [H * NQH, QH], BF16, kind="Internal").ap()
    md = nc.dram_tensor("md", [2 * D, D], BF16, kind="Internal").ap()

    S = Sched(nc)
    for sname in ("tP", "tV", "tA", "dO"):
        S.sem(sname)
    POOLS = {"sync": ["dS%d" % i for i in range(6)],
             "gpsimd": ["dG%d" % i for i in range(4)],
             "dB": ["dB%d" % i for i in range(2)]}
    for names in POOLS.values():
        for n in names:
            S.sem(n)
    pool_rr = {k: 0 for k in POOLS}
    U = Uses()

    cur = [nc.sbuf_top]

    def sb(name, shape, dt):
        nbytes = (shape[1] * mybir.dt.size(dt) + 31) // 32 * 32
        cur[0] -= nbytes
        return nc.alloc_sbuf_tensor_at(name, shape, dt, offset=cur[0])

    def sb_at(name, shape, dt, offset):
        return nc.alloc_sbuf_tensor_at(name, shape, dt, offset=offset)

    RA = cur[0] - max(C * NTOK * 2, 4 * C * 2048)
    h1T = [sb_at(f"h1T{c}", [128, NTOK], BF16, RA + c * NTOK * 2) for c in range(C)]
    Chi = [sb_at(f"Chi{c}", [128, D], BF16, RA + c * 2048) for c in range(C)]
    Shi = [sb_at(f"Shi{c}", [128, D], BF16, RA + (C + c) * 2048) for c in range(C)]
    Clo = [sb_at(f"Clo{c}", [128, D], BF16, RA + (2 * C + c) * 2048) for c in range(C)]
    Slo = [sb_at(f"Slo{c}", [128, D], BF16, RA + (3 * C + c) * 2048) for c in range(C)]
    cur[0] = RA

    kx_b = NTOK * 2
    qx_b = NOWN * 2
    vx_b = ((NB * 65 * 2 + 31) // 32) * 32
    RC = cur[0] - max(HPW * (kx_b + qx_b + vx_b), 4 * C * 2048, NOTILE * 2048 + 2 * C * 2048)
    k_ext = [sb_at(f"kext{h}", [128, NTOK], BF16, RC + h * kx_b) for h in range(HPW)]
    q_ext = [sb_at(f"qext{h}", [128, NOWN], BF16, RC + HPW * kx_b + h * qx_b)
             for h in range(HPW)]
    v_ext = [sb_at(f"vext{h}", [128, NB * 65], BF16,
                   RC + HPW * (kx_b + qx_b) + h * vx_b) for h in range(HPW)]
    R1h = [sb_at(f"R1h{c}", [128, D], BF16, RC + c * 2048) for c in range(C)]
    R1l = [sb_at(f"R1l{c}", [128, D], BF16, RC + (C + c) * 2048) for c in range(C)]
    R2h = [sb_at(f"R2h{c}", [128, D], BF16, RC + (2 * C + c) * 2048) for c in range(C)]
    R2l = [sb_at(f"R2l{c}", [128, D], BF16, RC + (3 * C + c) * 2048) for c in range(C)]
    atn = [sb_at(f"atn{t}", [128, D], BF16, RC + t * 2048) for t in range(NOTILE)]
    MhS = [sb_at(f"MhS{c}", [128, D], BF16, RC + NOTILE * 2048 + c * 2048) for c in range(C)]
    MlS = [sb_at(f"MlS{c}", [128, D], BF16, RC + NOTILE * 2048 + (C + c) * 2048) for c in range(C)]
    cur[0] = RC

    RD = cur[0] - max(C * NOWN * 2, 32768)
    ofullT = [sb_at(f"ofT{c}", [128, NOWN], BF16, RD + c * NOWN * 2) for c in range(C)]
    h2T = [sb_at(f"h2T{c}", [128, NOWN], BF16, RD + c * NOWN * 2) for c in range(C)]
    wfsr = [sb_at("wfsr%d" % i, [128, D], F32, RD + i * 4096) for i in range(2)]
    wfsi = [sb_at("wfsi%d" % i, [128, D], F32, RD + 8192 + i * 4096) for i in range(2)]
    whr = [sb_at("whr%d" % i, [128, D], BF16, RD + 16384 + i * 2048) for i in range(2)]
    wlr = [sb_at("wlr%d" % i, [128, D], BF16, RD + 20480 + i * 2048) for i in range(2)]
    whi = [sb_at("whi%d" % i, [128, D], BF16, RD + 24576 + i * 2048) for i in range(2)]
    wli = [sb_at("wli%d" % i, [128, D], BF16, RD + 28672 + i * 2048) for i in range(2)]
    xstn = 2
    xst = [sb_at("xst%d" % i, [128, 2 * D], F32, RD + i * 8192) for i in range(2)]
    wst = [sb_at("wst%d" % i, [128, 2 * D], F32, RD + 16384 + i * 8192) for i in range(2)]
    cur[0] = RD

    RB = cur[0] - 16384
    cstg = [sb_at("cstg%d" % i, [128, D], F32, RB + i * 4096) for i in range(2)]
    mhv = [sb_at("mhv0", [128, D], BF16, RB + 8192)]
    mlv = [sb_at("mlv0", [128, D], BF16, RB + 10240)]
    wvcols = 3 * DH * HPW
    wvW = [[sb_at(f"wvW{b}_{c}", [128, wvcols], BF16, RB + (b * C + c) * wvcols * 2)
            for c in range(C)] for b in range(1)]
    wpjb = [sb_at(f"wpjb{c}", [128, D], BF16, RB + c * 2048) for c in range(C)]
    cur[0] = RB

    R5 = cur[0] - 8192
    sc32 = [sb_at("sc32%d" % i, [128, D], F32, R5 + i * 4096) for i in range(2)]
    xot = [sb_at("xot%d" % i, [128, D], F32, R5 + i * 4096) for i in range(2)]
    cur[0] = R5

    R6 = cur[0] - 17504
    e32s = sb_at("e32s", [128, NB * NTILE], F32, R6)
    bmsb = sb_at("bmsb", [NB, D], F32, R6)
    bmT = sb_at("bmT", [128, C * NB], F32, R6 + 4096)
    qms = sb_at("qms", [NB, D], F32, R6 + 4096 + ((C * NB * 4 + 31) // 32) * 32)
    kms = sb_at("kms", [NB, D], F32, R6 + 8192 + ((C * NB * 4 + 31) // 32) * 32)
    _o = R6 + 12288 + ((C * NB * 4 + 31) // 32) * 32
    qmT = sb_at("qmT", [64, 2 * C * NB], F32, _o)
    kmT = sb_at("kmT", [64, 2 * C * NB], F32, _o + ((2 * C * NB * 4 + 31) // 32) * 32)
    _o2 = _o + 2 * (((C * NB * 4 + 31) // 32) * 32)
    sims = sb_at("sims", [NB, H * NB], F32, _o2)
    _oq = R6 + 4096 + ((C * NB * 4 + 31) // 32) * 32
    rankt = sb_at("rankt", [NB, H * NB], F32, _oq)
    gtt = sb_at("gtt", [NB, H * NB], F32, _oq + ((H * NB * 4 + 31) // 32) * 32)
    mk32 = sb_at("mk32", [NB, H * NB], F32, _oq + ((H * NB * 4 + 31) // 32) * 32)
    attnT = [sb_at("attnT%d" % i, [128, QH], BF16, R6 + i * (QH * 2)) for i in range(2)]
    recb = sb_at("recb", [64, QH], BF16, R6 + 2 * QH * 2)
    otmp = sb_at("otmp", [64, QH], BF16, R6 + 3 * QH * 2)
    outb = [sb_at("outb%d" % i, [128, D], F32, R6 + i * 4096) for i in range(3)]
    h1s6 = sb_at("h1s6", [128, D], F32, R6 + 12288)
    cur[0] = R6

    h1s = [sb_at("h1s0", [128, D], F32, RC + HPW * kx_b)]
    ekbb = sb_at("ekbb", [NB, NTOK], BF16, R6 + 8256)
    idf = sb("idf", [128, 128], F32)
    mTbb = sb("mTbb", [NB, H * NB], BF16)
    stat = [sb("stat%d" % i, [128, 20], F32) for i in range(2)]
    onesb = sb("onesb", [1, 128], BF16)
    pbb = sb_at("pbb", [1, D], BF16, R5)
    fbb = sb("fbb", [1, D], BF16)
    recr = sb("recr", [65, QH], BF16)
    wxs = [sb_at("wxs0", [NB, NS], BF16, R5 + 2048),
           sb_at("wxs1", [NB, NS], BF16, R6 + 16448)]
    goffs = sb("goffs", [1, 1], I32)
    epsb = sb("epsb", [128, 1], F32)
    assert cur[0] >= nc.sbuf_base + 160, f"SBUF overflow: cur={cur[0]} base={nc.sbuf_base}"

    ps = [nc.alloc_psum_tensor(f"ps{i}", [128, 1024], F32) for i in range(4)]

    def ps_waits(i, halves):
        w = []
        for hf in halves:
            w += list(U.waits(("psb", 2 * i + hf)))
        return w

    def ps_use(i, halves, tok):
        for hf in halves:
            U.use(("psb", 2 * i + hf), tok)

    def dma(eng, outap, inap, sem=None, waits=()):
        if sem == "dO":
            return S.op(eng, lambda e, o=outap, i=inap: e.dma_start(out=o, in_=i),
                        waits=waits, inc="dO", amt=16)
        pool = sem if sem in POOLS else eng
        names = POOLS[pool]
        sn = names[pool_rr[pool] % len(names)]
        pool_rr[pool] += 1
        w = list(waits) + [(sn, S.cnt[sn])]
        return S.op(eng, lambda e, o=outap, i=inap: e.dma_start(out=o, in_=i),
                    waits=w, inc=sn, amt=16)

    def ve(build, waits=()):
        return S.op("vector", build, waits=waits, inc="tV")

    def ae(build, waits=()):
        return S.op("scalar", build, waits=waits, inc="tA")

    def pe(build, waits=()):
        return S.op("tensor", build, waits=waits, inc="tP")

    d_id = dma("sync", idf[:], identf[:, :])
    d_e32 = dma("sync", e32s[:], e32v[:, :])
    d_go = dma("sync", goffs[:], goffd[:, :])
    d_fb = dma("gpsimd", fbb[:], fbd[:, :], sem="gpsimd")
    t_ones = ve(lambda e: e.memset(onesb[:], 1.0))
    t_eps = ve(lambda e: e.memset(epsb[:], 1e-5))

    goff_box = {}

    def load_goff(e):
        (ld,), (val,) = nc.values_load_multi_w_load_instructions(
            goffs[0:1, 0:1], engines=[PE],
            min_val=0, max_val=NOWN // QSEC, skip_runtime_bounds_check=True)
        goff_box["v"] = val
        return ld
    S.op("tensor", load_goff, waits=(d_go,))

    def wave_w_dma(w, buf=0):
        toks = []
        for c in range(C):
            src = bass.AP(wqkvT.tensor, (c * 128) * (3 * D) + w * (DH * HPW),
                          [[3 * D, 128], [D, 3], [1, DH * HPW]])
            wts = U.waits(("wvW", 0, c))
            toks.append(dma("gpsimd", wvW[0][c][:].rearrange("p (a b) -> p a b", a=3),
                        src, sem="gpsimd", waits=wts))
        return toks

    wv_tok = [None] * NW

    csC_tok = [None] * C
    csS_tok = [None] * C
    for c in range(C):
        for (hid, lod, d1, d2, toks) in ((Chi, Clo, c1d, c2d, csC_tok),
                                         (Shi, Slo, s1d, s2d, csS_tok)):
            dc1 = dma("sync", cstg[0][:], d1[c * 128:(c + 1) * 128, :],
                      waits=U.waits(("cstg", 0)))
            dc2 = dma("sync", cstg[1][:], d2[c * 128:(c + 1) * 128, :],
                      waits=U.waits(("cstg", 1)))
            wsc = U.waits(("sc32", 0))
            t1 = ve(lambda e: e.tensor_tensor(sc32[0][:], cstg[0][:], cstg[1][:], ALU.add),
                    waits=(dc1, dc2) + wsc)
            t2 = ae(lambda e, hid=hid, c=c: e.activation(hid[c][:], sc32[0][:], AF.Copy),
                    waits=(t1,))
            t3 = ve(lambda e, hid=hid, lod=lod, c=c: e.tensor_tensor(
                lod[c][:], sc32[0][:], hid[c][:], ALU.subtract),
                waits=(t2,))
            U.use(("cstg", 0), t1)
            U.use(("cstg", 1), t1)
            U.use(("sc32", 0), t3)
            toks[c] = t3

    r1_tok = [None] * C
    r2_tok = [None] * C
    for dc in range(C):
        bi = dc % 2
        wts = U.waits(("wfs", bi))
        srcr = bass.AP(wrTd.tensor, dc * 128, [[D, 128], [128 * D, C], [1, 128]])
        srci = bass.AP(wiTd.tensor, dc * 128, [[D, 128], [128 * D, C], [1, 128]])
        dwr = dma("sync", wfsr[bi][:].rearrange("p (o j) -> p o j", o=C), srcr, waits=wts)
        dwi = dma("sync", wfsi[bi][:].rearrange("p (o j) -> p o j", o=C), srci, waits=wts)
        wsp = U.waits(("wsp", bi))
        s1a = ve(lambda e, bi=bi: e.tensor_copy(whr[bi][:], wfsr[bi][:]), waits=(dwr,) + wsp)
        s1b = ve(lambda e, bi=bi: e.tensor_tensor(wlr[bi][:], wfsr[bi][:], whr[bi][:], ALU.subtract),
                 waits=(s1a,))
        s1c = ve(lambda e, bi=bi: e.tensor_copy(whi[bi][:], wfsi[bi][:]), waits=(dwi,) + wsp)
        s1d_ = ve(lambda e, bi=bi: e.tensor_tensor(wli[bi][:], wfsi[bi][:], whi[bi][:], ALU.subtract),
                  waits=(s1c,))
        U.use(("wfs", bi), s1b)
        U.use(("wfs", bi), s1d_)
        wsplit = [s1b, s1d_]
        wtr0 = ps_waits(0, [0, 1]) + wsplit
        wtr1 = ps_waits(1, [0, 1])
        wtr2 = ps_waits(2, [0, 1])
        last0 = last1 = last2 = None
        for oc in range(C):
            for nh in range(2):
                for ti, (wt, ct) in enumerate(((whr, Chi), (wlr, Chi), (whi, Shi), (wli, Shi))):
                    last0 = pe(lambda e, bi=bi, oc=oc, nh=nh, wt=wt, ct=ct,
                               st_c=(oc == 0 and ti == 0),
                               sp_c=(oc == C - 1 and ti == 3): e.matmul(
                        ps[0][:, nh * NS:(nh + 1) * NS],
                        wt[bi][oc * 128:(oc + 1) * 128, 0:128] if False else
                        wt[bi][:, oc * 128:(oc + 1) * 128],
                        ct[oc][:, nh * NS:(nh + 1) * NS],
                        start=st_c, stop=sp_c),
                        waits=(tuple(wtr0) if (oc == 0 and nh == 0 and ti == 0) else ())
                        + (csC_tok[oc] if ct is Chi else csS_tok[oc],))
                for ti, (wt, ct) in enumerate(((whr, Shi), (wlr, Shi))):
                    last1 = pe(lambda e, bi=bi, oc=oc, nh=nh, wt=wt, ct=ct,
                               st_c=(oc == 0 and ti == 0),
                               sp_c=(oc == C - 1 and ti == 1): e.matmul(
                        ps[1][:, nh * NS:(nh + 1) * NS],
                        wt[bi][:, oc * 128:(oc + 1) * 128],
                        ct[oc][:, nh * NS:(nh + 1) * NS],
                        start=st_c, stop=sp_c),
                        waits=(tuple(wtr1 + wsplit) if (oc == 0 and nh == 0 and ti == 0) else ())
                        + (csS_tok[oc],))
                for ti, (wt, ct) in enumerate(((whi, Chi), (wli, Chi))):
                    last2 = pe(lambda e, bi=bi, oc=oc, nh=nh, wt=wt, ct=ct,
                               st_c=(oc == 0 and ti == 0),
                               sp_c=(oc == C - 1 and ti == 1): e.matmul(
                        ps[2][:, nh * NS:(nh + 1) * NS],
                        wt[bi][:, oc * 128:(oc + 1) * 128],
                        ct[oc][:, nh * NS:(nh + 1) * NS],
                        start=st_c, stop=sp_c),
                        waits=(tuple(wtr2 + wsplit) if (oc == 0 and nh == 0 and ti == 0) else ())
                        + (csC_tok[oc],))
        U.use(("wsp", bi), last0)
        U.use(("wsp", bi), last1)
        U.use(("wsp", bi), last2)
        e1 = ve(lambda e, dc=dc: e.tensor_copy(R1h[dc][:], ps[0][:, 0:D]), waits=(last0,))
        e2 = ve(lambda e, dc=dc: e.tensor_tensor(R1l[dc][:], ps[0][:, 0:D], R1h[dc][:], ALU.subtract),
                waits=(e1,))
        ps_use(0, [0, 1], e2)
        wsc = U.waits(("sc32", 1))
        e3 = ve(lambda e: e.tensor_copy(sc32[1][:], ps[2][:, 0:D]), waits=(last2,) + wsc)
        ps_use(2, [0, 1], e3)
        e4 = ve(lambda e: e.tensor_tensor(sc32[0][:], ps[1][:, 0:D], sc32[1][:], ALU.subtract),
                waits=(last1, e3) + U.waits(("sc32", 0)))
        ps_use(1, [0, 1], e4)
        e5 = ve(lambda e, dc=dc: e.tensor_copy(R2h[dc][:], sc32[0][:]), waits=(e4,))
        e6 = ve(lambda e, dc=dc: e.tensor_tensor(R2l[dc][:], sc32[0][:], R2h[dc][:], ALU.subtract),
                waits=(e5,))
        U.use(("sc32", 0), e6)
        U.use(("sc32", 1), e4)
        r1_tok[dc] = e2
        r2_tok[dc] = e6

    s1_done = [("tP", S.cnt["tP"]), ("tV", S.cnt["tV"])]
    md_tok = []
    for tc in range(C):
        pt = tc % 2
        wtr = ps_waits(pt, [0, 1])
        seq = []
        for dc2 in range(C):
            seq += [(Chi[dc2], R1h[dc2], csC_tok[dc2], r1_tok[dc2]),
                    (Chi[dc2], R1l[dc2], csC_tok[dc2], r1_tok[dc2]),
                    (Clo[dc2], R1h[dc2], csC_tok[dc2], r1_tok[dc2]),
                    (Shi[dc2], R2h[dc2], csS_tok[dc2], r2_tok[dc2]),
                    (Shi[dc2], R2l[dc2], csS_tok[dc2], r2_tok[dc2]),
                    (Slo[dc2], R2h[dc2], csS_tok[dc2], r2_tok[dc2])]
        last = None
        for nh in range(2):
            for i, (lh, rh, t1, t2) in enumerate(seq):
                last = pe(lambda e, lh=lh, rh=rh, tc=tc, pt=pt, nh=nh,
                          st_c=(i == 0), sp_c=(i == len(seq) - 1): e.matmul(
                    ps[pt][:, nh * NS:(nh + 1) * NS],
                    lh[:, tc * 128:(tc + 1) * 128],
                    rh[:, nh * NS:(nh + 1) * NS],
                    start=st_c, stop=sp_c),
                    waits=tuple(wtr + [t1, t2]) if (nh == 0 and i == 0) else (t1, t2))
        bi = 0
        wmv = U.waits(("mhv", bi))
        m1 = ve(lambda e, bi=bi, pt=pt: e.tensor_scalar(
            mhv[bi][:], ps[pt][:, 0:D], 1.0 / D, None, ALU.mult), waits=(last,) + wmv)
        m2 = ve(lambda e, bi=bi, pt=pt: e.scalar_tensor_tensor(
            mlv[bi][:], ps[pt][:, 0:D], 1.0 / D, mhv[bi][:], ALU.mult, ALU.subtract),
            waits=(m1,))
        ps_use(pt, [0, 1], m2)
        dmh = dma("sync", md[tc * 128:(tc + 1) * 128, :], mhv[bi][:], waits=(m1,))
        dml = dma("sync", md[D + tc * 128:D + (tc + 1) * 128, :], mlv[bi][:], waits=(m2,))
        U.use(("mhv", bi), dmh)
        U.use(("mhv", bi), dml)
        md_tok.append(dmh)
        md_tok.append(dml)
    p7_tp = ("tP", S.cnt["tP"])
    d_pb = dma("gpsimd", pbb[:], pbd[:, :], sem="gpsimd",
               waits=(("tV", S.cnt["tV"]),))
    p7_done = list(md_tok)
    for c in range(C):
        U.use(("wvW", 0, c), p7_tp)
        for _t in p7_done:
            U.use(("wvW", 0, c), _t)
    wv_tok[0] = wave_w_dma(0)

    def ln_tile(src_fn, sc, st_, wload, extra_sc_waits=()):
        scw = tuple(U.waits(("h1s", id(sc) % 97))) + tuple(extra_sc_waits)
        stw = U.waits(("stat", id(st_) % 97))
        b0 = ve(lambda e: e.bn_stats(st_[:, 0:6], src_fn(0, 512)), waits=wload + stw)
        b1 = ve(lambda e: e.bn_stats(st_[:, 6:12], src_fn(512, 1024)), waits=wload + stw)
        ag = ve(lambda e: e.bn_aggr(st_[:, 12:14], st_[:, 0:12]), waits=(b0, b1))
        a2 = ae(lambda e: e.activation(st_[:, 15:16], st_[:, 13:14], AF.Sqrt,
                                       bias=epsb[:, 0:1], scale=1.0),
                waits=(ag, t_eps))
        v8 = ve(lambda e: e.reciprocal(st_[:, 16:17], st_[:, 15:16]), waits=(a2,))
        v13 = ve(lambda e: e.scalar_tensor_tensor(st_[:, 19:20], st_[:, 12:13], -1.0, st_[:, 16:17], ALU.mult, ALU.mult),
                 waits=(v8,))
        v14 = ve(lambda e: e.tensor_scalar(sc[:], src_fn(0, D), st_[:, 16:17], st_[:, 19:20], ALU.mult, ALU.add),
                 waits=(v13,) + scw)
        U.use(("stat", id(st_) % 97), v14)
        return v14, v14

    def _finish_early():
        z = ve(lambda e: e.memset(outb[0][:], 0.0), waits=(("tP", S.cnt["tP"]), ("tV", S.cnt["tV"]), ("tA", S.cnt["tA"])))
        for tb in range(NOTILE):
            dma("sync", out[tb * 128:(tb + 1) * 128, :], outb[0][:], waits=(z,))
        if dump:
            names = sorted(dump)
            row = 0
            prevtok = z
            for nm in names:
                ap = dump[nm]
                p_, f_ = ap.shape[0], ap.shape[1]
                cv = ve(lambda e, ap=ap, p_=p_, f_=f_: e.tensor_copy(outb[1][0:p_, 0:f_], ap), waits=(z, prevtok))
                prevtok = dma("sync", out[row:row + p_, 0:f_], outb[1][0:p_, 0:f_], waits=(cv,))
                row += p_
        S.op("sync", lambda e: None,
             waits=tuple((sn, S.cnt[sn]) for sn in POOLS["sync"])
             + tuple((sn, S.cnt[sn]) for sn in POOLS["gpsimd"])
             + tuple((sn, S.cnt[sn]) for sn in POOLS["dB"]))
        block = nc.Block()
        with block:
            S.replay(block)
        stats = {e: len(S.thunks[e]) for e in ENGS}
        return nc, stats, {}

    if stop_after == "p7":
        if dump is not None:
            dump.clear()
            dump["a_mhv"] = mhv[0][:, 0:D]
            dump["b_mlv"] = mlv[0][:, 0:D]
        return _finish_early()

    NGRP = NTILE // 2

    def x_group_dma(gI):
        srcap = xb[:, :].rearrange("(n p) m -> p n m", p=128)[:, 2 * gI:2 * gI + 2, :]
        wts = U.waits(("xst", gI % xstn))
        if gI < xstn:
            wts = wts + tuple(s1_done)
        return dma("sync", xst[gI % xstn][:].rearrange("p (n m) -> p n m", n=2),
                   srcap, waits=wts)

    xg_tok = [None] * NGRP
    for gI in range(xstn):
        xg_tok[gI] = x_group_dma(gI)

    tr_i = 0
    bm_pe = None
    p1_marks = (("tP", S.cnt["tP"]), ("tV", S.cnt["tV"]))
    for t in range(NTILE):
        gI, half = t // 2, t % 2
        xt = (lambda a, b, gI=gI, half=half: xst[gI % xstn][:, half * D + a:half * D + b])
        sc = h1s[0]
        st_ = stat[t % 2]
        v14, a1 = ln_tile(xt, sc, st_, (xg_tok[gI],),
                          extra_sc_waits=p1_marks if t == 0 else ())
        U.use(("xst", gI % xstn), v14)
        U.use(("xst", gI % xstn), a1)
        for nh in range(2):
            wts = [v14, d_e32] + (ps_waits(3, [nh]) if t == 0 else [])
            bm_pe = pe(lambda e, t=t, nh=nh, sc=sc: e.matmul(
                ps[3][0:NB, nh * NS:(nh + 1) * NS],
                e32s[:, NB * t:NB * t + NB],
                sc[:, nh * NS:(nh + 1) * NS],
                start=(t == 0), stop=(t == NTILE - 1)),
                waits=tuple(wts))
        for c in range(C):
            bank = tr_i % 2
            wtr = [v14, d_id] + ps_waits(2, [bank])
            if t == 0 and c == 0:
                wtr += [p7_tp] + p7_done
            p = pe(lambda e, sc=sc, c=c, bank=bank: e.transpose(
                ps[2][:, bank * NS:bank * NS + 128], sc[:, c * 128:(c + 1) * 128], idf[:]),
                waits=tuple(wtr))
            ev = ae(lambda e, c=c, t=t, bank=bank: e.activation(
                h1T[c][:, t * 128:(t + 1) * 128], ps[2][:, bank * NS:bank * NS + 128],
                AF.Copy),
                waits=(p,))
            ps_use(2, [bank], ev)
            U.use(("h1s", id(sc) % 97), p)
            tr_i += 1
        U.use(("h1s", id(sc) % 97), bm_pe)
        if half == 1 and gI + xstn < NGRP:
            xg_tok[gI + xstn] = x_group_dma(gI + xstn)
    h1T_done = [("tV", S.cnt["tV"]), ("tA", S.cnt["tA"])]
    if stop_after == "p1":
        return _finish_early()

    def emit_qkv(w, marks):
        buf = 0
        wtokw = wv_tok[w]
        qk_ev = {}
        slot_i = 0
        for fp in range(HPW // 2):
            hA, hB = 2 * fp, 2 * fp + 1
            for sect, ntn, colbase in (("q", NQSEC, 0), ("k", NTOK // NS, HPW * DH)):
                for nt in range(ntn):
                    pi, hf = (slot_i % 4) // 2, (slot_i % 4) % 2
                    slot_i += 1
                    off = hf * NS
                    wtr = ps_waits(pi, [hf]) + list(wtokw)
                    if w == 0:
                        wtr += h1T_done
                    last = None
                    for c in range(C):
                        if sect == "q":
                            def bq(e, c=c, fp=fp, nt=nt, pi=pi, off=off, buf=buf,
                                   st_c=(c == 0), sp_c=(c == C - 1),
                                   colbase=colbase):
                                return e.matmul(
                                    ps[pi][:, off:off + QSEC],
                                    wvW[0][c][:, colbase + fp * 128:colbase + (fp + 1) * 128],
                                    h1T[c][:, bass.ts(goff_box["v"] + nt, QSEC)],
                                    start=st_c, stop=sp_c)
                        else:
                            def bq(e, c=c, fp=fp, nt=nt, pi=pi, off=off, buf=buf,
                                   st_c=(c == 0), sp_c=(c == C - 1), colbase=colbase):
                                return e.matmul(
                                    ps[pi][:, off:off + NS],
                                    wvW[0][c][:, colbase + fp * 128:colbase + (fp + 1) * 128],
                                    h1T[c][:, nt * NS:(nt + 1) * NS],
                                    start=st_c, stop=sp_c)
                        last = pe(bq, waits=tuple(wtr) if c == 0 else ())
                    dstA = q_ext[hA] if sect == "q" else k_ext[hA]
                    dstB = q_ext[hB] if sect == "q" else k_ext[hB]
                    wdt = QSEC if sect == "q" else NS
                    w0e = marks if w == 0 else ()
                    evA = ve(lambda e, dstA=dstA, nt=nt, pi=pi, off=off, wdt=wdt: e.tensor_copy(
                        dstA[0:64, nt * wdt:(nt + 1) * wdt], ps[pi][0:64, off:off + wdt]),
                        waits=(last,) + w0e)
                    evB = ve(lambda e, dstB=dstB, nt=nt, pi=pi, off=off, wdt=wdt: e.tensor_copy(
                        dstB[64:128, nt * wdt:(nt + 1) * wdt], ps[pi][64:128, off:off + wdt]),
                        waits=(last,) + w0e)
                    ps_use(pi, [hf], evA)
                    ps_use(pi, [hf], evB)
                    qk_ev[(sect, fp, nt)] = (evA, evB)
        v_ev = {}
        qkv_last_mm = None
        for tb in range(NTILE):
            pi, hf = (slot_i % 4) // 2, (slot_i % 4) % 2
            slot_i += 1
            off = hf * NS
            wtr = ps_waits(pi, [hf]) + list(wtokw)
            if w == 0:
                wtr += h1T_done
            last = None
            for c in range(C):
                last = pe(lambda e, c=c, tb=tb, pi=pi, off=off, buf=buf,
                          st_c=(c == 0), sp_c=(c == C - 1): e.matmul(
                    ps[pi][:, off:off + HPW * DH],
                    h1T[c][:, tb * 128:(tb + 1) * 128],
                    wvW[0][c][:, 2 * HPW * DH:3 * HPW * DH],
                    start=st_c, stop=sp_c),
                    waits=tuple(wtr) if c == 0 else ())
            qkv_last_mm = last
            for hl in range(HPW):
                evv = ve(lambda e, hl=hl, tb=tb, pi=pi, off=off: e.tensor_copy(
                    v_ext[hl][:, tb * 65:tb * 65 + 64],
                    ps[pi][:, off + hl * DH:off + (hl + 1) * DH]),
                    waits=(last,) + (marks if w == 0 else ()))
                ps_use(pi, [hf], evv)
                v_ev[(tb, hl)] = evv

        for c in range(C):
            U.use(("wvW", 0, c), qkv_last_mm)
        return qk_ev, v_ev

    e_bm = ve(lambda e: e.tensor_copy(bmsb[:], ps[3][0:NB, :]), waits=(bm_pe,))
    ps_use(3, [0, 1], e_bm)

    def small_transpose(src_ap, dst_ap, rows, tok_extra, bank):
        wtr = list(tok_extra) + ps_waits(2, [bank])
        p = pe(lambda e: e.transpose(
            ps[2][0:rows if False else 128, bank * NS:bank * NS + rows][0:rows, :]
            if False else ps[2][0:rows, bank * NS:bank * NS + rows],
            src_ap, idf[0:rows, 0:rows]), waits=tuple(wtr))
        ev = ve(lambda e: e.tensor_copy(dst_ap, ps[2][0:rows, bank * NS:bank * NS + rows]),
                waits=(p,))
        ps_use(2, [bank], ev)
        return ev

    bmT_ev = [None] * C
    for c in range(C):
        bank = c % 2
        wtr = [e_bm, d_id] + ps_waits(2, [bank])
        p = pe(lambda e, c=c, bank=bank: e.transpose(
            ps[2][:, bank * NS:bank * NS + NB], bmsb[0:NB, c * 128:(c + 1) * 128],
            idf[0:NB, 0:NB]), waits=tuple(wtr))
        ev = ve(lambda e, c=c, bank=bank: e.tensor_copy(
            bmT[:, c * NB:(c + 1) * NB], ps[2][:, bank * NS:bank * NS + NB]),
            waits=(p,))
        ps_use(2, [bank], ev)
        bmT_ev[c] = ev
    if stop_after == "p2a":
        return _finish_early()
    qk_last = None
    for c in range(C):
        wts = U.waits(("wst", c % 2))
        dtok = dma("sync", wst[c % 2][:].rearrange("p (a b) -> p a b", a=2),
                   bass.AP(wqkvT.tensor, (c * 128) * (3 * D), [[3 * D, 128], [D, 2], [1, D]]),
                   waits=wts)
        for part, pi in ((0, 0), (1, 1)):
            for nh in range(2):
                wts2 = [dtok, bmT_ev[c]] + (ps_waits(pi, [nh]) if c == 0 else [])
                qk_last = pe(lambda e, c=c, part=part, nh=nh, pi=pi: e.matmul(
                    ps[pi][0:NB, nh * NS:(nh + 1) * NS],
                    bmT[:, c * NB:(c + 1) * NB],
                    wst[c % 2][:, part * D + nh * NS: part * D + (nh + 1) * NS],
                    start=(c == 0), stop=(c == C - 1)),
                    waits=tuple(wts2))
                U.use(("wst", c % 2), qk_last)
    e_qm = ve(lambda e: e.tensor_copy(qms[:], ps[0][0:NB, :]), waits=(qk_last,))
    e_km = ve(lambda e: e.tensor_copy(kms[:], ps[1][0:NB, :]), waits=(qk_last,))
    ps_use(0, [0, 1], e_qm)
    ps_use(1, [0, 1], e_km)
    if stop_after == "p2b":
        return _finish_early()
    qmT_ev = [None] * (2 * C)
    kmT_ev = [None] * (2 * C)
    for hc in range(2 * C):
        for (srcT, dstT, evs, etok) in ((qms, qmT, qmT_ev, e_qm), (kms, kmT, kmT_ev, e_km)):
            bank = hc % 2
            wtr = [etok] + ps_waits(2, [bank])
            p = pe(lambda e, srcT=srcT, hc=hc, bank=bank: e.transpose(
                ps[2][0:64, bank * NS:bank * NS + NB], srcT[0:NB, hc * 64:(hc + 1) * 64],
                idf[0:NB, 0:NB]), waits=tuple(wtr))
            ev = ve(lambda e, dstT=dstT, hc=hc, bank=bank: e.tensor_copy(
                dstT[:, hc * NB:(hc + 1) * NB], ps[2][0:64, bank * NS:bank * NS + NB]),
                waits=(p,))
            ps_use(2, [bank], ev)
            evs[hc] = ev
    if stop_after == "p2c":
        return _finish_early()
    sim_last = None
    for h in range(H):
        wts = [qmT_ev[h], kmT_ev[h]] + (ps_waits(3, [0]) if h == 0 else [])
        sim_last = pe(lambda e, h=h: e.matmul(
            ps[3][0:NB, h * NB:(h + 1) * NB],
            qmT[:, h * NB:(h + 1) * NB],
            kmT[:, h * NB:(h + 1) * NB],
            start=True, stop=True),
            waits=tuple(wts))
    e_sim = ve(lambda e: e.tensor_copy(sims[:], ps[3][0:NB, 0:H * NB]), waits=(sim_last,))
    ps_use(3, [0, 1], e_sim)
    if stop_after == "p2d":
        return _finish_early()
    qkv0_marks = (("tP", S.cnt["tP"]), ("tV", S.cnt["tV"]), ("tA", S.cnt["tA"]))
    qkv_cache = {0: emit_qkv(0, qkv0_marks)}
    prev = ve(lambda e: e.memset(rankt[:], 0.0),
              waits=tuple(qmT_ev + kmT_ev))
    sims3 = sims[:].rearrange("p (h k) -> p h k", k=NB)
    rank3 = rankt[:].rearrange("p (h k) -> p h k", k=NB)
    gtt3 = gtt[:].rearrange("p (h k) -> p h k", k=NB)
    for s in range(1, NB):
        wdt = NB - s
        g1 = ve(lambda e, s=s, w=wdt: e.tensor_tensor(
            gtt3[:, :, 0:w], sims3[:, :, s:NB], sims3[:, :, 0:w], ALU.is_gt),
            waits=(e_sim, prev))
        prev = ve(lambda e, w=wdt: e.tensor_tensor(
            rank3[:, :, 0:w], rank3[:, :, 0:w], gtt3[:, :, 0:w], ALU.add),
            waits=(g1, prev))
        g2 = ve(lambda e, s=s, w=wdt: e.tensor_tensor(
            gtt3[:, :, w:NB], sims3[:, :, 0:s], sims3[:, :, w:NB], ALU.is_gt),
            waits=(prev,))
        prev = ve(lambda e, w=wdt: e.tensor_tensor(
            rank3[:, :, w:NB], rank3[:, :, w:NB], gtt3[:, :, w:NB], ALU.add),
            waits=(g2, prev))
    if stop_after == "p2e":
        return _finish_early()
    e_mk = ve(lambda e: e.tensor_scalar(mk32[:], rankt[:], TOPK - 0.5, -1.0e9,
                                        ALU.is_gt, ALU.mult), waits=(prev,))
    mT_ev = [None] * H
    for h in range(H):
        bank = h % 2
        wtr = [e_mk] + ps_waits(2, [bank])
        p = pe(lambda e, h=h, bank=bank: e.transpose(
            ps[2][0:NB, bank * NS:bank * NS + NB], mk32[0:NB, h * NB:(h + 1) * NB],
            idf[0:NB, 0:NB]), waits=tuple(wtr))
        ev = ve(lambda e, h=h, bank=bank: e.tensor_copy(
            mTbb[:, h * NB:(h + 1) * NB], ps[2][0:NB, bank * NS:bank * NS + NB]),
            waits=(p,))
        ps_use(2, [bank], ev)
        mT_ev[h] = ev
    if stop_after == "p2":
        if dump is not None:
            dump.clear()
            dump["a_sims"] = sims[0:NB, :]
            dump["b_rank"] = rankt[0:NB, :]
            dump["c_mk32"] = mk32[0:NB, :]
            dump["d_mTbb"] = mTbb[0:NB, :]
        return _finish_early()
    d_ekb = dma("gpsimd", ekbb[:], ekb[:, :], sem="gpsimd",
                waits=(qk_last, ("tV", S.cnt["tV"]), ("tA", S.cnt["tA"])))

    band_tok = []
    w0_marks = (("tP", S.cnt["tP"]), ("tV", S.cnt["tV"]), ("tA", S.cnt["tA"]))
    for hl in range(HPW):
        b0 = 64 if hl % 2 == 0 else 0
        t1 = ve(lambda e, hl=hl, b0=b0: e.memset(k_ext[hl][b0:b0 + 64, :], 0.0),
                waits=(p7_tp,) + tuple(p7_done))
        t2 = ve(lambda e, hl=hl, b0=b0: e.memset(q_ext[hl][b0:b0 + 64, :], 0.0),
                waits=w0_marks)
        band_tok += [t1, t2]
    ofullT_done = []
    exp_scale = 0.125
    for w in range(NW):
        buf = 0
        wtokw = wv_tok[w]
        urow_tok = [None] * HPW
        wrow_tok = [None] * HPW
        ones_tok = [None] * HPW
        if w in qkv_cache:
            qk_ev, v_ev = qkv_cache.pop(w)
        else:
            qk_ev, v_ev = emit_qkv(w, w0_marks)
        if stop_after == f"w{w}q":
            if dump is not None:
                dump.clear()
                dump["a_mT"] = mTbb[0:NB, :]
                dump["b_kx"] = k_ext[0][:, 0:1024]
                dump["c_qx"] = q_ext[0][:, 0:1024]
            return _finish_early()
        for hl in range(HPW):
            h = w * HPW + hl
            odd = h % 2
            ur0 = 64 if not odd else 0
            wts = U.waits(("qext_u", hl)) + (tuple(band_tok) if w == 0 else ())
            urow_tok[hl] = dma("sync", q_ext[hl][ur0:ur0 + NB, :], ucst[:, :],
                               waits=wts)
            colp = ur0
            wrow_ev = None
            wrow_all = []
            for nt in range(NTOK // NS):
                pi = nt % 2
                wtr = ([mT_ev[h], d_ekb] + ps_waits(pi, [0])
                       + list(U.waits(("kext_w", hl)))
                       + (band_tok if w == 0 and hl == 0 and nt == 0 else []))
                p = pe(lambda e, h=h, nt=nt, pi=pi: e.matmul(
                    ps[pi][0:NB, 0:NS],
                    mTbb[:, h * NB:(h + 1) * NB],
                    ekbb[:, nt * NS:(nt + 1) * NS],
                    start=True, stop=True),
                    waits=tuple(wtr))
                if colp == 0:
                    wrow_ev = ve(lambda e, hl=hl, nt=nt, pi=pi: e.tensor_copy(
                        k_ext[hl][0:NB, nt * NS:(nt + 1) * NS],
                        ps[pi][0:NB, 0:NS]),
                        waits=(p,))
                    ps_use(pi, [0], wrow_ev)
                else:
                    wb = nt % 2
                    ww = U.waits(("wxs", wb))
                    cv = ve(lambda e, wb=wb, pi=pi: e.tensor_copy(
                        wxs[wb][:], ps[pi][0:NB, 0:NS]),
                        waits=(p,) + ww + (w0_marks if w == 0 else ()))
                    ps_use(pi, [0], cv)
                    wrow_ev = dma("sync", k_ext[hl][colp:colp + NB, nt * NS:(nt + 1) * NS],
                                  wxs[wb][:], waits=(cv,))
                    U.use(("wxs", wb), wrow_ev)
                wrow_all.append(wrow_ev)
            wrow_tok[hl] = wrow_all
            onw = U.waits(("vext_o", hl))
            if w == 0:
                onw = onw + (p7_tp,) + tuple(p7_done) + w0_marks
            ones_tok[hl] = ve(lambda e, hl=hl: e.memset(
                v_ext[hl][:].rearrange("p (a b) -> p a b", b=65)[:, :, 64:65], 1.0),
                waits=onw)

        if stop_after == f"w{w}u":
            if dump is not None:
                dump.clear()
                dump["a_mT"] = mTbb[0:NB, :]
                dump["b_idf"] = idf[:, :]
                dump["c_pbb"] = pbb[0:1, :]
                dump["d_fbb"] = fbb[0:1, :]
                dump["e_recr"] = recr[0:65, 0:QH]
                dump["f_ones"] = onesb[0:1, 0:128]
                dump["g_stat"] = stat[0][:, 0:12]
            return _finish_early()
        for hl in range(HPW):
            h = w * HPW + hl
            odd = h % 2
            base = 0
            head_ready = list(wrow_tok[hl]) + [urow_tok[hl], ones_tok[hl]]
            if w == 0:
                head_ready += band_tok
            head_ready += [qk_ev[("q", hl // 2, nt)][hl % 2] for nt in range(NQSEC)]
            head_ready += [qk_ev[("k", hl // 2, nt)][hl % 2] for nt in range(NTOK // NS)]
            head_ready += [v_ev[(tb, hl)] for tb in range(NTILE)]
            last_attn = None
            for qh in range(NQH):
                oslot = (h * NQH + qh) % 2
                po = ps[2 + oslot]
                lastv = None
                ex_half = {}
                for step in range(NB + 1):
                    if step < NB:
                        kb = step
                        pi = kb % 2
                        abuf = kb % 2
                        for nt in range(NSS):
                            wsc = ps_waits(pi, [nt])
                            if kb < 2 and nt == 0:
                                wsc += head_ready
                            last = pe(lambda e, hl=hl, kb=kb, qh=qh, nt=nt, pi=pi,
                                      base=base: e.matmul(
                                ps[pi][:, nt * SS:(nt + 1) * SS],
                                k_ext[hl][:, kb * 128:(kb + 1) * 128],
                                q_ext[hl][:, qh * QH + nt * SS: qh * QH + (nt + 1) * SS],
                                start=True, stop=True),
                                waits=tuple(wsc))
                            wex = [last] + list(U.waits(("attnT", abuf, nt)))
                            exn = ae(lambda e, abuf=abuf, pi=pi, nt=nt: e.activation(
                                attnT[abuf][:, nt * SS:(nt + 1) * SS],
                                ps[pi][:, nt * SS:(nt + 1) * SS],
                                AF.Exp, bias=0.0, scale=exp_scale),
                                waits=tuple(wex))
                            ps_use(pi, [nt], exn)
                            ex_half[(kb, nt)] = exn
                    if step >= 1:
                        kb = step - 1
                        abuf = kb % 2
                        for nt in range(NSS):
                            wv_ = [ex_half[(kb, nt)], t_ones]
                            if kb == 0:
                                wv_ += ps_waits(2 + oslot, [nt])
                            lastv = pe(lambda e, hl=hl, kb=kb, nt=nt, po=po: e.matmul(
                                po[0:65, nt * SS:(nt + 1) * SS],
                                v_ext[hl][:, kb * 65:kb * 65 + 65],
                                attnT[kb % 2][:, nt * SS:(nt + 1) * SS],
                                start=(kb == 0), stop=(kb == NB - 1)),
                                waits=tuple(wv_))
                            U.use(("attnT", abuf, nt), lastv)
                chunk = h // 2
                wrr = U.waits(("recr",))
                er = ve(lambda e, po=po: e.tensor_copy(recr[64:65, :], po[64:65, 0:QH]),
                        waits=(lastv,) + wrr)
                def _recip(e):
                    with nc.allow_low_precision(reason="softmax renorm factor; attention-path error is diluted ~50x in the output"):
                        return e.reciprocal(recr[64:65, :], recr[64:65, :])
                rr = ve(_recip, waits=(er,))
                dwr = dma("sync", recd[h * NQH + qh:h * NQH + qh + 1, :], recr[64:65, :],
                          sem="dB", waits=(rr,))
                U.use(("recr",), dwr)
                wrb = U.waits(("recb",))
                dbr = dma("sync", recb[:],
                          bass.AP(recd.tensor, (h * NQH + qh) * QH, [[0, 64], [1, QH]]),
                          sem="dB", waits=(dwr,) + wrb)
                if not odd:
                    evo = ve(lambda e, po=po, chunk=chunk, qh=qh: e.tensor_tensor(
                        ofullT[chunk][0:64, qh * QH:(qh + 1) * QH], po[0:64, 0:QH],
                        recb[:], ALU.mult), waits=(dbr, lastv))
                    fin = evo
                else:
                    wot = U.waits(("otmp",))
                    evo = ve(lambda e, po=po: e.tensor_tensor(
                        otmp[:], po[0:64, 0:QH], recb[:], ALU.mult),
                        waits=(dbr, lastv) + wot)
                    fin = dma("sync", ofullT[chunk][64:128, qh * QH:(qh + 1) * QH],
                              otmp[:], sem="dB", waits=(evo,))
                    U.use(("otmp",), fin)
                U.use(("recb",), evo)
                ps_use(2 + oslot, [0, 1], evo)
                ofullT_done.append(fin)
                last_attn = lastv
            if stop_after == f"w{w}h{hl}":
                if dump is not None:
                    dump.clear()
                    dump["a_mT"] = mTbb[0:NB, :]
                    dump["b_kx"] = k_ext[0][:, 0:1024]
                    dump["c_at"] = attnT[0][:, 0:1024]
                    dump["d_rb"] = recb[0:64, 0:QH]
                return _finish_early()
            U.use(("kext_w", hl), last_attn)
            U.use(("qext_u", hl), last_attn)
            U.use(("vext_o", hl), last_attn)
        if w + 1 < NW:
            wv_tok[w + 1] = wave_w_dma(w + 1)
    waves_marks = (("tP", S.cnt["tP"]), ("tV", S.cnt["tV"]), ("tA", S.cnt["tA"]))

    if stop_after == "waves":
        if dump is not None:
            dump.clear()
            dump["a_kx"] = k_ext[0][:, 0:1024]
            dump["b_qx"] = q_ext[0][:, 0:1024]
            dump["c_vx"] = v_ext[0][:, 0:1024]
            dump["d_ek"] = ekbb[0:NB, 0:1024]
            dump["e_mT"] = mTbb[0:NB, :]
        return _finish_early()
    wpj_tok = [None] * C
    for c in range(C):
        wpj_tok[c] = dma("gpsimd", wpjb[c][:], wprojT[c * 128:(c + 1) * 128, :],
                         sem="gpsimd", waits=U.waits(("wvW", 0, c)))
    of_done = tuple(ofullT_done)
    atn_tok = [None] * NOTILE
    proj_last_pe = None
    for tb in range(NOTILE):
        pt = tb % 4
        wtr = ps_waits(pt, [0, 1])
        if tb == 0:
            wtr += list(of_done)
        last = None
        for nh in range(2):
            for i, c in enumerate(range(C)):
                wmm = tuple(wtr + [wpj_tok[c]]) if (nh == 0 and i == 0) else (wpj_tok[c],)
                last = pe(lambda e, tb=tb, c=c, nh=nh, pt=pt,
                          st_c=(i == 0): e.matmul(
                    ps[pt][:, nh * NS:(nh + 1) * NS],
                    ofullT[c][:, tb * 128:(tb + 1) * 128],
                    wpjb[c][:, nh * NS:(nh + 1) * NS],
                    start=st_c, stop=False),
                    waits=wmm)
            last = pe(lambda e, nh=nh, pt=pt: e.matmul(
                ps[pt][:, nh * NS:(nh + 1) * NS],
                onesb[0:1, 0:128],
                pbb[0:1, nh * NS:(nh + 1) * NS],
                start=False, stop=True),
                waits=(d_pb, t_ones))
        proj_last_pe = last
        atw = U.waits(("atn", tb))
        atn_tok[tb] = ve(lambda e, tb=tb, pt=pt: e.tensor_copy(
            atn[tb][:], ps[pt][:, 0:D]), waits=(last,) + atw)
        ps_use(pt, [0, 1], atn_tok[tb])

    if stop_after == "p5":
        if dump is not None:
            dump.clear()
            dump["a_of0"] = ofullT[0][:, 0:min(NOWN, 1024)]
            dump["b_of7"] = ofullT[C - 1][:, 0:min(NOWN, 1024)]
            dump["c_atn0"] = atn[0][:, 0:D]
            dump["d_atnL"] = atn[NOTILE - 1][:, 0:D]
        return _finish_early()
    p6_marks = (("tP", S.cnt["tP"]), ("tV", S.cnt["tV"]))
    mh_tok = [None] * C
    ml_tok = [None] * C
    for c in range(C):
        mh_tok[c] = dma("sync", MhS[c][:], md[c * 128:(c + 1) * 128, :],
                        waits=waves_marks)
        ml_tok[c] = dma("sync", MlS[c][:], md[D + c * 128:D + (c + 1) * 128, :],
                        waits=waves_marks)
    xo_tok = [None] * NOTILE

    def xo_dma(tb):
        wts = U.waits(("xot", tb % 2))
        if tb < 2:
            wts = wts + tuple(s1_done) + p6_marks
        return dma("sync", xot[tb % 2][:], xo[tb * 128:(tb + 1) * 128, :],
                   waits=wts)

    for tb in range(min(2, NOTILE)):
        xo_tok[tb] = xo_dma(tb)
    tr_i = 0
    n_out_dma = 0
    for tb in range(NOTILE):
        sc = h1s6
        st_ = stat[tb % 2]
        ob = outb[tb % 3]
        wob = U.waits(("outb", tb % 3))
        if tb < 3:
            wob = wob + p6_marks
        x1t = ve(lambda e, tb=tb, ob=ob: e.tensor_tensor(
            ob[:], xot[tb % 2][:], atn[tb][:], ALU.add),
            waits=(xo_tok[tb], atn_tok[tb]) + wob)
        U.use(("xot", tb % 2), x1t)
        if tb + 2 < NOTILE:
            xo_tok[tb + 2] = xo_dma(tb + 2)
        v14, _ = ln_tile(lambda a, b, ob=ob: ob[:, a:b], sc, st_, (x1t,),
                         extra_sc_waits=p6_marks if tb == 0 else ())
        U.use(("outb", tb % 3), v14)
        h2ev_t = [None] * C
        for c in range(C):
            bank = tr_i % 2
            wtr = [v14] + ps_waits(2, [bank])
            if tb == 0 and c == 0:
                wtr.append(proj_last_pe)
            p = pe(lambda e, sc=sc, c=c, bank=bank: e.transpose(
                ps[2][:, bank * NS:bank * NS + 128], sc[:, c * 128:(c + 1) * 128], idf[:]),
                waits=tuple(wtr))
            ev = ae(lambda e, c=c, tb=tb, bank=bank: e.activation(
                h2T[c][:, tb * 128:(tb + 1) * 128], ps[2][:, bank * NS:bank * NS + 128],
                AF.Copy),
                waits=(p,))
            ps_use(2, [bank], ev)
            U.use(("h1s", id(sc) % 97), p)
            h2ev_t[c] = ev
            tr_i += 1
        pt = tb % 2
        wtr = ps_waits(pt, [0, 1])
        last = None
        for nh in range(2):
            seq = [(MhS[c], mh_tok[c], c) for c in range(C)] + \
                  [(MlS[c], ml_tok[c], c) for c in range(C)]
            for i, (mt, mtok, c) in enumerate(seq):
                wmm = [mtok, h2ev_t[c]]
                if nh == 0 and i == 0:
                    wmm = wtr + wmm
                last = pe(lambda e, tb=tb, c=c, nh=nh, pt=pt, mt=mt,
                          st_c=(i == 0): e.matmul(
                    ps[pt][:, nh * NS:(nh + 1) * NS],
                    h2T[c][:, tb * 128:(tb + 1) * 128],
                    mt[:, nh * NS:(nh + 1) * NS],
                    start=st_c, stop=False),
                    waits=tuple(wmm))
            last = pe(lambda e, nh=nh, pt=pt: e.matmul(
                ps[pt][:, nh * NS:(nh + 1) * NS],
                onesb[0:1, 0:128],
                fbb[0:1, nh * NS:(nh + 1) * NS],
                start=False, stop=True),
                waits=(d_fb, t_ones))
        evo = ve(lambda e, ob=ob, pt=pt: e.tensor_tensor(
            ob[:], ob[:], ps[pt][:, 0:D], ALU.add),
            waits=(last, v14))
        ps_use(pt, [0, 1], evo)
        dot = dma("sync", out[tb * 128:(tb + 1) * 128, :], ob[:],
                  waits=(evo,))
        U.use(("outb", tb % 3), dot)
        n_out_dma += 16

    if stop_after == "p6":
        if dump is not None:
            dump.clear()
            dump["a_h2T0"] = h2T[0][:, 0:min(NOWN, 1024)]
            dump["b_h2TL"] = h2T[C - 1][:, 0:min(NOWN, 1024)]
        return _finish_early()
    S.op("sync", lambda e: None,
         waits=tuple((sn, S.cnt[sn]) for sn in POOLS["sync"]))

    block = nc.Block()
    with block:
        S.replay(block)
    stats = {e: len(S.thunks[e]) for e in ENGS}
    probes = {"atn": [t.name for t in atn], "mk32": mk32.name,
              "ofullT": [t.name for t in ofullT], "h2T": [t.name for t in h2T],
              "sims": sims.name}
    return nc, stats, probes


def make_consts(NTOK=4096, NOWN=2048):
    D = 1024
    NB = NTOK // 128
    NTILE = NB
    i = np.arange(D, dtype=np.int64)
    Mm = (i[:, None] * i[None, :]) % D
    ang = 2.0 * np.pi * Mm.astype(np.float64) / D
    Cm = np.cos(ang)
    Sm = -np.sin(ang)
    c1 = Cm.astype(np.float32)
    c2 = (Cm - c1.astype(np.float64)).astype(np.float32)
    s1 = Sm.astype(np.float32)
    s2 = (Sm - s1.astype(np.float64)).astype(np.float32)
    e32v = np.zeros((128, NB * NTILE), dtype=np.float32)
    for t in range(NTILE):
        e32v[:, NB * t + t] = 1.0 / 128.0
    ekb = np.zeros((NB, NTOK), dtype=np.float32)
    for kbi in range(NB):
        ekb[kbi, kbi * 128:(kbi + 1) * 128] = 1.0
    identf = np.eye(128, dtype=np.float32)
    return dict(c1=c1, c2=c2, s1=s1, s2=s2, e32v=e32v, ekb=ekb, identf=identf)


def make_inputs_for_core(core, full, NTOK=4096, NOWN=2048, consts=None):
    g = core % 2
    b = core // 2
    x = np.asarray(full["L0_attn_x"], dtype=np.float32)
    NBl = NTOK // 128
    d = dict(consts)
    d["xb"] = np.ascontiguousarray(x[b])
    d["xo"] = np.ascontiguousarray(x[b, g * NOWN:(g + 1) * NOWN])
    d["wqkvT"] = np.ascontiguousarray(np.asarray(full["L0_attn_qkv_w"], np.float32).T)
    d["wprojT"] = np.ascontiguousarray(np.asarray(full["L0_attn_proj_w"], np.float32).T)
    d["pb"] = np.ascontiguousarray(np.asarray(full["L0_attn_proj_b"], np.float32).reshape(1, -1))
    d["wrT"] = np.ascontiguousarray(np.asarray(full["L0_ffn_w_freq_real"], np.float32).T)
    d["wiT"] = np.ascontiguousarray(np.asarray(full["L0_ffn_w_freq_imag"], np.float32).T)
    d["fb"] = np.ascontiguousarray(np.asarray(full["L0_ffn_bias"], np.float32).reshape(1, -1))
    u = np.zeros((NBl, NOWN), dtype=np.float32)
    for ql in range(NOWN):
        u[(g * NOWN + ql) // 128, ql] = 1.0
    d["ucst"] = u.astype(ml_dtypes.bfloat16)
    d["goff"] = np.array([[g * (NOWN // min(512, NOWN))]], dtype=np.int32)
    return d


from concourse.bass_utils import run_bass_kernel_spmd

_BUILD_CACHE = {}


def _get_program():
    if "nc" not in _BUILD_CACHE:
        nc, stats, probes = build_program(4096, 2048, 4)
        _BUILD_CACHE["nc"] = nc
    return _BUILD_CACHE["nc"]


def kernel(L0_attn_x, L0_attn_qkv_w, L0_attn_proj_w, L0_attn_proj_b,
           L0_ffn_w_freq_real, L0_ffn_w_freq_imag, L0_ffn_bias):
    full = {"L0_attn_x": L0_attn_x, "L0_attn_qkv_w": L0_attn_qkv_w,
            "L0_attn_proj_w": L0_attn_proj_w, "L0_attn_proj_b": L0_attn_proj_b,
            "L0_ffn_w_freq_real": L0_ffn_w_freq_real,
            "L0_ffn_w_freq_imag": L0_ffn_w_freq_imag, "L0_ffn_bias": L0_ffn_bias}
    consts = make_consts(4096, 2048)
    in_maps = [make_inputs_for_core(c, full, 4096, 2048, consts) for c in range(8)]
    nc = _get_program()
    res = run_bass_kernel_spmd(nc, in_maps, core_ids=list(range(8)))
    out = np.empty((4, 4096, 1024), np.float32)
    for c in range(8):
        out[c // 2, (c % 2) * 2048:(c % 2 + 1) * 2048] = res.results[c]["out"]
    return out


# version 12
# speedup vs baseline: 1.0216x; 1.0180x over previous
import numpy as np
import ml_dtypes
import concourse.bass as bass
import concourse.mybir as mybir

F32 = mybir.dt.float32
BF16 = mybir.dt.bfloat16
I32 = mybir.dt.int32
AF = mybir.ActivationFunctionType
ALU = mybir.AluOpType
PE = mybir.EngineType.PE
AXX = mybir.AxisListType.X

ENGS = ("sync", "gpsimd", "vector", "scalar", "tensor")


class Sched:
    def __init__(self, nc):
        self.nc = nc
        self.thunks = {e: [] for e in ENGS}
        self.sems = {}
        self.cnt = {}
        self.last_wait = {}

    def sem(self, name):
        self.sems[name] = self.nc.alloc_semaphore(name)
        self.cnt[name] = 0

    def op(self, eng, build, waits=(), inc=None, amt=1):
        eff = []
        for wt in waits:
            if wt is None:
                continue
            sn, v = wt
            if sn is None or v is None or v <= 0:
                continue
            key = (eng, sn)
            if self.last_wait.get(key, 0) >= v:
                continue
            self.last_wait[key] = v
            eff.append((self.sems[sn], v))
        newc = None
        semh = None
        if inc is not None:
            self.cnt[inc] += amt
            newc = self.cnt[inc]
            semh = self.sems[inc]

        def thunk(e, eff=eff, build=build, semh=semh, amt=amt):
            for (s, v) in eff:
                e.wait_ge(s, v)
            ins2 = build(e)
            if semh is not None:
                ins2.then_inc(semh, amt)
        self.thunks[eng].append(thunk)
        return (inc, newc)

    def replay(self, block):
        for e in ENGS:
            fns = self.thunks[e]

            def run(engine, fns=fns):
                for f in fns:
                    f(engine)
            getattr(block, e)(run)


class Uses:
    def __init__(self):
        self.m = {}

    def use(self, key, tok):
        if tok is None or tok[0] is None:
            return
        d = self.m.setdefault(key, {})
        sn, v = tok
        if d.get(sn, 0) < v:
            d[sn] = v

    def waits(self, key, clear=True):
        d = self.m.get(key, {})
        w = tuple(d.items())
        if clear and key in self.m:
            del self.m[key]
        return w


def build_program(NTOK=4096, NOWN=2048, HPW=4, stop_after=None, dump=None):
    D, H, DH = 1024, 16, 64
    NB = NTOK // 128
    NTILE = NB
    NOTILE = NOWN // 128
    QH = min(1024, NOWN)
    NQH = NOWN // QH
    NW = H // HPW
    C = D // 128
    NS = 512
    SS = min(NS, QH); NSS = QH // SS
    QSEC = min(NS, NOWN); NQSEC = NOWN // QSEC
    TOPK = max(1, NB // 2)
    NBR = 32
    KEXT = 64 + NBR

    nc = bass.Bass("TRN2", target_bir_lowering=False, debug=False,
                   dynamic_dma_scratch_size=16384)

    def din(name, shape, dt=F32):
        return nc.dram_tensor(name, shape, dt, kind="ExternalInput").ap()

    xb = din("xb", [NTOK, D])
    xo = din("xo", [NOWN, D])
    wqkvT = din("wqkvT", [D, 3 * D])
    wprojT = din("wprojT", [D, D])
    pbd = din("pb", [1, D])
    wrTd = din("wrT", [D, D])
    wiTd = din("wiT", [D, D])
    fbd = din("fb", [1, D])
    c1d = din("c1", [D, D]); c2d = din("c2", [D, D])
    s1d = din("s1", [D, D]); s2d = din("s2", [D, D])
    ucst = din("ucst", [NB, NOWN], BF16)
    e32v = din("e32v", [128, NB * NTILE])
    ekb = din("ekb", [NB, NTOK])
    identf = din("identf", [128, 128])
    goffd = nc.dram_tensor("goff", [1, 1], I32, kind="ExternalInput").ap()
    out = nc.dram_tensor("out", [NOWN, D], F32, kind="ExternalOutput").ap()
    recd = nc.dram_tensor("recd", [H * NQH, QH], BF16, kind="Internal").ap()
    md = nc.dram_tensor("md", [2 * D, D], BF16, kind="Internal").ap()

    S = Sched(nc)
    for sname in ("tP", "tV", "tA", "dO"):
        S.sem(sname)
    POOLS = {"sync": ["dS%d" % i for i in range(6)],
             "gpsimd": ["dG%d" % i for i in range(4)],
             "dB": ["dB%d" % i for i in range(2)]}
    for names in POOLS.values():
        for n in names:
            S.sem(n)
    pool_rr = {k: 0 for k in POOLS}
    U = Uses()

    cur = [nc.sbuf_top]

    def sb(name, shape, dt):
        nbytes = (shape[1] * mybir.dt.size(dt) + 31) // 32 * 32
        cur[0] -= nbytes
        return nc.alloc_sbuf_tensor_at(name, shape, dt, offset=cur[0])

    def sb_at(name, shape, dt, offset):
        return nc.alloc_sbuf_tensor_at(name, shape, dt, offset=offset)

    RA = cur[0] - max(C * NTOK * 2, 4 * C * 2048)
    h1T = [sb_at(f"h1T{c}", [128, NTOK], BF16, RA + c * NTOK * 2) for c in range(C)]
    Chi = [sb_at(f"Chi{c}", [128, D], BF16, RA + c * 2048) for c in range(C)]
    Shi = [sb_at(f"Shi{c}", [128, D], BF16, RA + (C + c) * 2048) for c in range(C)]
    Clo = [sb_at(f"Clo{c}", [128, D], BF16, RA + (2 * C + c) * 2048) for c in range(C)]
    Slo = [sb_at(f"Slo{c}", [128, D], BF16, RA + (3 * C + c) * 2048) for c in range(C)]
    cur[0] = RA

    kx_b = NTOK * 2
    qx_b = NOWN * 2
    vx_b = ((NB * 65 * 2 + 31) // 32) * 32
    RC = cur[0] - max(HPW * (kx_b + qx_b + vx_b), 4 * C * 2048, NOTILE * 2048 + 2 * C * 2048)
    k_ext = [sb_at(f"kext{h}", [128, NTOK], BF16, RC + h * kx_b) for h in range(HPW)]
    q_ext = [sb_at(f"qext{h}", [128, NOWN], BF16, RC + HPW * kx_b + h * qx_b)
             for h in range(HPW)]
    v_ext = [sb_at(f"vext{h}", [128, NB * 65], BF16,
                   RC + HPW * (kx_b + qx_b) + h * vx_b) for h in range(HPW)]
    R1h = [sb_at(f"R1h{c}", [128, D], BF16, RC + c * 2048) for c in range(C)]
    R1l = [sb_at(f"R1l{c}", [128, D], BF16, RC + (C + c) * 2048) for c in range(C)]
    R2h = [sb_at(f"R2h{c}", [128, D], BF16, RC + (2 * C + c) * 2048) for c in range(C)]
    R2l = [sb_at(f"R2l{c}", [128, D], BF16, RC + (3 * C + c) * 2048) for c in range(C)]
    atn = [sb_at(f"atn{t}", [128, D], BF16, RC + t * 2048) for t in range(NOTILE)]
    MhS = [sb_at(f"MhS{c}", [128, D], BF16, RC + NOTILE * 2048 + c * 2048) for c in range(C)]
    MlS = [sb_at(f"MlS{c}", [128, D], BF16, RC + NOTILE * 2048 + (C + c) * 2048) for c in range(C)]
    cur[0] = RC

    RD = cur[0] - max(C * NOWN * 2, 32768)
    ofullT = [sb_at(f"ofT{c}", [128, NOWN], BF16, RD + c * NOWN * 2) for c in range(C)]
    h2T = [sb_at(f"h2T{c}", [128, NOWN], BF16, RD + c * NOWN * 2) for c in range(C)]
    wfsr = [sb_at("wfsr%d" % i, [128, D], F32, RD + i * 4096) for i in range(2)]
    wfsi = [sb_at("wfsi%d" % i, [128, D], F32, RD + 8192 + i * 4096) for i in range(2)]
    whr = [sb_at("whr%d" % i, [128, D], BF16, RD + 16384 + i * 2048) for i in range(2)]
    wlr = [sb_at("wlr%d" % i, [128, D], BF16, RD + 20480 + i * 2048) for i in range(2)]
    whi = [sb_at("whi%d" % i, [128, D], BF16, RD + 24576 + i * 2048) for i in range(2)]
    wli = [sb_at("wli%d" % i, [128, D], BF16, RD + 28672 + i * 2048) for i in range(2)]
    xstn = 2
    xst = [sb_at("xst%d" % i, [128, 2 * D], F32, RD + i * 8192) for i in range(2)]
    wst = [sb_at("wst%d" % i, [128, 2 * D], F32, RD + 16384 + i * 8192) for i in range(2)]
    cur[0] = RD

    RB = cur[0] - 16384
    cstg = [sb_at("cstg%d" % i, [128, D], F32, RB + i * 4096) for i in range(2)]
    mhv = [sb_at("mhv0", [128, D], BF16, RB + 8192)]
    mlv = [sb_at("mlv0", [128, D], BF16, RB + 10240)]
    wvcols = 3 * DH * HPW
    wvW = [[sb_at(f"wvW{b}_{c}", [128, wvcols], BF16, RB + (b * C + c) * wvcols * 2)
            for c in range(C)] for b in range(1)]
    wpjb = [sb_at(f"wpjb{c}", [128, D], BF16, RB + c * 2048) for c in range(C)]
    cur[0] = RB

    R5 = cur[0] - 8192
    sc32 = [sb_at("sc32%d" % i, [128, D], F32, R5 + i * 4096) for i in range(2)]
    xot = [sb_at("xot%d" % i, [128, D], F32, R5 + i * 4096) for i in range(2)]
    cur[0] = R5

    R6 = cur[0] - 17504
    e32s = sb_at("e32s", [128, NB * NTILE], F32, R6)
    bmsb = sb_at("bmsb", [NB, D], F32, R6)
    bmT = sb_at("bmT", [128, C * NB], F32, R6 + 4096)
    qms = sb_at("qms", [NB, D], F32, R6 + 4096 + ((C * NB * 4 + 31) // 32) * 32)
    kms = sb_at("kms", [NB, D], F32, R6 + 8192 + ((C * NB * 4 + 31) // 32) * 32)
    _o = R6 + 12288 + ((C * NB * 4 + 31) // 32) * 32
    qmT = sb_at("qmT", [64, 2 * C * NB], F32, _o)
    kmT = sb_at("kmT", [64, 2 * C * NB], F32, _o + ((2 * C * NB * 4 + 31) // 32) * 32)
    _o2 = _o + 2 * (((C * NB * 4 + 31) // 32) * 32)
    sims = sb_at("sims", [NB, H * NB], F32, _o2)
    _oq = R6 + 4096 + ((C * NB * 4 + 31) // 32) * 32
    rankt = sb_at("rankt", [NB, H * NB], F32, _oq)
    gtt = sb_at("gtt", [NB, H * NB], F32, _oq + ((H * NB * 4 + 31) // 32) * 32)
    mk32 = sb_at("mk32", [NB, H * NB], F32, _oq + ((H * NB * 4 + 31) // 32) * 32)
    attnT = [sb_at("attnT%d" % i, [128, QH], BF16, R6 + i * (QH * 2)) for i in range(2)]
    recb = sb_at("recb", [64, QH], BF16, R6 + 2 * QH * 2)
    otmp = sb_at("otmp", [64, QH], BF16, R6 + 3 * QH * 2)
    outb = [sb_at("outb%d" % i, [128, D], F32, R6 + i * 4096) for i in range(3)]
    h1s6 = sb_at("h1s6", [128, D], F32, R6 + 12288)
    cur[0] = R6

    h1s = [sb_at("h1s0", [128, D], F32, RC + HPW * kx_b)]
    ekbb = sb_at("ekbb", [NB, NTOK], BF16, R6 + 8256)
    idf = sb("idf", [128, 128], F32)
    mTbb = sb("mTbb", [NB, H * NB], BF16)
    stat = [sb("stat%d" % i, [128, 20], F32) for i in range(2)]
    onesb = sb("onesb", [1, 128], BF16)
    pbb = sb_at("pbb", [1, D], BF16, R5)
    fbb = sb("fbb", [1, D], BF16)
    recr = sb("recr", [65, QH], BF16)
    wxs = [sb_at("wxs0", [NB, NS], BF16, R5 + 2048),
           sb_at("wxs1", [NB, NS], BF16, R6 + 16448)]
    goffs = sb("goffs", [1, 1], I32)
    epsb = sb("epsb", [128, 1], F32)
    assert cur[0] >= nc.sbuf_base + 160, f"SBUF overflow: cur={cur[0]} base={nc.sbuf_base}"

    ps = [nc.alloc_psum_tensor(f"ps{i}", [128, 1024], F32) for i in range(4)]

    def ps_waits(i, halves):
        w = []
        for hf in halves:
            w += list(U.waits(("psb", 2 * i + hf)))
        return w

    def ps_use(i, halves, tok):
        for hf in halves:
            U.use(("psb", 2 * i + hf), tok)

    def dma(eng, outap, inap, sem=None, waits=()):
        if sem == "dO":
            return S.op(eng, lambda e, o=outap, i=inap: e.dma_start(out=o, in_=i),
                        waits=waits, inc="dO", amt=16)
        pool = sem if sem in POOLS else eng
        names = POOLS[pool]
        sn = names[pool_rr[pool] % len(names)]
        pool_rr[pool] += 1
        w = list(waits) + [(sn, S.cnt[sn])]
        return S.op(eng, lambda e, o=outap, i=inap: e.dma_start(out=o, in_=i),
                    waits=w, inc=sn, amt=16)

    def ve(build, waits=()):
        return S.op("vector", build, waits=waits, inc="tV")

    def ae(build, waits=()):
        return S.op("scalar", build, waits=waits, inc="tA")

    def pe(build, waits=()):
        return S.op("tensor", build, waits=waits, inc="tP")

    d_id = dma("sync", idf[:], identf[:, :])
    d_e32 = dma("sync", e32s[:], e32v[:, :])
    d_go = dma("sync", goffs[:], goffd[:, :])
    d_fb = dma("gpsimd", fbb[:], fbd[:, :], sem="gpsimd")
    t_ones = ve(lambda e: e.memset(onesb[:], 1.0))
    t_eps = ve(lambda e: e.memset(epsb[:], 1e-5))

    goff_box = {}

    def load_goff(e):
        (ld,), (val,) = nc.values_load_multi_w_load_instructions(
            goffs[0:1, 0:1], engines=[PE],
            min_val=0, max_val=NOWN // QSEC, skip_runtime_bounds_check=True)
        goff_box["v"] = val
        return ld
    S.op("tensor", load_goff, waits=(d_go,))

    def wave_w_dma(w, buf=0):
        toks = []
        for c in range(C):
            src = bass.AP(wqkvT.tensor, (c * 128) * (3 * D) + w * (DH * HPW),
                          [[3 * D, 128], [D, 3], [1, DH * HPW]])
            wts = U.waits(("wvW", 0, c))
            toks.append(dma("gpsimd", wvW[0][c][:].rearrange("p (a b) -> p a b", a=3),
                        src, sem="gpsimd", waits=wts))
        return toks

    wv_tok = [None] * NW

    csC_tok = [None] * C
    csS_tok = [None] * C
    cst2a = sb_at("cst2a", [128, D], F32, RB + 8192)
    lane1_last = None
    k = 0
    for c in range(C):
        for (hid, lod, d1, d2, toks) in ((Chi, Clo, c1d, c2d, csC_tok),
                                         (Shi, Slo, s1d, s2d, csS_tok)):
            lane = k % 2
            k += 1
            stgA = cstg[0] if lane == 0 else cst2a
            stgB = cstg[1] if lane == 0 else sc32[1]
            dc1 = dma("sync", stgA[:], d1[c * 128:(c + 1) * 128, :],
                      waits=U.waits(("cstg", lane, 0)))
            dc2 = dma("sync", stgB[:], d2[c * 128:(c + 1) * 128, :],
                      waits=U.waits(("cstg", lane, 1)))
            wsc = U.waits(("sc32", 0))
            t1 = ve(lambda e, stgA=stgA, stgB=stgB: e.tensor_tensor(
                sc32[0][:], stgA[:], stgB[:], ALU.add),
                waits=(dc1, dc2) + wsc)
            t2 = ae(lambda e, hid=hid, c=c: e.activation(hid[c][:], sc32[0][:], AF.Copy),
                    waits=(t1,))
            t3 = ve(lambda e, hid=hid, lod=lod, c=c: e.tensor_tensor(
                lod[c][:], sc32[0][:], hid[c][:], ALU.subtract),
                waits=(t2,))
            U.use(("cstg", lane, 0), t1)
            U.use(("cstg", lane, 1), t1)
            U.use(("sc32", 0), t3)
            if lane == 1:
                lane1_last = t1
            toks[c] = t3
    U.use(("mhv", 0), lane1_last)
    U.use(("sc32", 1), lane1_last)

    r1_tok = [None] * C
    r2_tok = [None] * C
    for dc in range(C):
        bi = dc % 2
        wts = U.waits(("wfs", bi))
        srcr = bass.AP(wrTd.tensor, dc * 128, [[D, 128], [128 * D, C], [1, 128]])
        srci = bass.AP(wiTd.tensor, dc * 128, [[D, 128], [128 * D, C], [1, 128]])
        dwr = dma("sync", wfsr[bi][:].rearrange("p (o j) -> p o j", o=C), srcr, waits=wts)
        dwi = dma("sync", wfsi[bi][:].rearrange("p (o j) -> p o j", o=C), srci, waits=wts)
        wsp = U.waits(("wsp", bi))
        s1a = ve(lambda e, bi=bi: e.tensor_copy(whr[bi][:], wfsr[bi][:]), waits=(dwr,) + wsp)
        s1b = ve(lambda e, bi=bi: e.tensor_tensor(wlr[bi][:], wfsr[bi][:], whr[bi][:], ALU.subtract),
                 waits=(s1a,))
        s1c = ve(lambda e, bi=bi: e.tensor_copy(whi[bi][:], wfsi[bi][:]), waits=(dwi,) + wsp)
        s1d_ = ve(lambda e, bi=bi: e.tensor_tensor(wli[bi][:], wfsi[bi][:], whi[bi][:], ALU.subtract),
                  waits=(s1c,))
        U.use(("wfs", bi), s1b)
        U.use(("wfs", bi), s1d_)
        wsplit = [s1b, s1d_]
        wtr0 = ps_waits(0, [0, 1]) + wsplit
        wtr1 = ps_waits(1, [0, 1])
        wtr2 = ps_waits(2, [0, 1])
        last0 = last1 = last2 = None
        for oc in range(C):
            for nh in range(2):
                for ti, (wt, ct) in enumerate(((whr, Chi), (wlr, Chi), (whi, Shi), (wli, Shi))):
                    last0 = pe(lambda e, bi=bi, oc=oc, nh=nh, wt=wt, ct=ct,
                               st_c=(oc == 0 and ti == 0),
                               sp_c=(oc == C - 1 and ti == 3): e.matmul(
                        ps[0][:, nh * NS:(nh + 1) * NS],
                        wt[bi][oc * 128:(oc + 1) * 128, 0:128] if False else
                        wt[bi][:, oc * 128:(oc + 1) * 128],
                        ct[oc][:, nh * NS:(nh + 1) * NS],
                        start=st_c, stop=sp_c),
                        waits=(tuple(wtr0) if (oc == 0 and nh == 0 and ti == 0) else ())
                        + (csC_tok[oc] if ct is Chi else csS_tok[oc],))
                for ti, (wt, ct) in enumerate(((whr, Shi), (wlr, Shi))):
                    last1 = pe(lambda e, bi=bi, oc=oc, nh=nh, wt=wt, ct=ct,
                               st_c=(oc == 0 and ti == 0),
                               sp_c=(oc == C - 1 and ti == 1): e.matmul(
                        ps[1][:, nh * NS:(nh + 1) * NS],
                        wt[bi][:, oc * 128:(oc + 1) * 128],
                        ct[oc][:, nh * NS:(nh + 1) * NS],
                        start=st_c, stop=sp_c),
                        waits=(tuple(wtr1 + wsplit) if (oc == 0 and nh == 0 and ti == 0) else ())
                        + (csS_tok[oc],))
                for ti, (wt, ct) in enumerate(((whi, Chi), (wli, Chi))):
                    last2 = pe(lambda e, bi=bi, oc=oc, nh=nh, wt=wt, ct=ct,
                               st_c=(oc == 0 and ti == 0),
                               sp_c=(oc == C - 1 and ti == 1): e.matmul(
                        ps[2][:, nh * NS:(nh + 1) * NS],
                        wt[bi][:, oc * 128:(oc + 1) * 128],
                        ct[oc][:, nh * NS:(nh + 1) * NS],
                        start=st_c, stop=sp_c),
                        waits=(tuple(wtr2 + wsplit) if (oc == 0 and nh == 0 and ti == 0) else ())
                        + (csC_tok[oc],))
        U.use(("wsp", bi), last0)
        U.use(("wsp", bi), last1)
        U.use(("wsp", bi), last2)
        e1 = ve(lambda e, dc=dc: e.tensor_copy(R1h[dc][:], ps[0][:, 0:D]), waits=(last0,))
        e2 = ve(lambda e, dc=dc: e.tensor_tensor(R1l[dc][:], ps[0][:, 0:D], R1h[dc][:], ALU.subtract),
                waits=(e1,))
        ps_use(0, [0, 1], e2)
        wsc = U.waits(("sc32", 1))
        e3 = ve(lambda e: e.tensor_copy(sc32[1][:], ps[2][:, 0:D]), waits=(last2,) + wsc)
        ps_use(2, [0, 1], e3)
        e4 = ve(lambda e: e.tensor_tensor(sc32[0][:], ps[1][:, 0:D], sc32[1][:], ALU.subtract),
                waits=(last1, e3) + U.waits(("sc32", 0)))
        ps_use(1, [0, 1], e4)
        e5 = ve(lambda e, dc=dc: e.tensor_copy(R2h[dc][:], sc32[0][:]), waits=(e4,))
        e6 = ve(lambda e, dc=dc: e.tensor_tensor(R2l[dc][:], sc32[0][:], R2h[dc][:], ALU.subtract),
                waits=(e5,))
        U.use(("sc32", 0), e6)
        U.use(("sc32", 1), e4)
        r1_tok[dc] = e2
        r2_tok[dc] = e6

    s1_done = [("tP", S.cnt["tP"]), ("tV", S.cnt["tV"])]
    md_tok = []
    for tc in range(C):
        pt = tc % 2
        wtr = ps_waits(pt, [0, 1])
        seq = []
        for dc2 in range(C):
            seq += [(Chi[dc2], R1h[dc2], csC_tok[dc2], r1_tok[dc2]),
                    (Chi[dc2], R1l[dc2], csC_tok[dc2], r1_tok[dc2]),
                    (Clo[dc2], R1h[dc2], csC_tok[dc2], r1_tok[dc2]),
                    (Shi[dc2], R2h[dc2], csS_tok[dc2], r2_tok[dc2]),
                    (Shi[dc2], R2l[dc2], csS_tok[dc2], r2_tok[dc2]),
                    (Slo[dc2], R2h[dc2], csS_tok[dc2], r2_tok[dc2])]
        last = None
        for nh in range(2):
            for i, (lh, rh, t1, t2) in enumerate(seq):
                last = pe(lambda e, lh=lh, rh=rh, tc=tc, pt=pt, nh=nh,
                          st_c=(i == 0), sp_c=(i == len(seq) - 1): e.matmul(
                    ps[pt][:, nh * NS:(nh + 1) * NS],
                    lh[:, tc * 128:(tc + 1) * 128],
                    rh[:, nh * NS:(nh + 1) * NS],
                    start=st_c, stop=sp_c),
                    waits=tuple(wtr + [t1, t2]) if (nh == 0 and i == 0) else (t1, t2))
        bi = 0
        wmv = U.waits(("mhv", bi))
        m1 = ve(lambda e, bi=bi, pt=pt: e.tensor_scalar(
            mhv[bi][:], ps[pt][:, 0:D], 1.0 / D, None, ALU.mult), waits=(last,) + wmv)
        m2 = ve(lambda e, bi=bi, pt=pt: e.scalar_tensor_tensor(
            mlv[bi][:], ps[pt][:, 0:D], 1.0 / D, mhv[bi][:], ALU.mult, ALU.subtract),
            waits=(m1,))
        ps_use(pt, [0, 1], m2)
        dmh = dma("sync", md[tc * 128:(tc + 1) * 128, :], mhv[bi][:], waits=(m1,))
        dml = dma("sync", md[D + tc * 128:D + (tc + 1) * 128, :], mlv[bi][:], waits=(m2,))
        U.use(("mhv", bi), dmh)
        U.use(("mhv", bi), dml)
        md_tok.append(dmh)
        md_tok.append(dml)
    p7_tp = ("tP", S.cnt["tP"])
    d_pb = dma("gpsimd", pbb[:], pbd[:, :], sem="gpsimd",
               waits=(("tV", S.cnt["tV"]),))
    p7_done = list(md_tok)
    for c in range(C):
        U.use(("wvW", 0, c), p7_tp)
        for _t in p7_done:
            U.use(("wvW", 0, c), _t)
    wv_tok[0] = wave_w_dma(0)

    def ln_tile(src_fn, sc, st_, wload, extra_sc_waits=()):
        scw = tuple(U.waits(("h1s", id(sc) % 97))) + tuple(extra_sc_waits)
        stw = U.waits(("stat", id(st_) % 97))
        b0 = ve(lambda e: e.bn_stats(st_[:, 0:6], src_fn(0, 512)), waits=wload + stw)
        b1 = ve(lambda e: e.bn_stats(st_[:, 6:12], src_fn(512, 1024)), waits=wload + stw)
        ag = ve(lambda e: e.bn_aggr(st_[:, 12:14], st_[:, 0:12]), waits=(b0, b1))
        a2 = ae(lambda e: e.activation(st_[:, 15:16], st_[:, 13:14], AF.Sqrt,
                                       bias=epsb[:, 0:1], scale=1.0),
                waits=(ag, t_eps))
        v8 = ve(lambda e: e.reciprocal(st_[:, 16:17], st_[:, 15:16]), waits=(a2,))
        v13 = ve(lambda e: e.scalar_tensor_tensor(st_[:, 19:20], st_[:, 12:13], -1.0, st_[:, 16:17], ALU.mult, ALU.mult),
                 waits=(v8,))
        v14 = ve(lambda e: e.tensor_scalar(sc[:], src_fn(0, D), st_[:, 16:17], st_[:, 19:20], ALU.mult, ALU.add),
                 waits=(v13,) + scw)
        U.use(("stat", id(st_) % 97), v14)
        return v14, v14

    def _finish_early():
        z = ve(lambda e: e.memset(outb[0][:], 0.0), waits=(("tP", S.cnt["tP"]), ("tV", S.cnt["tV"]), ("tA", S.cnt["tA"])))
        for tb in range(NOTILE):
            dma("sync", out[tb * 128:(tb + 1) * 128, :], outb[0][:], waits=(z,))
        if dump:
            names = sorted(dump)
            row = 0
            prevtok = z
            for nm in names:
                ap = dump[nm]
                p_, f_ = ap.shape[0], ap.shape[1]
                cv = ve(lambda e, ap=ap, p_=p_, f_=f_: e.tensor_copy(outb[1][0:p_, 0:f_], ap), waits=(z, prevtok))
                prevtok = dma("sync", out[row:row + p_, 0:f_], outb[1][0:p_, 0:f_], waits=(cv,))
                row += p_
        S.op("sync", lambda e: None,
             waits=tuple((sn, S.cnt[sn]) for sn in POOLS["sync"])
             + tuple((sn, S.cnt[sn]) for sn in POOLS["gpsimd"])
             + tuple((sn, S.cnt[sn]) for sn in POOLS["dB"]))
        block = nc.Block()
        with block:
            S.replay(block)
        stats = {e: len(S.thunks[e]) for e in ENGS}
        return nc, stats, {}

    if stop_after == "p7":
        if dump is not None:
            dump.clear()
            dump["a_mhv"] = mhv[0][:, 0:D]
            dump["b_mlv"] = mlv[0][:, 0:D]
        return _finish_early()

    NGRP = NTILE // 2

    def x_group_dma(gI):
        srcap = xb[:, :].rearrange("(n p) m -> p n m", p=128)[:, 2 * gI:2 * gI + 2, :]
        wts = U.waits(("xst", gI % xstn))
        if gI < xstn:
            wts = wts + tuple(s1_done)
        return dma("sync", xst[gI % xstn][:].rearrange("p (n m) -> p n m", n=2),
                   srcap, waits=wts)

    xg_tok = [None] * NGRP
    for gI in range(xstn):
        xg_tok[gI] = x_group_dma(gI)

    tr_i = 0
    bm_pe = None
    p1_marks = (("tP", S.cnt["tP"]), ("tV", S.cnt["tV"]))
    for t in range(NTILE):
        gI, half = t // 2, t % 2
        xt = (lambda a, b, gI=gI, half=half: xst[gI % xstn][:, half * D + a:half * D + b])
        sc = h1s[0]
        st_ = stat[t % 2]
        v14, a1 = ln_tile(xt, sc, st_, (xg_tok[gI],),
                          extra_sc_waits=p1_marks if t == 0 else ())
        U.use(("xst", gI % xstn), v14)
        U.use(("xst", gI % xstn), a1)
        for nh in range(2):
            wts = [v14, d_e32] + (ps_waits(3, [nh]) if t == 0 else [])
            bm_pe = pe(lambda e, t=t, nh=nh, sc=sc: e.matmul(
                ps[3][0:NB, nh * NS:(nh + 1) * NS],
                e32s[:, NB * t:NB * t + NB],
                sc[:, nh * NS:(nh + 1) * NS],
                start=(t == 0), stop=(t == NTILE - 1)),
                waits=tuple(wts))
        for c in range(C):
            bank = tr_i % 2
            wtr = [v14, d_id] + ps_waits(2, [bank])
            if t == 0 and c == 0:
                wtr += [p7_tp] + p7_done
            p = pe(lambda e, sc=sc, c=c, bank=bank: e.transpose(
                ps[2][:, bank * NS:bank * NS + 128], sc[:, c * 128:(c + 1) * 128], idf[:]),
                waits=tuple(wtr))
            ev = ae(lambda e, c=c, t=t, bank=bank: e.activation(
                h1T[c][:, t * 128:(t + 1) * 128], ps[2][:, bank * NS:bank * NS + 128],
                AF.Copy),
                waits=(p,))
            ps_use(2, [bank], ev)
            U.use(("h1s", id(sc) % 97), p)
            tr_i += 1
        U.use(("h1s", id(sc) % 97), bm_pe)
        if half == 1 and gI + xstn < NGRP:
            xg_tok[gI + xstn] = x_group_dma(gI + xstn)
    h1T_done = [("tV", S.cnt["tV"]), ("tA", S.cnt["tA"])]
    if stop_after == "p1":
        return _finish_early()

    def emit_qkv(w, marks):
        buf = 0
        wtokw = wv_tok[w]
        qk_ev = {}
        slot_i = 0
        for fp in range(HPW // 2):
            hA, hB = 2 * fp, 2 * fp + 1
            for sect, ntn, colbase in (("q", NQSEC, 0), ("k", NTOK // NS, HPW * DH)):
                for nt in range(ntn):
                    pi, hf = (slot_i % 4) // 2, (slot_i % 4) % 2
                    slot_i += 1
                    off = hf * NS
                    wtr = ps_waits(pi, [hf]) + list(wtokw)
                    if w == 0:
                        wtr += h1T_done
                    last = None
                    for c in range(C):
                        if sect == "q":
                            def bq(e, c=c, fp=fp, nt=nt, pi=pi, off=off, buf=buf,
                                   st_c=(c == 0), sp_c=(c == C - 1),
                                   colbase=colbase):
                                return e.matmul(
                                    ps[pi][:, off:off + QSEC],
                                    wvW[0][c][:, colbase + fp * 128:colbase + (fp + 1) * 128],
                                    h1T[c][:, bass.ts(goff_box["v"] + nt, QSEC)],
                                    start=st_c, stop=sp_c)
                        else:
                            def bq(e, c=c, fp=fp, nt=nt, pi=pi, off=off, buf=buf,
                                   st_c=(c == 0), sp_c=(c == C - 1), colbase=colbase):
                                return e.matmul(
                                    ps[pi][:, off:off + NS],
                                    wvW[0][c][:, colbase + fp * 128:colbase + (fp + 1) * 128],
                                    h1T[c][:, nt * NS:(nt + 1) * NS],
                                    start=st_c, stop=sp_c)
                        last = pe(bq, waits=tuple(wtr) if c == 0 else ())
                    dstA = q_ext[hA] if sect == "q" else k_ext[hA]
                    dstB = q_ext[hB] if sect == "q" else k_ext[hB]
                    wdt = QSEC if sect == "q" else NS
                    w0e = marks if w == 0 else ()
                    evA = ve(lambda e, dstA=dstA, nt=nt, pi=pi, off=off, wdt=wdt: e.tensor_copy(
                        dstA[0:64, nt * wdt:(nt + 1) * wdt], ps[pi][0:64, off:off + wdt]),
                        waits=(last,) + w0e)
                    evB = ve(lambda e, dstB=dstB, nt=nt, pi=pi, off=off, wdt=wdt: e.tensor_copy(
                        dstB[64:128, nt * wdt:(nt + 1) * wdt], ps[pi][64:128, off:off + wdt]),
                        waits=(last,) + w0e)
                    ps_use(pi, [hf], evA)
                    ps_use(pi, [hf], evB)
                    qk_ev[(sect, fp, nt)] = (evA, evB)
        v_ev = {}
        qkv_last_mm = None
        for tb in range(NTILE):
            pi, hf = (slot_i % 4) // 2, (slot_i % 4) % 2
            slot_i += 1
            off = hf * NS
            wtr = ps_waits(pi, [hf]) + list(wtokw)
            if w == 0:
                wtr += h1T_done
            last = None
            for c in range(C):
                last = pe(lambda e, c=c, tb=tb, pi=pi, off=off, buf=buf,
                          st_c=(c == 0), sp_c=(c == C - 1): e.matmul(
                    ps[pi][:, off:off + HPW * DH],
                    h1T[c][:, tb * 128:(tb + 1) * 128],
                    wvW[0][c][:, 2 * HPW * DH:3 * HPW * DH],
                    start=st_c, stop=sp_c),
                    waits=tuple(wtr) if c == 0 else ())
            qkv_last_mm = last
            for hl in range(HPW):
                evv = ve(lambda e, hl=hl, tb=tb, pi=pi, off=off: e.tensor_copy(
                    v_ext[hl][:, tb * 65:tb * 65 + 64],
                    ps[pi][:, off + hl * DH:off + (hl + 1) * DH]),
                    waits=(last,) + (marks if w == 0 else ()))
                ps_use(pi, [hf], evv)
                v_ev[(tb, hl)] = evv

        for c in range(C):
            U.use(("wvW", 0, c), qkv_last_mm)
        return qk_ev, v_ev

    e_bm = ve(lambda e: e.tensor_copy(bmsb[:], ps[3][0:NB, :]), waits=(bm_pe,))
    ps_use(3, [0, 1], e_bm)

    def small_transpose(src_ap, dst_ap, rows, tok_extra, bank):
        wtr = list(tok_extra) + ps_waits(2, [bank])
        p = pe(lambda e: e.transpose(
            ps[2][0:rows if False else 128, bank * NS:bank * NS + rows][0:rows, :]
            if False else ps[2][0:rows, bank * NS:bank * NS + rows],
            src_ap, idf[0:rows, 0:rows]), waits=tuple(wtr))
        ev = ve(lambda e: e.tensor_copy(dst_ap, ps[2][0:rows, bank * NS:bank * NS + rows]),
                waits=(p,))
        ps_use(2, [bank], ev)
        return ev

    bmT_ev = [None] * C
    for c in range(C):
        bank = c % 2
        wtr = [e_bm, d_id] + ps_waits(2, [bank])
        p = pe(lambda e, c=c, bank=bank: e.transpose(
            ps[2][:, bank * NS:bank * NS + NB], bmsb[0:NB, c * 128:(c + 1) * 128],
            idf[0:NB, 0:NB]), waits=tuple(wtr))
        ev = ve(lambda e, c=c, bank=bank: e.tensor_copy(
            bmT[:, c * NB:(c + 1) * NB], ps[2][:, bank * NS:bank * NS + NB]),
            waits=(p,))
        ps_use(2, [bank], ev)
        bmT_ev[c] = ev
    if stop_after == "p2a":
        return _finish_early()
    qk_last = None
    for c in range(C):
        wts = U.waits(("wst", c % 2))
        dtok = dma("sync", wst[c % 2][:].rearrange("p (a b) -> p a b", a=2),
                   bass.AP(wqkvT.tensor, (c * 128) * (3 * D), [[3 * D, 128], [D, 2], [1, D]]),
                   waits=wts)
        for part, pi in ((0, 0), (1, 1)):
            for nh in range(2):
                wts2 = [dtok, bmT_ev[c]] + (ps_waits(pi, [nh]) if c == 0 else [])
                qk_last = pe(lambda e, c=c, part=part, nh=nh, pi=pi: e.matmul(
                    ps[pi][0:NB, nh * NS:(nh + 1) * NS],
                    bmT[:, c * NB:(c + 1) * NB],
                    wst[c % 2][:, part * D + nh * NS: part * D + (nh + 1) * NS],
                    start=(c == 0), stop=(c == C - 1)),
                    waits=tuple(wts2))
                U.use(("wst", c % 2), qk_last)
    e_qm = ve(lambda e: e.tensor_copy(qms[:], ps[0][0:NB, :]), waits=(qk_last,))
    e_km = ve(lambda e: e.tensor_copy(kms[:], ps[1][0:NB, :]), waits=(qk_last,))
    ps_use(0, [0, 1], e_qm)
    ps_use(1, [0, 1], e_km)
    if stop_after == "p2b":
        return _finish_early()
    qmT_ev = [None] * (2 * C)
    kmT_ev = [None] * (2 * C)
    for hc in range(2 * C):
        for (srcT, dstT, evs, etok) in ((qms, qmT, qmT_ev, e_qm), (kms, kmT, kmT_ev, e_km)):
            bank = hc % 2
            wtr = [etok] + ps_waits(2, [bank])
            p = pe(lambda e, srcT=srcT, hc=hc, bank=bank: e.transpose(
                ps[2][0:64, bank * NS:bank * NS + NB], srcT[0:NB, hc * 64:(hc + 1) * 64],
                idf[0:NB, 0:NB]), waits=tuple(wtr))
            ev = ve(lambda e, dstT=dstT, hc=hc, bank=bank: e.tensor_copy(
                dstT[:, hc * NB:(hc + 1) * NB], ps[2][0:64, bank * NS:bank * NS + NB]),
                waits=(p,))
            ps_use(2, [bank], ev)
            evs[hc] = ev
    if stop_after == "p2c":
        return _finish_early()
    sim_last = None
    for h in range(H):
        wts = [qmT_ev[h], kmT_ev[h]] + (ps_waits(3, [0]) if h == 0 else [])
        sim_last = pe(lambda e, h=h: e.matmul(
            ps[3][0:NB, h * NB:(h + 1) * NB],
            qmT[:, h * NB:(h + 1) * NB],
            kmT[:, h * NB:(h + 1) * NB],
            start=True, stop=True),
            waits=tuple(wts))
    e_sim = ve(lambda e: e.tensor_copy(sims[:], ps[3][0:NB, 0:H * NB]), waits=(sim_last,))
    ps_use(3, [0, 1], e_sim)
    if stop_after == "p2d":
        return _finish_early()
    qkv0_marks = (("tP", S.cnt["tP"]), ("tV", S.cnt["tV"]), ("tA", S.cnt["tA"]))
    qkv_cache = {0: emit_qkv(0, qkv0_marks)}
    prev = ve(lambda e: e.memset(rankt[:], 0.0),
              waits=tuple(qmT_ev + kmT_ev))
    sims3 = sims[:].rearrange("p (h k) -> p h k", k=NB)
    rank3 = rankt[:].rearrange("p (h k) -> p h k", k=NB)
    gtt3 = gtt[:].rearrange("p (h k) -> p h k", k=NB)
    for s in range(1, NB):
        wdt = NB - s
        g1 = ve(lambda e, s=s, w=wdt: e.tensor_tensor(
            gtt3[:, :, 0:w], sims3[:, :, s:NB], sims3[:, :, 0:w], ALU.is_gt),
            waits=(e_sim, prev))
        prev = ve(lambda e, w=wdt: e.tensor_tensor(
            rank3[:, :, 0:w], rank3[:, :, 0:w], gtt3[:, :, 0:w], ALU.add),
            waits=(g1, prev))
        g2 = ve(lambda e, s=s, w=wdt: e.tensor_tensor(
            gtt3[:, :, w:NB], sims3[:, :, 0:s], sims3[:, :, w:NB], ALU.is_gt),
            waits=(prev,))
        prev = ve(lambda e, w=wdt: e.tensor_tensor(
            rank3[:, :, w:NB], rank3[:, :, w:NB], gtt3[:, :, w:NB], ALU.add),
            waits=(g2, prev))
    if stop_after == "p2e":
        return _finish_early()
    e_mk = ve(lambda e: e.tensor_scalar(mk32[:], rankt[:], TOPK - 0.5, -1.0e9,
                                        ALU.is_gt, ALU.mult), waits=(prev,))
    mT_ev = [None] * H
    for h in range(H):
        bank = h % 2
        wtr = [e_mk] + ps_waits(2, [bank])
        p = pe(lambda e, h=h, bank=bank: e.transpose(
            ps[2][0:NB, bank * NS:bank * NS + NB], mk32[0:NB, h * NB:(h + 1) * NB],
            idf[0:NB, 0:NB]), waits=tuple(wtr))
        ev = ve(lambda e, h=h, bank=bank: e.tensor_copy(
            mTbb[:, h * NB:(h + 1) * NB], ps[2][0:NB, bank * NS:bank * NS + NB]),
            waits=(p,))
        ps_use(2, [bank], ev)
        mT_ev[h] = ev
    if stop_after == "p2":
        if dump is not None:
            dump.clear()
            dump["a_sims"] = sims[0:NB, :]
            dump["b_rank"] = rankt[0:NB, :]
            dump["c_mk32"] = mk32[0:NB, :]
            dump["d_mTbb"] = mTbb[0:NB, :]
        return _finish_early()
    d_ekb = dma("gpsimd", ekbb[:], ekb[:, :], sem="gpsimd",
                waits=(qk_last, ("tV", S.cnt["tV"]), ("tA", S.cnt["tA"])))

    band_tok = []
    w0_marks = (("tP", S.cnt["tP"]), ("tV", S.cnt["tV"]), ("tA", S.cnt["tA"]))
    for hl in range(HPW):
        b0 = 64 if hl % 2 == 0 else 0
        t1 = ve(lambda e, hl=hl, b0=b0: e.memset(k_ext[hl][b0:b0 + 64, :], 0.0),
                waits=(p7_tp,) + tuple(p7_done))
        t2 = ve(lambda e, hl=hl, b0=b0: e.memset(q_ext[hl][b0:b0 + 64, :], 0.0),
                waits=w0_marks)
        band_tok += [t1, t2]
    ofullT_done = []
    exp_scale = 0.125
    for w in range(NW):
        buf = 0
        wtokw = wv_tok[w]
        urow_tok = [None] * HPW
        wrow_tok = [None] * HPW
        ones_tok = [None] * HPW
        if w in qkv_cache:
            qk_ev, v_ev = qkv_cache.pop(w)
        else:
            qk_ev, v_ev = emit_qkv(w, w0_marks)
        if stop_after == f"w{w}q":
            if dump is not None:
                dump.clear()
                dump["a_mT"] = mTbb[0:NB, :]
                dump["b_kx"] = k_ext[0][:, 0:1024]
                dump["c_qx"] = q_ext[0][:, 0:1024]
            return _finish_early()
        for hl in range(HPW):
            h = w * HPW + hl
            odd = h % 2
            ur0 = 64 if not odd else 0
            wts = U.waits(("qext_u", hl)) + (tuple(band_tok) if w == 0 else ())
            urow_tok[hl] = dma("sync", q_ext[hl][ur0:ur0 + NB, :], ucst[:, :],
                               waits=wts)
            colp = ur0
            wrow_ev = None
            wrow_all = []
            for nt in range(NTOK // NS):
                pi = nt % 2
                wtr = ([mT_ev[h], d_ekb] + ps_waits(pi, [0])
                       + list(U.waits(("kext_w", hl)))
                       + (band_tok if w == 0 and hl == 0 and nt == 0 else []))
                p = pe(lambda e, h=h, nt=nt, pi=pi: e.matmul(
                    ps[pi][0:NB, 0:NS],
                    mTbb[:, h * NB:(h + 1) * NB],
                    ekbb[:, nt * NS:(nt + 1) * NS],
                    start=True, stop=True),
                    waits=tuple(wtr))
                if colp == 0:
                    wrow_ev = ve(lambda e, hl=hl, nt=nt, pi=pi: e.tensor_copy(
                        k_ext[hl][0:NB, nt * NS:(nt + 1) * NS],
                        ps[pi][0:NB, 0:NS]),
                        waits=(p,))
                    ps_use(pi, [0], wrow_ev)
                else:
                    wb = nt % 2
                    ww = U.waits(("wxs", wb))
                    cv = ve(lambda e, wb=wb, pi=pi: e.tensor_copy(
                        wxs[wb][:], ps[pi][0:NB, 0:NS]),
                        waits=(p,) + ww + (w0_marks if w == 0 else ()))
                    ps_use(pi, [0], cv)
                    wrow_ev = dma("sync", k_ext[hl][colp:colp + NB, nt * NS:(nt + 1) * NS],
                                  wxs[wb][:], waits=(cv,))
                    U.use(("wxs", wb), wrow_ev)
                wrow_all.append(wrow_ev)
            wrow_tok[hl] = wrow_all
            onw = U.waits(("vext_o", hl))
            if w == 0:
                onw = onw + (p7_tp,) + tuple(p7_done) + w0_marks
            ones_tok[hl] = ve(lambda e, hl=hl: e.memset(
                v_ext[hl][:].rearrange("p (a b) -> p a b", b=65)[:, :, 64:65], 1.0),
                waits=onw)

        if stop_after == f"w{w}u":
            if dump is not None:
                dump.clear()
                dump["a_mT"] = mTbb[0:NB, :]
                dump["b_idf"] = idf[:, :]
                dump["c_pbb"] = pbb[0:1, :]
                dump["d_fbb"] = fbb[0:1, :]
                dump["e_recr"] = recr[0:65, 0:QH]
                dump["f_ones"] = onesb[0:1, 0:128]
                dump["g_stat"] = stat[0][:, 0:12]
            return _finish_early()
        for hl in range(HPW):
            h = w * HPW + hl
            odd = h % 2
            base = 0
            head_ready = list(wrow_tok[hl]) + [urow_tok[hl], ones_tok[hl]]
            if w == 0:
                head_ready += band_tok
            head_ready += [qk_ev[("q", hl // 2, nt)][hl % 2] for nt in range(NQSEC)]
            head_ready += [qk_ev[("k", hl // 2, nt)][hl % 2] for nt in range(NTOK // NS)]
            head_ready += [v_ev[(tb, hl)] for tb in range(NTILE)]
            last_attn = None
            for qh in range(NQH):
                oslot = (h * NQH + qh) % 2
                po = ps[2 + oslot]
                lastv = None
                ex_half = {}
                for step in range(NB + 1):
                    if step < NB:
                        kb = step
                        pi = kb % 2
                        abuf = kb % 2
                        for nt in range(NSS):
                            wsc = ps_waits(pi, [nt])
                            if kb < 2 and nt == 0:
                                wsc += head_ready
                            last = pe(lambda e, hl=hl, kb=kb, qh=qh, nt=nt, pi=pi,
                                      base=base: e.matmul(
                                ps[pi][:, nt * SS:(nt + 1) * SS],
                                k_ext[hl][:, kb * 128:(kb + 1) * 128],
                                q_ext[hl][:, qh * QH + nt * SS: qh * QH + (nt + 1) * SS],
                                start=True, stop=True),
                                waits=tuple(wsc))
                            wex = [last] + list(U.waits(("attnT", abuf, nt)))
                            exn = ae(lambda e, abuf=abuf, pi=pi, nt=nt: e.activation(
                                attnT[abuf][:, nt * SS:(nt + 1) * SS],
                                ps[pi][:, nt * SS:(nt + 1) * SS],
                                AF.Exp, bias=0.0, scale=exp_scale),
                                waits=tuple(wex))
                            ps_use(pi, [nt], exn)
                            ex_half[(kb, nt)] = exn
                    if step >= 1:
                        kb = step - 1
                        abuf = kb % 2
                        for nt in range(NSS):
                            wv_ = [ex_half[(kb, nt)], t_ones]
                            if kb == 0:
                                wv_ += ps_waits(2 + oslot, [nt])
                            lastv = pe(lambda e, hl=hl, kb=kb, nt=nt, po=po: e.matmul(
                                po[0:65, nt * SS:(nt + 1) * SS],
                                v_ext[hl][:, kb * 65:kb * 65 + 65],
                                attnT[kb % 2][:, nt * SS:(nt + 1) * SS],
                                start=(kb == 0), stop=(kb == NB - 1)),
                                waits=tuple(wv_))
                            U.use(("attnT", abuf, nt), lastv)
                chunk = h // 2
                wrr = U.waits(("recr",))
                er = ve(lambda e, po=po: e.tensor_copy(recr[64:65, :], po[64:65, 0:QH]),
                        waits=(lastv,) + wrr)
                def _recip(e):
                    with nc.allow_low_precision(reason="softmax renorm factor; attention-path error is diluted ~50x in the output"):
                        return e.reciprocal(recr[64:65, :], recr[64:65, :])
                rr = ve(_recip, waits=(er,))
                dwr = dma("sync", recd[h * NQH + qh:h * NQH + qh + 1, :], recr[64:65, :],
                          sem="dB", waits=(rr,))
                U.use(("recr",), dwr)
                wrb = U.waits(("recb",))
                dbr = dma("sync", recb[:],
                          bass.AP(recd.tensor, (h * NQH + qh) * QH, [[0, 64], [1, QH]]),
                          sem="dB", waits=(dwr,) + wrb)
                if not odd:
                    evo = ve(lambda e, po=po, chunk=chunk, qh=qh: e.tensor_tensor(
                        ofullT[chunk][0:64, qh * QH:(qh + 1) * QH], po[0:64, 0:QH],
                        recb[:], ALU.mult), waits=(dbr, lastv))
                    fin = evo
                else:
                    wot = U.waits(("otmp",))
                    evo = ve(lambda e, po=po: e.tensor_tensor(
                        otmp[:], po[0:64, 0:QH], recb[:], ALU.mult),
                        waits=(dbr, lastv) + wot)
                    fin = dma("sync", ofullT[chunk][64:128, qh * QH:(qh + 1) * QH],
                              otmp[:], sem="dB", waits=(evo,))
                    U.use(("otmp",), fin)
                U.use(("recb",), evo)
                ps_use(2 + oslot, [0, 1], evo)
                ofullT_done.append(fin)
                last_attn = lastv
            if stop_after == f"w{w}h{hl}":
                if dump is not None:
                    dump.clear()
                    dump["a_mT"] = mTbb[0:NB, :]
                    dump["b_kx"] = k_ext[0][:, 0:1024]
                    dump["c_at"] = attnT[0][:, 0:1024]
                    dump["d_rb"] = recb[0:64, 0:QH]
                return _finish_early()
            U.use(("kext_w", hl), last_attn)
            U.use(("qext_u", hl), last_attn)
            U.use(("vext_o", hl), last_attn)
        if w + 1 < NW:
            wv_tok[w + 1] = wave_w_dma(w + 1)
    waves_marks = (("tP", S.cnt["tP"]), ("tV", S.cnt["tV"]), ("tA", S.cnt["tA"]))

    if stop_after == "waves":
        if dump is not None:
            dump.clear()
            dump["a_kx"] = k_ext[0][:, 0:1024]
            dump["b_qx"] = q_ext[0][:, 0:1024]
            dump["c_vx"] = v_ext[0][:, 0:1024]
            dump["d_ek"] = ekbb[0:NB, 0:1024]
            dump["e_mT"] = mTbb[0:NB, :]
        return _finish_early()
    wpj_tok = [None] * C
    for c in range(C):
        wpj_tok[c] = dma("gpsimd", wpjb[c][:], wprojT[c * 128:(c + 1) * 128, :],
                         sem="gpsimd", waits=U.waits(("wvW", 0, c)))
    of_done = tuple(ofullT_done)
    atn_tok = [None] * NOTILE
    proj_last_pe = None
    for tb in range(NOTILE):
        pt = tb % 4
        wtr = ps_waits(pt, [0, 1])
        if tb == 0:
            wtr += list(of_done)
        last = None
        for nh in range(2):
            for i, c in enumerate(range(C)):
                wmm = tuple(wtr + [wpj_tok[c]]) if (nh == 0 and i == 0) else (wpj_tok[c],)
                last = pe(lambda e, tb=tb, c=c, nh=nh, pt=pt,
                          st_c=(i == 0): e.matmul(
                    ps[pt][:, nh * NS:(nh + 1) * NS],
                    ofullT[c][:, tb * 128:(tb + 1) * 128],
                    wpjb[c][:, nh * NS:(nh + 1) * NS],
                    start=st_c, stop=False),
                    waits=wmm)
            last = pe(lambda e, nh=nh, pt=pt: e.matmul(
                ps[pt][:, nh * NS:(nh + 1) * NS],
                onesb[0:1, 0:128],
                pbb[0:1, nh * NS:(nh + 1) * NS],
                start=False, stop=True),
                waits=(d_pb, t_ones))
        proj_last_pe = last
        atw = U.waits(("atn", tb))
        atn_tok[tb] = ve(lambda e, tb=tb, pt=pt: e.tensor_copy(
            atn[tb][:], ps[pt][:, 0:D]), waits=(last,) + atw)
        ps_use(pt, [0, 1], atn_tok[tb])

    if stop_after == "p5":
        if dump is not None:
            dump.clear()
            dump["a_of0"] = ofullT[0][:, 0:min(NOWN, 1024)]
            dump["b_of7"] = ofullT[C - 1][:, 0:min(NOWN, 1024)]
            dump["c_atn0"] = atn[0][:, 0:D]
            dump["d_atnL"] = atn[NOTILE - 1][:, 0:D]
        return _finish_early()
    p6_marks = (("tP", S.cnt["tP"]), ("tV", S.cnt["tV"]))
    mh_tok = [None] * C
    ml_tok = [None] * C
    for c in range(C):
        mh_tok[c] = dma("sync", MhS[c][:], md[c * 128:(c + 1) * 128, :],
                        waits=waves_marks)
        ml_tok[c] = dma("sync", MlS[c][:], md[D + c * 128:D + (c + 1) * 128, :],
                        waits=waves_marks)
    xo_tok = [None] * NOTILE

    def xo_dma(tb):
        wts = U.waits(("xot", tb % 2))
        if tb < 2:
            wts = wts + tuple(s1_done) + p6_marks
        return dma("sync", xot[tb % 2][:], xo[tb * 128:(tb + 1) * 128, :],
                   waits=wts)

    for tb in range(min(2, NOTILE)):
        xo_tok[tb] = xo_dma(tb)
    tr_i = 0
    n_out_dma = 0
    for tb in range(NOTILE):
        sc = h1s6
        st_ = stat[tb % 2]
        ob = outb[tb % 3]
        wob = U.waits(("outb", tb % 3))
        if tb < 3:
            wob = wob + p6_marks
        x1t = ve(lambda e, tb=tb, ob=ob: e.tensor_tensor(
            ob[:], xot[tb % 2][:], atn[tb][:], ALU.add),
            waits=(xo_tok[tb], atn_tok[tb]) + wob)
        U.use(("xot", tb % 2), x1t)
        if tb + 2 < NOTILE:
            xo_tok[tb + 2] = xo_dma(tb + 2)
        v14, _ = ln_tile(lambda a, b, ob=ob: ob[:, a:b], sc, st_, (x1t,),
                         extra_sc_waits=p6_marks if tb == 0 else ())
        U.use(("outb", tb % 3), v14)
        h2ev_t = [None] * C
        for c in range(C):
            bank = tr_i % 2
            wtr = [v14] + ps_waits(2, [bank])
            if tb == 0 and c == 0:
                wtr.append(proj_last_pe)
            p = pe(lambda e, sc=sc, c=c, bank=bank: e.transpose(
                ps[2][:, bank * NS:bank * NS + 128], sc[:, c * 128:(c + 1) * 128], idf[:]),
                waits=tuple(wtr))
            ev = ae(lambda e, c=c, tb=tb, bank=bank: e.activation(
                h2T[c][:, tb * 128:(tb + 1) * 128], ps[2][:, bank * NS:bank * NS + 128],
                AF.Copy),
                waits=(p,))
            ps_use(2, [bank], ev)
            U.use(("h1s", id(sc) % 97), p)
            h2ev_t[c] = ev
            tr_i += 1
        pt = tb % 2
        wtr = ps_waits(pt, [0, 1])
        last = None
        for nh in range(2):
            seq = [(MhS[c], mh_tok[c], c) for c in range(C)] + \
                  [(MlS[c], ml_tok[c], c) for c in range(C)]
            for i, (mt, mtok, c) in enumerate(seq):
                wmm = [mtok, h2ev_t[c]]
                if nh == 0 and i == 0:
                    wmm = wtr + wmm
                last = pe(lambda e, tb=tb, c=c, nh=nh, pt=pt, mt=mt,
                          st_c=(i == 0): e.matmul(
                    ps[pt][:, nh * NS:(nh + 1) * NS],
                    h2T[c][:, tb * 128:(tb + 1) * 128],
                    mt[:, nh * NS:(nh + 1) * NS],
                    start=st_c, stop=False),
                    waits=tuple(wmm))
            last = pe(lambda e, nh=nh, pt=pt: e.matmul(
                ps[pt][:, nh * NS:(nh + 1) * NS],
                onesb[0:1, 0:128],
                fbb[0:1, nh * NS:(nh + 1) * NS],
                start=False, stop=True),
                waits=(d_fb, t_ones))
        evo = ve(lambda e, ob=ob, pt=pt: e.tensor_tensor(
            ob[:], ob[:], ps[pt][:, 0:D], ALU.add),
            waits=(last, v14))
        ps_use(pt, [0, 1], evo)
        dot = dma("sync", out[tb * 128:(tb + 1) * 128, :], ob[:],
                  waits=(evo,))
        U.use(("outb", tb % 3), dot)
        n_out_dma += 16

    if stop_after == "p6":
        if dump is not None:
            dump.clear()
            dump["a_h2T0"] = h2T[0][:, 0:min(NOWN, 1024)]
            dump["b_h2TL"] = h2T[C - 1][:, 0:min(NOWN, 1024)]
        return _finish_early()
    S.op("sync", lambda e: None,
         waits=tuple((sn, S.cnt[sn]) for sn in POOLS["sync"]))

    block = nc.Block()
    with block:
        S.replay(block)
    stats = {e: len(S.thunks[e]) for e in ENGS}
    probes = {"atn": [t.name for t in atn], "mk32": mk32.name,
              "ofullT": [t.name for t in ofullT], "h2T": [t.name for t in h2T],
              "sims": sims.name}
    return nc, stats, probes


def make_consts(NTOK=4096, NOWN=2048):
    D = 1024
    NB = NTOK // 128
    NTILE = NB
    i = np.arange(D, dtype=np.int64)
    Mm = (i[:, None] * i[None, :]) % D
    ang = 2.0 * np.pi * Mm.astype(np.float64) / D
    Cm = np.cos(ang)
    Sm = -np.sin(ang)
    c1 = Cm.astype(np.float32)
    c2 = (Cm - c1.astype(np.float64)).astype(np.float32)
    s1 = Sm.astype(np.float32)
    s2 = (Sm - s1.astype(np.float64)).astype(np.float32)
    e32v = np.zeros((128, NB * NTILE), dtype=np.float32)
    for t in range(NTILE):
        e32v[:, NB * t + t] = 1.0 / 128.0
    ekb = np.zeros((NB, NTOK), dtype=np.float32)
    for kbi in range(NB):
        ekb[kbi, kbi * 128:(kbi + 1) * 128] = 1.0
    identf = np.eye(128, dtype=np.float32)
    return dict(c1=c1, c2=c2, s1=s1, s2=s2, e32v=e32v, ekb=ekb, identf=identf)


def make_inputs_for_core(core, full, NTOK=4096, NOWN=2048, consts=None):
    g = core % 2
    b = core // 2
    x = np.asarray(full["L0_attn_x"], dtype=np.float32)
    NBl = NTOK // 128
    d = dict(consts)
    d["xb"] = np.ascontiguousarray(x[b])
    d["xo"] = np.ascontiguousarray(x[b, g * NOWN:(g + 1) * NOWN])
    d["wqkvT"] = np.ascontiguousarray(np.asarray(full["L0_attn_qkv_w"], np.float32).T)
    d["wprojT"] = np.ascontiguousarray(np.asarray(full["L0_attn_proj_w"], np.float32).T)
    d["pb"] = np.ascontiguousarray(np.asarray(full["L0_attn_proj_b"], np.float32).reshape(1, -1))
    d["wrT"] = np.ascontiguousarray(np.asarray(full["L0_ffn_w_freq_real"], np.float32).T)
    d["wiT"] = np.ascontiguousarray(np.asarray(full["L0_ffn_w_freq_imag"], np.float32).T)
    d["fb"] = np.ascontiguousarray(np.asarray(full["L0_ffn_bias"], np.float32).reshape(1, -1))
    u = np.zeros((NBl, NOWN), dtype=np.float32)
    for ql in range(NOWN):
        u[(g * NOWN + ql) // 128, ql] = 1.0
    d["ucst"] = u.astype(ml_dtypes.bfloat16)
    d["goff"] = np.array([[g * (NOWN // min(512, NOWN))]], dtype=np.int32)
    return d


from concourse.bass_utils import run_bass_kernel_spmd

_BUILD_CACHE = {}


def _get_program():
    if "nc" not in _BUILD_CACHE:
        nc, stats, probes = build_program(4096, 2048, 4)
        _BUILD_CACHE["nc"] = nc
    return _BUILD_CACHE["nc"]


def kernel(L0_attn_x, L0_attn_qkv_w, L0_attn_proj_w, L0_attn_proj_b,
           L0_ffn_w_freq_real, L0_ffn_w_freq_imag, L0_ffn_bias):
    full = {"L0_attn_x": L0_attn_x, "L0_attn_qkv_w": L0_attn_qkv_w,
            "L0_attn_proj_w": L0_attn_proj_w, "L0_attn_proj_b": L0_attn_proj_b,
            "L0_ffn_w_freq_real": L0_ffn_w_freq_real,
            "L0_ffn_w_freq_imag": L0_ffn_w_freq_imag, "L0_ffn_bias": L0_ffn_bias}
    consts = make_consts(4096, 2048)
    in_maps = [make_inputs_for_core(c, full, 4096, 2048, consts) for c in range(8)]
    nc = _get_program()
    res = run_bass_kernel_spmd(nc, in_maps, core_ids=list(range(8)))
    out = np.empty((4, 4096, 1024), np.float32)
    for c in range(8):
        out[c // 2, (c % 2) * 2048:(c % 2 + 1) * 2048] = res.results[c]["out"]
    return out


# version 13
# speedup vs baseline: 1.0230x; 1.0014x over previous
import numpy as np
import ml_dtypes
import concourse.bass as bass
import concourse.mybir as mybir

F32 = mybir.dt.float32
BF16 = mybir.dt.bfloat16
I32 = mybir.dt.int32
AF = mybir.ActivationFunctionType
ALU = mybir.AluOpType
PE = mybir.EngineType.PE
AXX = mybir.AxisListType.X

ENGS = ("sync", "gpsimd", "vector", "scalar", "tensor")


class Sched:
    def __init__(self, nc):
        self.nc = nc
        self.thunks = {e: [] for e in ENGS}
        self.sems = {}
        self.cnt = {}
        self.last_wait = {}

    def sem(self, name):
        self.sems[name] = self.nc.alloc_semaphore(name)
        self.cnt[name] = 0

    def op(self, eng, build, waits=(), inc=None, amt=1):
        eff = []
        for wt in waits:
            if wt is None:
                continue
            sn, v = wt
            if sn is None or v is None or v <= 0:
                continue
            key = (eng, sn)
            if self.last_wait.get(key, 0) >= v:
                continue
            self.last_wait[key] = v
            eff.append((self.sems[sn], v))
        newc = None
        semh = None
        if inc is not None:
            self.cnt[inc] += amt
            newc = self.cnt[inc]
            semh = self.sems[inc]

        def thunk(e, eff=eff, build=build, semh=semh, amt=amt):
            for (s, v) in eff:
                e.wait_ge(s, v)
            ins2 = build(e)
            if semh is not None:
                ins2.then_inc(semh, amt)
        self.thunks[eng].append(thunk)
        return (inc, newc)

    def replay(self, block):
        for e in ENGS:
            fns = self.thunks[e]

            def run(engine, fns=fns):
                for f in fns:
                    f(engine)
            getattr(block, e)(run)


class Uses:
    def __init__(self):
        self.m = {}

    def use(self, key, tok):
        if tok is None or tok[0] is None:
            return
        d = self.m.setdefault(key, {})
        sn, v = tok
        if d.get(sn, 0) < v:
            d[sn] = v

    def waits(self, key, clear=True):
        d = self.m.get(key, {})
        w = tuple(d.items())
        if clear and key in self.m:
            del self.m[key]
        return w


def build_program(NTOK=4096, NOWN=2048, HPW=4, stop_after=None, dump=None):
    D, H, DH = 1024, 16, 64
    NB = NTOK // 128
    NTILE = NB
    NOTILE = NOWN // 128
    QH = min(1024, NOWN)
    NQH = NOWN // QH
    NW = H // HPW
    C = D // 128
    NS = 512
    SS = min(NS, QH); NSS = QH // SS
    QSEC = min(NS, NOWN); NQSEC = NOWN // QSEC
    TOPK = max(1, NB // 2)
    NBR = 32
    KEXT = 64 + NBR

    nc = bass.Bass("TRN2", target_bir_lowering=False, debug=False,
                   dynamic_dma_scratch_size=16384)

    def din(name, shape, dt=F32):
        return nc.dram_tensor(name, shape, dt, kind="ExternalInput").ap()

    xb = din("xb", [NTOK, D])
    xo = din("xo", [NOWN, D])
    wqkvT = din("wqkvT", [D, 3 * D])
    wprojT = din("wprojT", [D, D])
    pbd = din("pb", [1, D])
    wrTd = din("wrT", [D, D])
    wiTd = din("wiT", [D, D])
    fbd = din("fb", [1, D])
    c1d = din("c1", [D, D]); c2d = din("c2", [D, D])
    s1d = din("s1", [D, D]); s2d = din("s2", [D, D])
    ucst = din("ucst", [NB, NOWN], BF16)
    e32v = din("e32v", [128, NB * NTILE])
    ekb = din("ekb", [NB, NTOK])
    identf = din("identf", [128, 128])
    goffd = nc.dram_tensor("goff", [1, 1], I32, kind="ExternalInput").ap()
    out = nc.dram_tensor("out", [NOWN, D], F32, kind="ExternalOutput").ap()
    recd = nc.dram_tensor("recd", [H * NQH, QH], BF16, kind="Internal").ap()
    md = nc.dram_tensor("md", [2 * D, D], BF16, kind="Internal").ap()

    S = Sched(nc)
    for sname in ("tP", "tV", "tA", "dO"):
        S.sem(sname)
    POOLS = {"sync": ["dS%d" % i for i in range(6)],
             "gpsimd": ["dG%d" % i for i in range(4)],
             "dB": ["dB%d" % i for i in range(2)]}
    for names in POOLS.values():
        for n in names:
            S.sem(n)
    pool_rr = {k: 0 for k in POOLS}
    U = Uses()

    cur = [nc.sbuf_top]

    def sb(name, shape, dt):
        nbytes = (shape[1] * mybir.dt.size(dt) + 31) // 32 * 32
        cur[0] -= nbytes
        return nc.alloc_sbuf_tensor_at(name, shape, dt, offset=cur[0])

    def sb_at(name, shape, dt, offset):
        return nc.alloc_sbuf_tensor_at(name, shape, dt, offset=offset)

    RA = cur[0] - max(C * NTOK * 2, 4 * C * 2048)
    h1T = [sb_at(f"h1T{c}", [128, NTOK], BF16, RA + c * NTOK * 2) for c in range(C)]
    Chi = [sb_at(f"Chi{c}", [128, D], BF16, RA + c * 2048) for c in range(C)]
    Shi = [sb_at(f"Shi{c}", [128, D], BF16, RA + (C + c) * 2048) for c in range(C)]
    Clo = [sb_at(f"Clo{c}", [128, D], BF16, RA + (2 * C + c) * 2048) for c in range(C)]
    Slo = [sb_at(f"Slo{c}", [128, D], BF16, RA + (3 * C + c) * 2048) for c in range(C)]
    cur[0] = RA

    kx_b = NTOK * 2
    qx_b = NOWN * 2
    vx_b = ((NB * 65 * 2 + 31) // 32) * 32
    RC = cur[0] - max(HPW * (kx_b + qx_b + vx_b), 4 * C * 2048, NOTILE * 2048 + 2 * C * 2048)
    k_ext = [sb_at(f"kext{h}", [128, NTOK], BF16, RC + h * kx_b) for h in range(HPW)]
    q_ext = [sb_at(f"qext{h}", [128, NOWN], BF16, RC + HPW * kx_b + h * qx_b)
             for h in range(HPW)]
    v_ext = [sb_at(f"vext{h}", [128, NB * 65], BF16,
                   RC + HPW * (kx_b + qx_b) + h * vx_b) for h in range(HPW)]
    R1h = [sb_at(f"R1h{c}", [128, D], BF16, RC + c * 2048) for c in range(C)]
    R1l = [sb_at(f"R1l{c}", [128, D], BF16, RC + (C + c) * 2048) for c in range(C)]
    R2h = [sb_at(f"R2h{c}", [128, D], BF16, RC + (2 * C + c) * 2048) for c in range(C)]
    R2l = [sb_at(f"R2l{c}", [128, D], BF16, RC + (3 * C + c) * 2048) for c in range(C)]
    atn = [sb_at(f"atn{t}", [128, D], BF16, RC + t * 2048) for t in range(NOTILE)]
    MhS = [sb_at(f"MhS{c}", [128, D], BF16, RC + NOTILE * 2048 + c * 2048) for c in range(C)]
    MlS = [sb_at(f"MlS{c}", [128, D], BF16, RC + NOTILE * 2048 + (C + c) * 2048) for c in range(C)]
    cur[0] = RC

    RD = cur[0] - max(C * NOWN * 2, 32768)
    ofullT = [sb_at(f"ofT{c}", [128, NOWN], BF16, RD + c * NOWN * 2) for c in range(C)]
    h2T = [sb_at(f"h2T{c}", [128, NOWN], BF16, RD + c * NOWN * 2) for c in range(C)]
    wfsr = [sb_at("wfsr%d" % i, [128, D], F32, RD + i * 4096) for i in range(2)]
    wfsi = [sb_at("wfsi%d" % i, [128, D], F32, RD + 8192 + i * 4096) for i in range(2)]
    whr = [sb_at("whr%d" % i, [128, D], BF16, RD + 16384 + i * 2048) for i in range(2)]
    wlr = [sb_at("wlr%d" % i, [128, D], BF16, RD + 20480 + i * 2048) for i in range(2)]
    whi = [sb_at("whi%d" % i, [128, D], BF16, RD + 24576 + i * 2048) for i in range(2)]
    wli = [sb_at("wli%d" % i, [128, D], BF16, RD + 28672 + i * 2048) for i in range(2)]
    xstn = 2
    xst = [sb_at("xst%d" % i, [128, 2 * D], F32, RD + i * 8192) for i in range(2)]
    wst = [sb_at("wst%d" % i, [128, 2 * D], F32, RD + 16384 + i * 8192) for i in range(2)]
    cur[0] = RD

    RB = cur[0] - 16384
    cstg = [sb_at("cstg%d" % i, [128, D], F32, RB + i * 4096) for i in range(2)]
    mhv = [sb_at("mhv0", [128, D], BF16, RB + 8192)]
    mlv = [sb_at("mlv0", [128, D], BF16, RB + 10240)]
    wvcols = 3 * DH * HPW
    wvW = [[sb_at(f"wvW{b}_{c}", [128, wvcols], BF16, RB + (b * C + c) * wvcols * 2)
            for c in range(C)] for b in range(1)]
    wpjb = [sb_at(f"wpjb{c}", [128, D], BF16, RB + c * 2048) for c in range(C)]
    cur[0] = RB

    R5 = cur[0] - 8192
    sc32 = [sb_at("sc32%d" % i, [128, D], F32, R5 + i * 4096) for i in range(2)]
    xot = [sb_at("xot%d" % i, [128, D], F32, R5 + i * 4096) for i in range(2)]
    cur[0] = R5

    R6 = cur[0] - 17504
    e32s = sb_at("e32s", [128, NB * NTILE], F32, R6)
    bmsb = sb_at("bmsb", [NB, D], F32, R6)
    bmT = sb_at("bmT", [128, C * NB], F32, R6 + 4096)
    qms = sb_at("qms", [NB, D], F32, R6 + 4096 + ((C * NB * 4 + 31) // 32) * 32)
    kms = sb_at("kms", [NB, D], F32, R6 + 8192 + ((C * NB * 4 + 31) // 32) * 32)
    _o = R6 + 12288 + ((C * NB * 4 + 31) // 32) * 32
    qmT = sb_at("qmT", [64, 2 * C * NB], F32, _o)
    kmT = sb_at("kmT", [64, 2 * C * NB], F32, _o + ((2 * C * NB * 4 + 31) // 32) * 32)
    _o2 = _o + 2 * (((C * NB * 4 + 31) // 32) * 32)
    sims = sb_at("sims", [NB, H * NB], F32, _o2)
    _oq = R6 + 4096 + ((C * NB * 4 + 31) // 32) * 32
    rankt = sb_at("rankt", [NB, H * NB], F32, _oq)
    gtt = sb_at("gtt", [NB, H * NB], F32, _oq + ((H * NB * 4 + 31) // 32) * 32)
    mk32 = sb_at("mk32", [NB, H * NB], F32, _oq + ((H * NB * 4 + 31) // 32) * 32)
    attnT = [sb_at("attnT%d" % i, [128, QH], BF16, R6 + i * (QH * 2)) for i in range(2)]
    recb = sb_at("recb", [64, QH], BF16, R6 + 2 * QH * 2)
    otmp = sb_at("otmp", [64, QH], BF16, R6 + 3 * QH * 2)
    outb = [sb_at("outb%d" % i, [128, D], F32, R6 + i * 4096) for i in range(3)]
    h1s6 = sb_at("h1s6", [128, D], F32, R6 + 12288)
    cur[0] = R6

    h1s = [sb_at("h1s0", [128, D], F32, RC + HPW * kx_b)]
    ekbb = sb_at("ekbb", [NB, NTOK], BF16, R6 + 8256)
    idf = sb("idf", [128, 128], F32)
    mTbb = sb("mTbb", [NB, H * NB], BF16)
    stat = [sb("stat%d" % i, [128, 20], F32) for i in range(2)]
    onesb = sb("onesb", [1, 128], BF16)
    pbb = sb_at("pbb", [1, D], BF16, R5)
    fbb = sb("fbb", [1, D], BF16)
    recr = sb("recr", [65, QH], BF16)
    wxs = [sb_at("wxs0", [NB, NS], BF16, R5 + 2048),
           sb_at("wxs1", [NB, NS], BF16, R6 + 16448)]
    goffs = sb("goffs", [1, 1], I32)
    epsb = sb("epsb", [128, 1], F32)
    assert cur[0] >= nc.sbuf_base + 160, f"SBUF overflow: cur={cur[0]} base={nc.sbuf_base}"

    ps = [nc.alloc_psum_tensor(f"ps{i}", [128, 1024], F32) for i in range(4)]

    def ps_waits(i, halves):
        w = []
        for hf in halves:
            w += list(U.waits(("psb", 2 * i + hf)))
        return w

    def ps_use(i, halves, tok):
        for hf in halves:
            U.use(("psb", 2 * i + hf), tok)

    def dma(eng, outap, inap, sem=None, waits=()):
        if sem == "dO":
            return S.op(eng, lambda e, o=outap, i=inap: e.dma_start(out=o, in_=i),
                        waits=waits, inc="dO", amt=16)
        pool = sem if sem in POOLS else eng
        names = POOLS[pool]
        sn = names[pool_rr[pool] % len(names)]
        pool_rr[pool] += 1
        w = list(waits) + [(sn, S.cnt[sn])]
        return S.op(eng, lambda e, o=outap, i=inap: e.dma_start(out=o, in_=i),
                    waits=w, inc=sn, amt=16)

    def ve(build, waits=()):
        return S.op("vector", build, waits=waits, inc="tV")

    def ae(build, waits=()):
        return S.op("scalar", build, waits=waits, inc="tA")

    def pe(build, waits=()):
        return S.op("tensor", build, waits=waits, inc="tP")

    d_id = dma("sync", idf[:], identf[:, :])
    d_e32 = dma("sync", e32s[:], e32v[:, :])
    d_go = dma("sync", goffs[:], goffd[:, :])
    d_fb = dma("gpsimd", fbb[:], fbd[:, :], sem="gpsimd")
    t_ones = ve(lambda e: e.memset(onesb[:], 1.0))
    t_eps = ve(lambda e: e.memset(epsb[:], 1e-5))

    goff_box = {}

    def load_goff(e):
        (ld,), (val,) = nc.values_load_multi_w_load_instructions(
            goffs[0:1, 0:1], engines=[PE],
            min_val=0, max_val=NOWN // QSEC, skip_runtime_bounds_check=True)
        goff_box["v"] = val
        return ld
    S.op("tensor", load_goff, waits=(d_go,))

    def wave_w_dma(w, buf=0):
        toks = []
        for c in range(C):
            src = bass.AP(wqkvT.tensor, (c * 128) * (3 * D) + w * (DH * HPW),
                          [[3 * D, 128], [D, 3], [1, DH * HPW]])
            wts = U.waits(("wvW", 0, c))
            toks.append(dma("gpsimd", wvW[0][c][:].rearrange("p (a b) -> p a b", a=3),
                        src, sem="gpsimd", waits=wts))
        return toks

    wv_tok = [None] * NW

    csC_tok = [None] * C
    csS_tok = [None] * C
    cst2a = sb_at("cst2a", [128, D], F32, RB + 8192)
    lane1_last = None
    k = 0
    for c in range(C):
        for (hid, lod, d1, d2, toks) in ((Chi, Clo, c1d, c2d, csC_tok),
                                         (Shi, Slo, s1d, s2d, csS_tok)):
            lane = k % 2
            k += 1
            stgA = cstg[0] if lane == 0 else cst2a
            stgB = cstg[1] if lane == 0 else sc32[1]
            dc1 = dma("sync", stgA[:], d1[c * 128:(c + 1) * 128, :],
                      waits=U.waits(("cstg", lane, 0)))
            dc2 = dma("sync", stgB[:], d2[c * 128:(c + 1) * 128, :],
                      waits=U.waits(("cstg", lane, 1)))
            wsc = U.waits(("sc32", 0))
            t1 = ve(lambda e, stgA=stgA, stgB=stgB: e.tensor_tensor(
                sc32[0][:], stgA[:], stgB[:], ALU.add),
                waits=(dc1, dc2) + wsc)
            t2 = ae(lambda e, hid=hid, c=c: e.activation(hid[c][:], sc32[0][:], AF.Copy),
                    waits=(t1,))
            t3 = ve(lambda e, hid=hid, lod=lod, c=c: e.tensor_tensor(
                lod[c][:], sc32[0][:], hid[c][:], ALU.subtract),
                waits=(t2,))
            U.use(("cstg", lane, 0), t1)
            U.use(("cstg", lane, 1), t1)
            U.use(("sc32", 0), t3)
            if lane == 1:
                lane1_last = t1
            toks[c] = t3
    U.use(("mhv", 0), lane1_last)
    U.use(("sc32", 1), lane1_last)

    r1_tok = [None] * C
    r2_tok = [None] * C
    for dc in range(C):
        bi = dc % 2
        wts = U.waits(("wfs", bi))
        srcr = bass.AP(wrTd.tensor, dc * 128, [[D, 128], [128 * D, C], [1, 128]])
        srci = bass.AP(wiTd.tensor, dc * 128, [[D, 128], [128 * D, C], [1, 128]])
        dwr = dma("sync", wfsr[bi][:].rearrange("p (o j) -> p o j", o=C), srcr, waits=wts)
        dwi = dma("sync", wfsi[bi][:].rearrange("p (o j) -> p o j", o=C), srci, waits=wts)
        wsp = U.waits(("wsp", bi))
        s1a = ve(lambda e, bi=bi: e.tensor_copy(whr[bi][:], wfsr[bi][:]), waits=(dwr,) + wsp)
        s1b = ve(lambda e, bi=bi: e.tensor_tensor(wlr[bi][:], wfsr[bi][:], whr[bi][:], ALU.subtract),
                 waits=(s1a,))
        s1c = ve(lambda e, bi=bi: e.tensor_copy(whi[bi][:], wfsi[bi][:]), waits=(dwi,) + wsp)
        s1d_ = ve(lambda e, bi=bi: e.tensor_tensor(wli[bi][:], wfsi[bi][:], whi[bi][:], ALU.subtract),
                  waits=(s1c,))
        U.use(("wfs", bi), s1b)
        U.use(("wfs", bi), s1d_)
        wsplit = [s1b, s1d_]
        wtr0 = ps_waits(0, [0, 1]) + wsplit
        wtr1 = ps_waits(1, [0, 1])
        wtr2 = ps_waits(2, [0, 1])
        last0 = last1 = last2 = None
        for oc in range(C):
            for nh in range(2):
                for ti, (wt, ct) in enumerate(((whr, Chi), (wlr, Chi), (whi, Shi), (wli, Shi))):
                    last0 = pe(lambda e, bi=bi, oc=oc, nh=nh, wt=wt, ct=ct,
                               st_c=(oc == 0 and ti == 0),
                               sp_c=(oc == C - 1 and ti == 3): e.matmul(
                        ps[0][:, nh * NS:(nh + 1) * NS],
                        wt[bi][oc * 128:(oc + 1) * 128, 0:128] if False else
                        wt[bi][:, oc * 128:(oc + 1) * 128],
                        ct[oc][:, nh * NS:(nh + 1) * NS],
                        start=st_c, stop=sp_c),
                        waits=(tuple(wtr0) if (oc == 0 and nh == 0 and ti == 0) else ())
                        + (csC_tok[oc] if ct is Chi else csS_tok[oc],))
                for ti, (wt, ct) in enumerate(((whr, Shi), (wlr, Shi))):
                    last1 = pe(lambda e, bi=bi, oc=oc, nh=nh, wt=wt, ct=ct,
                               st_c=(oc == 0 and ti == 0),
                               sp_c=(oc == C - 1 and ti == 1): e.matmul(
                        ps[1][:, nh * NS:(nh + 1) * NS],
                        wt[bi][:, oc * 128:(oc + 1) * 128],
                        ct[oc][:, nh * NS:(nh + 1) * NS],
                        start=st_c, stop=sp_c),
                        waits=(tuple(wtr1 + wsplit) if (oc == 0 and nh == 0 and ti == 0) else ())
                        + (csS_tok[oc],))
                for ti, (wt, ct) in enumerate(((whi, Chi), (wli, Chi))):
                    last2 = pe(lambda e, bi=bi, oc=oc, nh=nh, wt=wt, ct=ct,
                               st_c=(oc == 0 and ti == 0),
                               sp_c=(oc == C - 1 and ti == 1): e.matmul(
                        ps[2][:, nh * NS:(nh + 1) * NS],
                        wt[bi][:, oc * 128:(oc + 1) * 128],
                        ct[oc][:, nh * NS:(nh + 1) * NS],
                        start=st_c, stop=sp_c),
                        waits=(tuple(wtr2 + wsplit) if (oc == 0 and nh == 0 and ti == 0) else ())
                        + (csC_tok[oc],))
        U.use(("wsp", bi), last0)
        U.use(("wsp", bi), last1)
        U.use(("wsp", bi), last2)
        e1 = ae(lambda e, dc=dc: e.activation(R1h[dc][:], ps[0][:, 0:D], AF.Copy),
                waits=(last0,))
        e2 = ve(lambda e, dc=dc: e.tensor_tensor(R1l[dc][:], ps[0][:, 0:D], R1h[dc][:], ALU.subtract),
                waits=(e1,))
        ps_use(0, [0, 1], e2)
        wsc = U.waits(("sc32", 1))
        e3 = ae(lambda e: e.activation(sc32[1][:], ps[2][:, 0:D], AF.Copy),
                waits=(last2,) + wsc)
        ps_use(2, [0, 1], e3)
        e4 = ve(lambda e: e.tensor_tensor(sc32[0][:], ps[1][:, 0:D], sc32[1][:], ALU.subtract),
                waits=(last1, e3) + U.waits(("sc32", 0)))
        ps_use(1, [0, 1], e4)
        e5 = ae(lambda e, dc=dc: e.activation(R2h[dc][:], sc32[0][:], AF.Copy),
                waits=(e4,))
        e6 = ve(lambda e, dc=dc: e.tensor_tensor(R2l[dc][:], sc32[0][:], R2h[dc][:], ALU.subtract),
                waits=(e5,))
        U.use(("sc32", 0), e6)
        U.use(("sc32", 1), e4)
        r1_tok[dc] = e2
        r2_tok[dc] = e6

    s1_done = [("tP", S.cnt["tP"]), ("tV", S.cnt["tV"])]
    md_tok = []
    for tc in range(C):
        pt = tc % 2
        wtr = ps_waits(pt, [0, 1])
        seq = []
        for dc2 in range(C):
            seq += [(Chi[dc2], R1h[dc2], csC_tok[dc2], r1_tok[dc2]),
                    (Chi[dc2], R1l[dc2], csC_tok[dc2], r1_tok[dc2]),
                    (Clo[dc2], R1h[dc2], csC_tok[dc2], r1_tok[dc2]),
                    (Shi[dc2], R2h[dc2], csS_tok[dc2], r2_tok[dc2]),
                    (Shi[dc2], R2l[dc2], csS_tok[dc2], r2_tok[dc2]),
                    (Slo[dc2], R2h[dc2], csS_tok[dc2], r2_tok[dc2])]
        last = None
        for nh in range(2):
            for i, (lh, rh, t1, t2) in enumerate(seq):
                last = pe(lambda e, lh=lh, rh=rh, tc=tc, pt=pt, nh=nh,
                          st_c=(i == 0), sp_c=(i == len(seq) - 1): e.matmul(
                    ps[pt][:, nh * NS:(nh + 1) * NS],
                    lh[:, tc * 128:(tc + 1) * 128],
                    rh[:, nh * NS:(nh + 1) * NS],
                    start=st_c, stop=sp_c),
                    waits=tuple(wtr + [t1, t2]) if (nh == 0 and i == 0) else (t1, t2))
        bi = 0
        wmv = U.waits(("mhv", bi))
        m1 = ve(lambda e, bi=bi, pt=pt: e.tensor_scalar(
            mhv[bi][:], ps[pt][:, 0:D], 1.0 / D, None, ALU.mult), waits=(last,) + wmv)
        m2 = ve(lambda e, bi=bi, pt=pt: e.scalar_tensor_tensor(
            mlv[bi][:], ps[pt][:, 0:D], 1.0 / D, mhv[bi][:], ALU.mult, ALU.subtract),
            waits=(m1,))
        ps_use(pt, [0, 1], m2)
        dmh = dma("sync", md[tc * 128:(tc + 1) * 128, :], mhv[bi][:], waits=(m1,))
        dml = dma("sync", md[D + tc * 128:D + (tc + 1) * 128, :], mlv[bi][:], waits=(m2,))
        U.use(("mhv", bi), dmh)
        U.use(("mhv", bi), dml)
        md_tok.append(dmh)
        md_tok.append(dml)
    p7_tp = ("tP", S.cnt["tP"])
    d_pb = dma("gpsimd", pbb[:], pbd[:, :], sem="gpsimd",
               waits=(("tV", S.cnt["tV"]),))
    p7_done = list(md_tok)
    for c in range(C):
        U.use(("wvW", 0, c), p7_tp)
        for _t in p7_done:
            U.use(("wvW", 0, c), _t)
    wv_tok[0] = wave_w_dma(0)

    def ln_tile(src_fn, sc, st_, wload, extra_sc_waits=()):
        scw = tuple(U.waits(("h1s", id(sc) % 97))) + tuple(extra_sc_waits)
        stw = U.waits(("stat", id(st_) % 97))
        b0 = ve(lambda e: e.bn_stats(st_[:, 0:6], src_fn(0, 512)), waits=wload + stw)
        b1 = ve(lambda e: e.bn_stats(st_[:, 6:12], src_fn(512, 1024)), waits=wload + stw)
        ag = ve(lambda e: e.bn_aggr(st_[:, 12:14], st_[:, 0:12]), waits=(b0, b1))
        a2 = ae(lambda e: e.activation(st_[:, 15:16], st_[:, 13:14], AF.Sqrt,
                                       bias=epsb[:, 0:1], scale=1.0),
                waits=(ag, t_eps))
        v8 = ve(lambda e: e.reciprocal(st_[:, 16:17], st_[:, 15:16]), waits=(a2,))
        v13 = ve(lambda e: e.scalar_tensor_tensor(st_[:, 19:20], st_[:, 12:13], -1.0, st_[:, 16:17], ALU.mult, ALU.mult),
                 waits=(v8,))
        v14 = ve(lambda e: e.tensor_scalar(sc[:], src_fn(0, D), st_[:, 16:17], st_[:, 19:20], ALU.mult, ALU.add),
                 waits=(v13,) + scw)
        U.use(("stat", id(st_) % 97), v14)
        return v14, v14

    def _finish_early():
        z = ve(lambda e: e.memset(outb[0][:], 0.0), waits=(("tP", S.cnt["tP"]), ("tV", S.cnt["tV"]), ("tA", S.cnt["tA"])))
        for tb in range(NOTILE):
            dma("sync", out[tb * 128:(tb + 1) * 128, :], outb[0][:], waits=(z,))
        if dump:
            names = sorted(dump)
            row = 0
            prevtok = z
            for nm in names:
                ap = dump[nm]
                p_, f_ = ap.shape[0], ap.shape[1]
                cv = ve(lambda e, ap=ap, p_=p_, f_=f_: e.tensor_copy(outb[1][0:p_, 0:f_], ap), waits=(z, prevtok))
                prevtok = dma("sync", out[row:row + p_, 0:f_], outb[1][0:p_, 0:f_], waits=(cv,))
                row += p_
        S.op("sync", lambda e: None,
             waits=tuple((sn, S.cnt[sn]) for sn in POOLS["sync"])
             + tuple((sn, S.cnt[sn]) for sn in POOLS["gpsimd"])
             + tuple((sn, S.cnt[sn]) for sn in POOLS["dB"]))
        block = nc.Block()
        with block:
            S.replay(block)
        stats = {e: len(S.thunks[e]) for e in ENGS}
        return nc, stats, {}

    if stop_after == "p7":
        if dump is not None:
            dump.clear()
            dump["a_mhv"] = mhv[0][:, 0:D]
            dump["b_mlv"] = mlv[0][:, 0:D]
        return _finish_early()

    NGRP = NTILE // 2

    def x_group_dma(gI):
        srcap = xb[:, :].rearrange("(n p) m -> p n m", p=128)[:, 2 * gI:2 * gI + 2, :]
        wts = U.waits(("xst", gI % xstn))
        if gI < xstn:
            wts = wts + tuple(s1_done)
        return dma("sync", xst[gI % xstn][:].rearrange("p (n m) -> p n m", n=2),
                   srcap, waits=wts)

    xg_tok = [None] * NGRP
    for gI in range(xstn):
        xg_tok[gI] = x_group_dma(gI)

    tr_i = 0
    bm_pe = None
    p1_marks = (("tP", S.cnt["tP"]), ("tV", S.cnt["tV"]))
    for t in range(NTILE):
        gI, half = t // 2, t % 2
        xt = (lambda a, b, gI=gI, half=half: xst[gI % xstn][:, half * D + a:half * D + b])
        sc = h1s[0]
        st_ = stat[t % 2]
        v14, a1 = ln_tile(xt, sc, st_, (xg_tok[gI],),
                          extra_sc_waits=p1_marks if t == 0 else ())
        U.use(("xst", gI % xstn), v14)
        U.use(("xst", gI % xstn), a1)
        for nh in range(2):
            wts = [v14, d_e32] + (ps_waits(3, [nh]) if t == 0 else [])
            bm_pe = pe(lambda e, t=t, nh=nh, sc=sc: e.matmul(
                ps[3][0:NB, nh * NS:(nh + 1) * NS],
                e32s[:, NB * t:NB * t + NB],
                sc[:, nh * NS:(nh + 1) * NS],
                start=(t == 0), stop=(t == NTILE - 1)),
                waits=tuple(wts))
        for c in range(C):
            bank = tr_i % 2
            wtr = [v14, d_id] + ps_waits(2, [bank])
            if t == 0 and c == 0:
                wtr += [p7_tp] + p7_done
            p = pe(lambda e, sc=sc, c=c, bank=bank: e.transpose(
                ps[2][:, bank * NS:bank * NS + 128], sc[:, c * 128:(c + 1) * 128], idf[:]),
                waits=tuple(wtr))
            ev = ae(lambda e, c=c, t=t, bank=bank: e.activation(
                h1T[c][:, t * 128:(t + 1) * 128], ps[2][:, bank * NS:bank * NS + 128],
                AF.Copy),
                waits=(p,))
            ps_use(2, [bank], ev)
            U.use(("h1s", id(sc) % 97), p)
            tr_i += 1
        U.use(("h1s", id(sc) % 97), bm_pe)
        if half == 1 and gI + xstn < NGRP:
            xg_tok[gI + xstn] = x_group_dma(gI + xstn)
    h1T_done = [("tV", S.cnt["tV"]), ("tA", S.cnt["tA"])]
    if stop_after == "p1":
        return _finish_early()

    def emit_qkv(w, marks):
        buf = 0
        wtokw = wv_tok[w]
        qk_ev = {}
        slot_i = 0
        for fp in range(HPW // 2):
            hA, hB = 2 * fp, 2 * fp + 1
            for sect, ntn, colbase in (("q", NQSEC, 0), ("k", NTOK // NS, HPW * DH)):
                for nt in range(ntn):
                    pi, hf = (slot_i % 4) // 2, (slot_i % 4) % 2
                    slot_i += 1
                    off = hf * NS
                    wtr = ps_waits(pi, [hf]) + list(wtokw)
                    if w == 0:
                        wtr += h1T_done
                    last = None
                    for c in range(C):
                        if sect == "q":
                            def bq(e, c=c, fp=fp, nt=nt, pi=pi, off=off, buf=buf,
                                   st_c=(c == 0), sp_c=(c == C - 1),
                                   colbase=colbase):
                                return e.matmul(
                                    ps[pi][:, off:off + QSEC],
                                    wvW[0][c][:, colbase + fp * 128:colbase + (fp + 1) * 128],
                                    h1T[c][:, bass.ts(goff_box["v"] + nt, QSEC)],
                                    start=st_c, stop=sp_c)
                        else:
                            def bq(e, c=c, fp=fp, nt=nt, pi=pi, off=off, buf=buf,
                                   st_c=(c == 0), sp_c=(c == C - 1), colbase=colbase):
                                return e.matmul(
                                    ps[pi][:, off:off + NS],
                                    wvW[0][c][:, colbase + fp * 128:colbase + (fp + 1) * 128],
                                    h1T[c][:, nt * NS:(nt + 1) * NS],
                                    start=st_c, stop=sp_c)
                        last = pe(bq, waits=tuple(wtr) if c == 0 else ())
                    dstA = q_ext[hA] if sect == "q" else k_ext[hA]
                    dstB = q_ext[hB] if sect == "q" else k_ext[hB]
                    wdt = QSEC if sect == "q" else NS
                    w0e = marks if w == 0 else ()
                    evA = ve(lambda e, dstA=dstA, nt=nt, pi=pi, off=off, wdt=wdt: e.tensor_copy(
                        dstA[0:64, nt * wdt:(nt + 1) * wdt], ps[pi][0:64, off:off + wdt]),
                        waits=(last,) + w0e)
                    evB = ve(lambda e, dstB=dstB, nt=nt, pi=pi, off=off, wdt=wdt: e.tensor_copy(
                        dstB[64:128, nt * wdt:(nt + 1) * wdt], ps[pi][64:128, off:off + wdt]),
                        waits=(last,) + w0e)
                    ps_use(pi, [hf], evA)
                    ps_use(pi, [hf], evB)
                    qk_ev[(sect, fp, nt)] = (evA, evB)
        v_ev = {}
        qkv_last_mm = None
        for tb in range(NTILE):
            pi, hf = (slot_i % 4) // 2, (slot_i % 4) % 2
            slot_i += 1
            off = hf * NS
            wtr = ps_waits(pi, [hf]) + list(wtokw)
            if w == 0:
                wtr += h1T_done
            last = None
            for c in range(C):
                last = pe(lambda e, c=c, tb=tb, pi=pi, off=off, buf=buf,
                          st_c=(c == 0), sp_c=(c == C - 1): e.matmul(
                    ps[pi][:, off:off + HPW * DH],
                    h1T[c][:, tb * 128:(tb + 1) * 128],
                    wvW[0][c][:, 2 * HPW * DH:3 * HPW * DH],
                    start=st_c, stop=sp_c),
                    waits=tuple(wtr) if c == 0 else ())
            qkv_last_mm = last
            for hl in range(HPW):
                evv = ve(lambda e, hl=hl, tb=tb, pi=pi, off=off: e.tensor_copy(
                    v_ext[hl][:, tb * 65:tb * 65 + 64],
                    ps[pi][:, off + hl * DH:off + (hl + 1) * DH]),
                    waits=(last,) + (marks if w == 0 else ()))
                ps_use(pi, [hf], evv)
                v_ev[(tb, hl)] = evv

        for c in range(C):
            U.use(("wvW", 0, c), qkv_last_mm)
        return qk_ev, v_ev

    e_bm = ve(lambda e: e.tensor_copy(bmsb[:], ps[3][0:NB, :]), waits=(bm_pe,))
    ps_use(3, [0, 1], e_bm)

    def small_transpose(src_ap, dst_ap, rows, tok_extra, bank):
        wtr = list(tok_extra) + ps_waits(2, [bank])
        p = pe(lambda e: e.transpose(
            ps[2][0:rows if False else 128, bank * NS:bank * NS + rows][0:rows, :]
            if False else ps[2][0:rows, bank * NS:bank * NS + rows],
            src_ap, idf[0:rows, 0:rows]), waits=tuple(wtr))
        ev = ve(lambda e: e.tensor_copy(dst_ap, ps[2][0:rows, bank * NS:bank * NS + rows]),
                waits=(p,))
        ps_use(2, [bank], ev)
        return ev

    bmT_ev = [None] * C
    for c in range(C):
        bank = c % 2
        wtr = [e_bm, d_id] + ps_waits(2, [bank])
        p = pe(lambda e, c=c, bank=bank: e.transpose(
            ps[2][:, bank * NS:bank * NS + NB], bmsb[0:NB, c * 128:(c + 1) * 128],
            idf[0:NB, 0:NB]), waits=tuple(wtr))
        ev = ve(lambda e, c=c, bank=bank: e.tensor_copy(
            bmT[:, c * NB:(c + 1) * NB], ps[2][:, bank * NS:bank * NS + NB]),
            waits=(p,))
        ps_use(2, [bank], ev)
        bmT_ev[c] = ev
    if stop_after == "p2a":
        return _finish_early()
    qk_last = None
    for c in range(C):
        wts = U.waits(("wst", c % 2))
        dtok = dma("sync", wst[c % 2][:].rearrange("p (a b) -> p a b", a=2),
                   bass.AP(wqkvT.tensor, (c * 128) * (3 * D), [[3 * D, 128], [D, 2], [1, D]]),
                   waits=wts)
        for part, pi in ((0, 0), (1, 1)):
            for nh in range(2):
                wts2 = [dtok, bmT_ev[c]] + (ps_waits(pi, [nh]) if c == 0 else [])
                qk_last = pe(lambda e, c=c, part=part, nh=nh, pi=pi: e.matmul(
                    ps[pi][0:NB, nh * NS:(nh + 1) * NS],
                    bmT[:, c * NB:(c + 1) * NB],
                    wst[c % 2][:, part * D + nh * NS: part * D + (nh + 1) * NS],
                    start=(c == 0), stop=(c == C - 1)),
                    waits=tuple(wts2))
                U.use(("wst", c % 2), qk_last)
    e_qm = ve(lambda e: e.tensor_copy(qms[:], ps[0][0:NB, :]), waits=(qk_last,))
    e_km = ve(lambda e: e.tensor_copy(kms[:], ps[1][0:NB, :]), waits=(qk_last,))
    ps_use(0, [0, 1], e_qm)
    ps_use(1, [0, 1], e_km)
    if stop_after == "p2b":
        return _finish_early()
    qmT_ev = [None] * (2 * C)
    kmT_ev = [None] * (2 * C)
    for hc in range(2 * C):
        for (srcT, dstT, evs, etok) in ((qms, qmT, qmT_ev, e_qm), (kms, kmT, kmT_ev, e_km)):
            bank = hc % 2
            wtr = [etok] + ps_waits(2, [bank])
            p = pe(lambda e, srcT=srcT, hc=hc, bank=bank: e.transpose(
                ps[2][0:64, bank * NS:bank * NS + NB], srcT[0:NB, hc * 64:(hc + 1) * 64],
                idf[0:NB, 0:NB]), waits=tuple(wtr))
            ev = ve(lambda e, dstT=dstT, hc=hc, bank=bank: e.tensor_copy(
                dstT[:, hc * NB:(hc + 1) * NB], ps[2][0:64, bank * NS:bank * NS + NB]),
                waits=(p,))
            ps_use(2, [bank], ev)
            evs[hc] = ev
    if stop_after == "p2c":
        return _finish_early()
    sim_last = None
    for h in range(H):
        wts = [qmT_ev[h], kmT_ev[h]] + (ps_waits(3, [0]) if h == 0 else [])
        sim_last = pe(lambda e, h=h: e.matmul(
            ps[3][0:NB, h * NB:(h + 1) * NB],
            qmT[:, h * NB:(h + 1) * NB],
            kmT[:, h * NB:(h + 1) * NB],
            start=True, stop=True),
            waits=tuple(wts))
    e_sim = ve(lambda e: e.tensor_copy(sims[:], ps[3][0:NB, 0:H * NB]), waits=(sim_last,))
    ps_use(3, [0, 1], e_sim)
    if stop_after == "p2d":
        return _finish_early()
    qkv0_marks = (("tP", S.cnt["tP"]), ("tV", S.cnt["tV"]), ("tA", S.cnt["tA"]))
    qkv_cache = {0: emit_qkv(0, qkv0_marks)}
    prev = ve(lambda e: e.memset(rankt[:], 0.0),
              waits=tuple(qmT_ev + kmT_ev))
    sims3 = sims[:].rearrange("p (h k) -> p h k", k=NB)
    rank3 = rankt[:].rearrange("p (h k) -> p h k", k=NB)
    gtt3 = gtt[:].rearrange("p (h k) -> p h k", k=NB)
    for s in range(1, NB):
        wdt = NB - s
        g1 = ve(lambda e, s=s, w=wdt: e.tensor_tensor(
            gtt3[:, :, 0:w], sims3[:, :, s:NB], sims3[:, :, 0:w], ALU.is_gt),
            waits=(e_sim, prev))
        prev = ve(lambda e, w=wdt: e.tensor_tensor(
            rank3[:, :, 0:w], rank3[:, :, 0:w], gtt3[:, :, 0:w], ALU.add),
            waits=(g1, prev))
        g2 = ve(lambda e, s=s, w=wdt: e.tensor_tensor(
            gtt3[:, :, w:NB], sims3[:, :, 0:s], sims3[:, :, w:NB], ALU.is_gt),
            waits=(prev,))
        prev = ve(lambda e, w=wdt: e.tensor_tensor(
            rank3[:, :, w:NB], rank3[:, :, w:NB], gtt3[:, :, w:NB], ALU.add),
            waits=(g2, prev))
    if stop_after == "p2e":
        return _finish_early()
    e_mk = ve(lambda e: e.tensor_scalar(mk32[:], rankt[:], TOPK - 0.5, -1.0e9,
                                        ALU.is_gt, ALU.mult), waits=(prev,))
    mT_ev = [None] * H
    for h in range(H):
        bank = h % 2
        wtr = [e_mk] + ps_waits(2, [bank])
        p = pe(lambda e, h=h, bank=bank: e.transpose(
            ps[2][0:NB, bank * NS:bank * NS + NB], mk32[0:NB, h * NB:(h + 1) * NB],
            idf[0:NB, 0:NB]), waits=tuple(wtr))
        ev = ve(lambda e, h=h, bank=bank: e.tensor_copy(
            mTbb[:, h * NB:(h + 1) * NB], ps[2][0:NB, bank * NS:bank * NS + NB]),
            waits=(p,))
        ps_use(2, [bank], ev)
        mT_ev[h] = ev
    if stop_after == "p2":
        if dump is not None:
            dump.clear()
            dump["a_sims"] = sims[0:NB, :]
            dump["b_rank"] = rankt[0:NB, :]
            dump["c_mk32"] = mk32[0:NB, :]
            dump["d_mTbb"] = mTbb[0:NB, :]
        return _finish_early()
    d_ekb = dma("gpsimd", ekbb[:], ekb[:, :], sem="gpsimd",
                waits=(qk_last, ("tV", S.cnt["tV"]), ("tA", S.cnt["tA"])))

    band_tok = []
    w0_marks = (("tP", S.cnt["tP"]), ("tV", S.cnt["tV"]), ("tA", S.cnt["tA"]))
    for hl in range(HPW):
        b0 = 64 if hl % 2 == 0 else 0
        t1 = ve(lambda e, hl=hl, b0=b0: e.memset(k_ext[hl][b0:b0 + 64, :], 0.0),
                waits=(p7_tp,) + tuple(p7_done))
        t2 = ve(lambda e, hl=hl, b0=b0: e.memset(q_ext[hl][b0:b0 + 64, :], 0.0),
                waits=w0_marks)
        band_tok += [t1, t2]
    ofullT_done = []
    exp_scale = 0.125
    for w in range(NW):
        buf = 0
        wtokw = wv_tok[w]
        urow_tok = [None] * HPW
        wrow_tok = [None] * HPW
        ones_tok = [None] * HPW
        if w in qkv_cache:
            qk_ev, v_ev = qkv_cache.pop(w)
        else:
            qk_ev, v_ev = emit_qkv(w, w0_marks)
        if stop_after == f"w{w}q":
            if dump is not None:
                dump.clear()
                dump["a_mT"] = mTbb[0:NB, :]
                dump["b_kx"] = k_ext[0][:, 0:1024]
                dump["c_qx"] = q_ext[0][:, 0:1024]
            return _finish_early()
        for hl in range(HPW):
            h = w * HPW + hl
            odd = h % 2
            ur0 = 64 if not odd else 0
            wts = U.waits(("qext_u", hl)) + (tuple(band_tok) if w == 0 else ())
            urow_tok[hl] = dma("sync", q_ext[hl][ur0:ur0 + NB, :], ucst[:, :],
                               waits=wts)
            colp = ur0
            wrow_ev = None
            wrow_all = []
            for nt in range(NTOK // NS):
                pi = nt % 2
                wtr = ([mT_ev[h], d_ekb] + ps_waits(pi, [0])
                       + list(U.waits(("kext_w", hl)))
                       + (band_tok if w == 0 and hl == 0 and nt == 0 else []))
                p = pe(lambda e, h=h, nt=nt, pi=pi: e.matmul(
                    ps[pi][0:NB, 0:NS],
                    mTbb[:, h * NB:(h + 1) * NB],
                    ekbb[:, nt * NS:(nt + 1) * NS],
                    start=True, stop=True),
                    waits=tuple(wtr))
                if colp == 0:
                    wrow_ev = ve(lambda e, hl=hl, nt=nt, pi=pi: e.tensor_copy(
                        k_ext[hl][0:NB, nt * NS:(nt + 1) * NS],
                        ps[pi][0:NB, 0:NS]),
                        waits=(p,))
                    ps_use(pi, [0], wrow_ev)
                else:
                    wb = nt % 2
                    ww = U.waits(("wxs", wb))
                    cv = ve(lambda e, wb=wb, pi=pi: e.tensor_copy(
                        wxs[wb][:], ps[pi][0:NB, 0:NS]),
                        waits=(p,) + ww + (w0_marks if w == 0 else ()))
                    ps_use(pi, [0], cv)
                    wrow_ev = dma("sync", k_ext[hl][colp:colp + NB, nt * NS:(nt + 1) * NS],
                                  wxs[wb][:], waits=(cv,))
                    U.use(("wxs", wb), wrow_ev)
                wrow_all.append(wrow_ev)
            wrow_tok[hl] = wrow_all
            onw = U.waits(("vext_o", hl))
            if w == 0:
                onw = onw + (p7_tp,) + tuple(p7_done) + w0_marks
            ones_tok[hl] = ve(lambda e, hl=hl: e.memset(
                v_ext[hl][:].rearrange("p (a b) -> p a b", b=65)[:, :, 64:65], 1.0),
                waits=onw)

        if stop_after == f"w{w}u":
            if dump is not None:
                dump.clear()
                dump["a_mT"] = mTbb[0:NB, :]
                dump["b_idf"] = idf[:, :]
                dump["c_pbb"] = pbb[0:1, :]
                dump["d_fbb"] = fbb[0:1, :]
                dump["e_recr"] = recr[0:65, 0:QH]
                dump["f_ones"] = onesb[0:1, 0:128]
                dump["g_stat"] = stat[0][:, 0:12]
            return _finish_early()
        for hl in range(HPW):
            h = w * HPW + hl
            odd = h % 2
            base = 0
            head_ready = list(wrow_tok[hl]) + [urow_tok[hl], ones_tok[hl]]
            if w == 0:
                head_ready += band_tok
            head_ready += [qk_ev[("q", hl // 2, nt)][hl % 2] for nt in range(NQSEC)]
            head_ready += [qk_ev[("k", hl // 2, nt)][hl % 2] for nt in range(NTOK // NS)]
            head_ready += [v_ev[(tb, hl)] for tb in range(NTILE)]
            last_attn = None
            for qh in range(NQH):
                oslot = (h * NQH + qh) % 2
                po = ps[2 + oslot]
                lastv = None
                ex_half = {}
                for step in range(NB + 1):
                    if step < NB:
                        kb = step
                        pi = kb % 2
                        abuf = kb % 2
                        for nt in range(NSS):
                            wsc = ps_waits(pi, [nt])
                            if kb < 2 and nt == 0:
                                wsc += head_ready
                            last = pe(lambda e, hl=hl, kb=kb, qh=qh, nt=nt, pi=pi,
                                      base=base: e.matmul(
                                ps[pi][:, nt * SS:(nt + 1) * SS],
                                k_ext[hl][:, kb * 128:(kb + 1) * 128],
                                q_ext[hl][:, qh * QH + nt * SS: qh * QH + (nt + 1) * SS],
                                start=True, stop=True),
                                waits=tuple(wsc))
                            wex = [last] + list(U.waits(("attnT", abuf, nt)))
                            exn = ae(lambda e, abuf=abuf, pi=pi, nt=nt: e.activation(
                                attnT[abuf][:, nt * SS:(nt + 1) * SS],
                                ps[pi][:, nt * SS:(nt + 1) * SS],
                                AF.Exp, bias=0.0, scale=exp_scale),
                                waits=tuple(wex))
                            ps_use(pi, [nt], exn)
                            ex_half[(kb, nt)] = exn
                    if step >= 1:
                        kb = step - 1
                        abuf = kb % 2
                        for nt in range(NSS):
                            wv_ = [ex_half[(kb, nt)], t_ones]
                            if kb == 0:
                                wv_ += ps_waits(2 + oslot, [nt])
                            lastv = pe(lambda e, hl=hl, kb=kb, nt=nt, po=po: e.matmul(
                                po[0:65, nt * SS:(nt + 1) * SS],
                                v_ext[hl][:, kb * 65:kb * 65 + 65],
                                attnT[kb % 2][:, nt * SS:(nt + 1) * SS],
                                start=(kb == 0), stop=(kb == NB - 1)),
                                waits=tuple(wv_))
                            U.use(("attnT", abuf, nt), lastv)
                chunk = h // 2
                wrr = U.waits(("recr",))
                er = ve(lambda e, po=po: e.tensor_copy(recr[64:65, :], po[64:65, 0:QH]),
                        waits=(lastv,) + wrr)
                def _recip(e):
                    with nc.allow_low_precision(reason="softmax renorm factor; attention-path error is diluted ~50x in the output"):
                        return e.reciprocal(recr[64:65, :], recr[64:65, :])
                rr = ve(_recip, waits=(er,))
                dwr = dma("sync", recd[h * NQH + qh:h * NQH + qh + 1, :], recr[64:65, :],
                          sem="dB", waits=(rr,))
                U.use(("recr",), dwr)
                wrb = U.waits(("recb",))
                dbr = dma("sync", recb[:],
                          bass.AP(recd.tensor, (h * NQH + qh) * QH, [[0, 64], [1, QH]]),
                          sem="dB", waits=(dwr,) + wrb)
                if not odd:
                    evo = ve(lambda e, po=po, chunk=chunk, qh=qh: e.tensor_tensor(
                        ofullT[chunk][0:64, qh * QH:(qh + 1) * QH], po[0:64, 0:QH],
                        recb[:], ALU.mult), waits=(dbr, lastv))
                    fin = evo
                else:
                    wot = U.waits(("otmp",))
                    evo = ve(lambda e, po=po: e.tensor_tensor(
                        otmp[:], po[0:64, 0:QH], recb[:], ALU.mult),
                        waits=(dbr, lastv) + wot)
                    fin = dma("sync", ofullT[chunk][64:128, qh * QH:(qh + 1) * QH],
                              otmp[:], sem="dB", waits=(evo,))
                    U.use(("otmp",), fin)
                U.use(("recb",), evo)
                ps_use(2 + oslot, [0, 1], evo)
                ofullT_done.append(fin)
                last_attn = lastv
            if stop_after == f"w{w}h{hl}":
                if dump is not None:
                    dump.clear()
                    dump["a_mT"] = mTbb[0:NB, :]
                    dump["b_kx"] = k_ext[0][:, 0:1024]
                    dump["c_at"] = attnT[0][:, 0:1024]
                    dump["d_rb"] = recb[0:64, 0:QH]
                return _finish_early()
            U.use(("kext_w", hl), last_attn)
            U.use(("qext_u", hl), last_attn)
            U.use(("vext_o", hl), last_attn)
        if w + 1 < NW:
            wv_tok[w + 1] = wave_w_dma(w + 1)
    waves_marks = (("tP", S.cnt["tP"]), ("tV", S.cnt["tV"]), ("tA", S.cnt["tA"]))

    if stop_after == "waves":
        if dump is not None:
            dump.clear()
            dump["a_kx"] = k_ext[0][:, 0:1024]
            dump["b_qx"] = q_ext[0][:, 0:1024]
            dump["c_vx"] = v_ext[0][:, 0:1024]
            dump["d_ek"] = ekbb[0:NB, 0:1024]
            dump["e_mT"] = mTbb[0:NB, :]
        return _finish_early()
    wpj_tok = [None] * C
    for c in range(C):
        wpj_tok[c] = dma("gpsimd", wpjb[c][:], wprojT[c * 128:(c + 1) * 128, :],
                         sem="gpsimd", waits=U.waits(("wvW", 0, c)))
    of_done = tuple(ofullT_done)
    atn_tok = [None] * NOTILE
    proj_last_pe = None
    for tb in range(NOTILE):
        pt = tb % 4
        wtr = ps_waits(pt, [0, 1])
        if tb == 0:
            wtr += list(of_done)
        last = None
        for nh in range(2):
            for i, c in enumerate(range(C)):
                wmm = tuple(wtr + [wpj_tok[c]]) if (nh == 0 and i == 0) else (wpj_tok[c],)
                last = pe(lambda e, tb=tb, c=c, nh=nh, pt=pt,
                          st_c=(i == 0): e.matmul(
                    ps[pt][:, nh * NS:(nh + 1) * NS],
                    ofullT[c][:, tb * 128:(tb + 1) * 128],
                    wpjb[c][:, nh * NS:(nh + 1) * NS],
                    start=st_c, stop=False),
                    waits=wmm)
            last = pe(lambda e, nh=nh, pt=pt: e.matmul(
                ps[pt][:, nh * NS:(nh + 1) * NS],
                onesb[0:1, 0:128],
                pbb[0:1, nh * NS:(nh + 1) * NS],
                start=False, stop=True),
                waits=(d_pb, t_ones))
        proj_last_pe = last
        atw = U.waits(("atn", tb))
        atn_tok[tb] = ve(lambda e, tb=tb, pt=pt: e.tensor_copy(
            atn[tb][:], ps[pt][:, 0:D]), waits=(last,) + atw)
        ps_use(pt, [0, 1], atn_tok[tb])

    if stop_after == "p5":
        if dump is not None:
            dump.clear()
            dump["a_of0"] = ofullT[0][:, 0:min(NOWN, 1024)]
            dump["b_of7"] = ofullT[C - 1][:, 0:min(NOWN, 1024)]
            dump["c_atn0"] = atn[0][:, 0:D]
            dump["d_atnL"] = atn[NOTILE - 1][:, 0:D]
        return _finish_early()
    p6_marks = (("tP", S.cnt["tP"]), ("tV", S.cnt["tV"]))
    mh_tok = [None] * C
    ml_tok = [None] * C
    for c in range(C):
        mh_tok[c] = dma("sync", MhS[c][:], md[c * 128:(c + 1) * 128, :],
                        waits=waves_marks)
        ml_tok[c] = dma("sync", MlS[c][:], md[D + c * 128:D + (c + 1) * 128, :],
                        waits=waves_marks)
    xo_tok = [None] * NOTILE

    def xo_dma(tb):
        wts = U.waits(("xot", tb % 2))
        if tb < 2:
            wts = wts + tuple(s1_done) + p6_marks
        return dma("sync", xot[tb % 2][:], xo[tb * 128:(tb + 1) * 128, :],
                   waits=wts)

    for tb in range(min(2, NOTILE)):
        xo_tok[tb] = xo_dma(tb)
    tr_i = 0
    n_out_dma = 0
    for tb in range(NOTILE):
        sc = h1s6
        st_ = stat[tb % 2]
        ob = outb[tb % 3]
        wob = U.waits(("outb", tb % 3))
        if tb < 3:
            wob = wob + p6_marks
        x1t = ve(lambda e, tb=tb, ob=ob: e.tensor_tensor(
            ob[:], xot[tb % 2][:], atn[tb][:], ALU.add),
            waits=(xo_tok[tb], atn_tok[tb]) + wob)
        U.use(("xot", tb % 2), x1t)
        if tb + 2 < NOTILE:
            xo_tok[tb + 2] = xo_dma(tb + 2)
        v14, _ = ln_tile(lambda a, b, ob=ob: ob[:, a:b], sc, st_, (x1t,),
                         extra_sc_waits=p6_marks if tb == 0 else ())
        U.use(("outb", tb % 3), v14)
        h2ev_t = [None] * C
        for c in range(C):
            bank = tr_i % 2
            wtr = [v14] + ps_waits(2, [bank])
            if tb == 0 and c == 0:
                wtr.append(proj_last_pe)
            p = pe(lambda e, sc=sc, c=c, bank=bank: e.transpose(
                ps[2][:, bank * NS:bank * NS + 128], sc[:, c * 128:(c + 1) * 128], idf[:]),
                waits=tuple(wtr))
            ev = ae(lambda e, c=c, tb=tb, bank=bank: e.activation(
                h2T[c][:, tb * 128:(tb + 1) * 128], ps[2][:, bank * NS:bank * NS + 128],
                AF.Copy),
                waits=(p,))
            ps_use(2, [bank], ev)
            U.use(("h1s", id(sc) % 97), p)
            h2ev_t[c] = ev
            tr_i += 1
        pt = tb % 2
        wtr = ps_waits(pt, [0, 1])
        last = None
        for nh in range(2):
            seq = [(MhS[c], mh_tok[c], c) for c in range(C)] + \
                  [(MlS[c], ml_tok[c], c) for c in range(C)]
            for i, (mt, mtok, c) in enumerate(seq):
                wmm = [mtok, h2ev_t[c]]
                if nh == 0 and i == 0:
                    wmm = wtr + wmm
                last = pe(lambda e, tb=tb, c=c, nh=nh, pt=pt, mt=mt,
                          st_c=(i == 0): e.matmul(
                    ps[pt][:, nh * NS:(nh + 1) * NS],
                    h2T[c][:, tb * 128:(tb + 1) * 128],
                    mt[:, nh * NS:(nh + 1) * NS],
                    start=st_c, stop=False),
                    waits=tuple(wmm))
            last = pe(lambda e, nh=nh, pt=pt: e.matmul(
                ps[pt][:, nh * NS:(nh + 1) * NS],
                onesb[0:1, 0:128],
                fbb[0:1, nh * NS:(nh + 1) * NS],
                start=False, stop=True),
                waits=(d_fb, t_ones))
        evo = ve(lambda e, ob=ob, pt=pt: e.tensor_tensor(
            ob[:], ob[:], ps[pt][:, 0:D], ALU.add),
            waits=(last, v14))
        ps_use(pt, [0, 1], evo)
        dot = dma("sync", out[tb * 128:(tb + 1) * 128, :], ob[:],
                  waits=(evo,))
        U.use(("outb", tb % 3), dot)
        n_out_dma += 16

    if stop_after == "p6":
        if dump is not None:
            dump.clear()
            dump["a_h2T0"] = h2T[0][:, 0:min(NOWN, 1024)]
            dump["b_h2TL"] = h2T[C - 1][:, 0:min(NOWN, 1024)]
        return _finish_early()
    S.op("sync", lambda e: None,
         waits=tuple((sn, S.cnt[sn]) for sn in POOLS["sync"]))

    block = nc.Block()
    with block:
        S.replay(block)
    stats = {e: len(S.thunks[e]) for e in ENGS}
    probes = {"atn": [t.name for t in atn], "mk32": mk32.name,
              "ofullT": [t.name for t in ofullT], "h2T": [t.name for t in h2T],
              "sims": sims.name}
    return nc, stats, probes


def make_consts(NTOK=4096, NOWN=2048):
    D = 1024
    NB = NTOK // 128
    NTILE = NB
    i = np.arange(D, dtype=np.int64)
    Mm = (i[:, None] * i[None, :]) % D
    ang = 2.0 * np.pi * Mm.astype(np.float64) / D
    Cm = np.cos(ang)
    Sm = -np.sin(ang)
    c1 = Cm.astype(np.float32)
    c2 = (Cm - c1.astype(np.float64)).astype(np.float32)
    s1 = Sm.astype(np.float32)
    s2 = (Sm - s1.astype(np.float64)).astype(np.float32)
    e32v = np.zeros((128, NB * NTILE), dtype=np.float32)
    for t in range(NTILE):
        e32v[:, NB * t + t] = 1.0 / 128.0
    ekb = np.zeros((NB, NTOK), dtype=np.float32)
    for kbi in range(NB):
        ekb[kbi, kbi * 128:(kbi + 1) * 128] = 1.0
    identf = np.eye(128, dtype=np.float32)
    return dict(c1=c1, c2=c2, s1=s1, s2=s2, e32v=e32v, ekb=ekb, identf=identf)


def make_inputs_for_core(core, full, NTOK=4096, NOWN=2048, consts=None):
    g = core % 2
    b = core // 2
    x = np.asarray(full["L0_attn_x"], dtype=np.float32)
    NBl = NTOK // 128
    d = dict(consts)
    d["xb"] = np.ascontiguousarray(x[b])
    d["xo"] = np.ascontiguousarray(x[b, g * NOWN:(g + 1) * NOWN])
    d["wqkvT"] = np.ascontiguousarray(np.asarray(full["L0_attn_qkv_w"], np.float32).T)
    d["wprojT"] = np.ascontiguousarray(np.asarray(full["L0_attn_proj_w"], np.float32).T)
    d["pb"] = np.ascontiguousarray(np.asarray(full["L0_attn_proj_b"], np.float32).reshape(1, -1))
    d["wrT"] = np.ascontiguousarray(np.asarray(full["L0_ffn_w_freq_real"], np.float32).T)
    d["wiT"] = np.ascontiguousarray(np.asarray(full["L0_ffn_w_freq_imag"], np.float32).T)
    d["fb"] = np.ascontiguousarray(np.asarray(full["L0_ffn_bias"], np.float32).reshape(1, -1))
    u = np.zeros((NBl, NOWN), dtype=np.float32)
    for ql in range(NOWN):
        u[(g * NOWN + ql) // 128, ql] = 1.0
    d["ucst"] = u.astype(ml_dtypes.bfloat16)
    d["goff"] = np.array([[g * (NOWN // min(512, NOWN))]], dtype=np.int32)
    return d


from concourse.bass_utils import run_bass_kernel_spmd

_BUILD_CACHE = {}


def _get_program():
    if "nc" not in _BUILD_CACHE:
        nc, stats, probes = build_program(4096, 2048, 4)
        _BUILD_CACHE["nc"] = nc
    return _BUILD_CACHE["nc"]


def kernel(L0_attn_x, L0_attn_qkv_w, L0_attn_proj_w, L0_attn_proj_b,
           L0_ffn_w_freq_real, L0_ffn_w_freq_imag, L0_ffn_bias):
    full = {"L0_attn_x": L0_attn_x, "L0_attn_qkv_w": L0_attn_qkv_w,
            "L0_attn_proj_w": L0_attn_proj_w, "L0_attn_proj_b": L0_attn_proj_b,
            "L0_ffn_w_freq_real": L0_ffn_w_freq_real,
            "L0_ffn_w_freq_imag": L0_ffn_w_freq_imag, "L0_ffn_bias": L0_ffn_bias}
    consts = make_consts(4096, 2048)
    in_maps = [make_inputs_for_core(c, full, 4096, 2048, consts) for c in range(8)]
    nc = _get_program()
    res = run_bass_kernel_spmd(nc, in_maps, core_ids=list(range(8)))
    out = np.empty((4, 4096, 1024), np.float32)
    for c in range(8):
        out[c // 2, (c % 2) * 2048:(c % 2 + 1) * 2048] = res.results[c]["out"]
    return out
